# version 08
# baseline (speedup 1.0000x reference)
import numpy as np
from contextlib import ExitStack

import concourse.bass as bass
import concourse.mybir as mybir
from concourse.bass_utils import run_bass_kernel_spmd

F32 = mybir.dt.float32
BF16 = mybir.dt.bfloat16
I32 = mybir.dt.int32
AF = mybir.ActivationFunctionType
ALU = mybir.AluOpType
AX = mybir.AxisListType

C = 128
NPRE = 16
NOWN = 16
D = 1024
NS = 64
NE = 8
DFF = 4096
TT_TOK = 512
NTT = 4
INVF = 1.0 / 256.0
TWO_PI = float(2.0 * np.pi)
ARENA = 54272
COALESCE_OFF = True


class Ctr:
    def __init__(self, h, name):
        self.h = h
        self.name = name
        self.v = 0


class Builder:
    ENGS = ("sync", "gpsimd", "tensor", "vector", "scalar")

    def __init__(self, nc):
        self.nc = nc
        self.plan = {e: [] for e in self.ENGS}
        self.hw = {}
        self.waited = set()

    def ctr(self, ctx, name):
        return Ctr(ctx.enter_context(self.nc.semaphore(name)), name)

    def wait(self, eng, cv):
        if cv is None:
            return
        c, v = cv
        if v <= 0:
            return
        if self.hw.get((eng, c.name), 0) >= v:
            return
        self.plan[eng].append(("wait", c.h, v))
        self.hw[(eng, c.name)] = v
        self.waited.add((c.name, v))

    force_ss = False

    def op(self, eng, fn, inc=None, k=1, waits=(), ss=False):
        for w in waits:
            self.wait(eng, w)
        incs = []
        r = None
        if inc is not None:
            inc.v += k
            incs = [(inc, k, inc.v)]
            r = (inc, inc.v)
        self.plan[eng].append(("op", fn, incs))
        if (ss or self.force_ss) and r is not None:
            self.wait(eng, r)
        return r

    def emit(self):
        plan = self.plan
        waited = self.waited
        with self.nc.Block() as block:
            def run(eng_obj, key):
                pend = {}
                last_inst = None
                for item in plan[key]:
                    if item[0] == "wait":
                        eng_obj.wait_ge(item[1], item[2])
                    else:
                        inst = item[1](eng_obj)
                        last_inst = inst
                        for (c, k, vafter) in item[2]:
                            if k == 16:
                                inst.then_inc(c.h, k)
                            else:
                                pend[c.name] = (c, pend.get(c.name, (c, 0))[1] + k)
                                if COALESCE_OFF or (c.name, vafter) in waited:
                                    cc, p = pend.pop(c.name)
                                    inst.then_inc(cc.h, p)
                for name, (c, p) in pend.items():
                    if last_inst is not None:
                        last_inst.then_inc(c.h, p)

            @block.sync
            def _(e):
                run(e, "sync")

            @block.gpsimd
            def _(e):
                run(e, "gpsimd")

            @block.tensor
            def _(e):
                run(e, "tensor")

            @block.vector
            def _(e):
                run(e, "vector")

            @block.scalar
            def _(e):
                run(e, "scalar")


def bcast_ap(dram_ap, n):
    return bass.AP(dram_ap.tensor, 0, [[0, 128], [1, n]])


def build_program(run_attn=True, n_experts=NE, run_final=True, n_own=NOWN, n_pre=NPRE, chunk_ops=6, dbg=False):
    nc = bass.Bass("TRN2", target_bir_lowering=False)

    xfull = nc.dram_tensor("xfull", [4096, D], F32, kind="ExternalInput").ap()
    w_in = nc.dram_tensor("w_in", [D, D], F32, kind="ExternalInput").ap()
    b_in = nc.dram_tensor("b_in", [D], F32, kind="ExternalInput").ap()
    gamma = nc.dram_tensor("gamma", [D], F32, kind="ExternalInput").ap()
    beta = nc.dram_tensor("beta", [D], F32, kind="ExternalInput").ap()
    a_re = nc.dram_tensor("a_re", [NS], F32, kind="ExternalInput").ap()
    a_im = nc.dram_tensor("a_im", [NS], F32, kind="ExternalInput").ap()
    b_re = nc.dram_tensor("b_re", [NS], F32, kind="ExternalInput").ap()
    b_im = nc.dram_tensor("b_im", [NS], F32, kind="ExternalInput").ap()
    c_re = nc.dram_tensor("c_re", [D, NS], F32, kind="ExternalInput").ap()
    c_im = nc.dram_tensor("c_im", [D, NS], F32, kind="ExternalInput").ap()
    d_re = nc.dram_tensor("d_re", [D], F32, kind="ExternalInput").ap()
    r_w = nc.dram_tensor("r_w", [D, NE], F32, kind="ExternalInput").ap()
    r_b = nc.dram_tensor("r_b", [NE], F32, kind="ExternalInput").ap()
    w1 = nc.dram_tensor("w1", [NE, D, DFF], F32, kind="ExternalInput").ap()
    b1 = nc.dram_tensor("b1", [NE, DFF], F32, kind="ExternalInput").ap()
    w2 = nc.dram_tensor("w2", [NE, DFF, D], F32, kind="ExternalInput").ap()
    b2 = nc.dram_tensor("b2", [NE, D], F32, kind="ExternalInput").ap()
    sel = nc.dram_tensor("sel", [128, 1], F32, kind="ExternalInput").ap()
    out = nc.dram_tensor("out", [2048, D], F32, kind="ExternalOutput").ap()
    kind_scr = "ExternalOutput" if dbg else "Internal"
    x1scr = nc.dram_tensor("x1scr", [2048, D], F32, kind=kind_scr).ap()
    accd = nc.dram_tensor("accd", [2048, D], F32, kind=kind_scr).ap()
    dbg1 = nc.dram_tensor("dbg1", [128, 8192], F32, kind="ExternalOutput").ap() if dbg else None

    with ExitStack() as ctx:
        e = ctx.enter_context
        B = Builder(nc)

        AR = e(nc.sbuf_tensor("AR", [128, ARENA], BF16))
        off = [0]

        def carve(n):
            o = off[0]
            off[0] += n
            assert off[0] <= ARENA, f"bf16 arena overflow {off[0]}"
            return AR[:, o:o + n]

        winT = [carve(D) for _ in range(8)]
        ZG = [carve(D) for _ in range(NOWN)]
        khR = carve(D)
        khI = carve(D)
        fcos = carve(128)
        fsinn = carve(128)
        icosw = carve(128)
        isinnw = carve(128)
        kr0 = carve(D)
        lhsTD = [carve(128) for _ in range(16)]
        lhsTz = carve(128)
        megaR = carve(2048)
        lhsTmega = [megaR[:, c * 128:(c + 1) * 128] for c in range(NPRE)]
        sg_home = megaR[:, 0:1024]
        tri = carve(128)
        identb = carve(128)
        u_hiB = [carve(D) for _ in range(2)]
        u_loB = [carve(D) for _ in range(2)]
        x0lo = carve(D)
        sxb = [carve(D) for _ in range(2)]
        xfm = carve(D)
        ufR = carve(D)
        ufI = carve(D)
        yR = carve(D)
        yI = carve(D)
        rowpA = carve(2304)
        rowpB = carve(4096)
        gstk = carve(D)
        gswp = carve(D)
        stgb = carve(128)
        attn_end = off[0]
        assert attn_end <= ARENA

        w1res = [AR[:, k * DFF:(k + 1) * DFF] for k in range(8)]
        hbuf = AR[:, 32768:32768 + 32 * TT_TOK]
        w2buf = [AR[:, 49152 + i * D:49152 + (i + 1) * D] for i in range(4)]

        x1fm = [e(nc.sbuf_tensor(f"x1fm{k}", [128, 2048], BF16)) for k in range(8)]
        wrbf = e(nc.sbuf_tensor("wrbf", [128, NE * 8], BF16))
        b2f = e(nc.sbuf_tensor("b2f", [8, D], F32))
        b2hi_t = e(nc.sbuf_tensor("b2hi_t", [8, D], BF16))
        b2lo_t = e(nc.sbuf_tensor("b2lo_t", [8, D], BF16))
        geT = e(nc.sbuf_tensor("geT", [8, 128], BF16))
        onescol = e(nc.sbuf_tensor("onescol", [128, 1], BF16))
        onesall = e(nc.sbuf_tensor("onesall", [128, 128], BF16))
        altcol = e(nc.sbuf_tensor("altcol", [128, 1], BF16))

        AR2 = e(nc.sbuf_tensor("AR2", [128, 7680], F32))
        uf32 = AR2[:, 0:1024]
        x1pre = AR2[:, 1024:2048]
        x1f = AR2[:, 2048:3072]
        grep = AR2[:, 3072:4096]
        brep = AR2[:, 4096:5120]
        drep = AR2[:, 5120:6144]
        scrq = AR2[:, 6144:6656]
        ftmp = AR2[:, 6656:7680]
        csrow = AR2[0:1, 6656:7680]
        x1g = [AR2[:, 0:1024], AR2[:, 1024:2048]]
        accg = [AR2[:, 2048:3072], AR2[:, 3072:4096]]
        resb = AR2[:, 4096:5120]
        outb = [AR2[:, 5120:6144], AR2[:, 6144:7168]]
        tmpg = [AR2[:, i * 1024:(i + 1) * 1024] for i in range(4)]

        sx = [e(nc.sbuf_tensor(f"sx{i}", [128, D], F32)) for i in range(2)]
        rp1 = e(nc.sbuf_tensor("rp1", [128, D], F32))
        identf = e(nc.sbuf_tensor("identf", [128, 128], F32))
        b1c = e(nc.sbuf_tensor("b1c", [128, NE * 32], F32))
        geAll = e(nc.sbuf_tensor("geAll", [128, NOWN * 8], F32))
        selc = e(nc.sbuf_tensor("selc", [128, 1], F32))
        Tr = e(nc.sbuf_tensor("Tr", [64, 128], F32))
        Ti = e(nc.sbuf_tensor("Ti", [64, 128], F32))
        Pfr = e(nc.sbuf_tensor("Pfr", [64, 128], F32))
        Pfi = e(nc.sbuf_tensor("Pfi", [64, 128], F32))
        Pvr = e(nc.sbuf_tensor("Pvr", [64, 128], F32))
        Pvi = e(nc.sbuf_tensor("Pvi", [64, 128], F32))
        At1 = e(nc.sbuf_tensor("At1", [64, 128], F32))
        At2 = e(nc.sbuf_tensor("At2", [64, 128], F32))
        Acol = e(nc.sbuf_tensor("Acol", [64, 8], F32))
        decr = e(nc.sbuf_tensor("decr", [64, 16], F32))
        deci = e(nc.sbuf_tensor("deci", [64, 16], F32))
        CrT = AR2[0:64, 0:1024]
        CiT = AR2[0:64, 1024:2048]
        stg32 = e(nc.sbuf_tensor("stg32", [128, 128], F32))
        jrow = e(nc.sbuf_tensor("jrow", [1, 128], F32))
        irow = e(nc.sbuf_tensor("irow", [1, 128], I32))
        itmp = e(nc.sbuf_tensor("itmp", [128, 128], I32))
        icol = e(nc.sbuf_tensor("icol", [128, 1], I32))
        fcol = e(nc.sbuf_tensor("fcol", [128, 1], F32))
        frows = e(nc.sbuf_tensor("frows", [128, D], F32))
        crow = e(nc.sbuf_tensor("crow", [1, D], F32))
        carry = frows[0:1, :]
        rbrow = frows[0:1, 0:NE]
        stat = e(nc.sbuf_tensor("stat", [128, 48], F32))
        l2t = e(nc.sbuf_tensor("l2t", [128, NE], F32))
        exp8 = e(nc.sbuf_tensor("exp8", [128, NE], F32))
        msk8 = e(nc.sbuf_tensor("msk8", [128, NE], F32))
        cs8 = e(nc.sbuf_tensor("cs8", [128, NE], F32))
        ones8 = e(nc.sbuf_tensor("ones8", [128, NE], F32))

        dbgS = e(nc.sbuf_tensor("dbgS", [128, 512], F32)) if dbg else None
        pA = e(nc.psum_tensor("pA", [128, 1024], F32))
        pB = e(nc.psum_tensor("pB", [128, 1024], F32))
        p0 = e(nc.psum_tensor("p0", [128, 512], F32))
        p1 = e(nc.psum_tensor("p1", [128, 512], F32))
        p2 = e(nc.psum_tensor("p2", [128, 512], F32))
        p3 = e(nc.psum_tensor("p3", [128, 512], F32))

        cLD = B.ctr(ctx, "cLD")
        cGL = B.ctr(ctx, "cGL")
        cGP = B.ctr(ctx, "cGP")
        cPE = B.ctr(ctx, "cPE")
        cDV = B.ctr(ctx, "cDV")
        cAC = B.ctr(ctx, "cAC")
        cST = B.ctr(ctx, "cST")

        OP = B.op

        def dma(eng, dst, src, inc, waits=()):
            return OP(eng, lambda en, d=dst, s=src: en.dma_start(out=d, in_=s), inc=inc, k=16, waits=waits)

        def vop(fn, waits=(), ss=False):
            return OP("vector", fn, cDV, waits=waits, ss=ss)

        def aop(fn, waits=(), ss=False):
            return OP("scalar", fn, cAC, waits=waits, ss=ss)

        def pop(fn, waits=()):
            return OP("tensor", fn, cPE, waits=waits)

        def ln_stats(src0, src1, from_psum, waits=(), uf_dst=None, sb=32):
            c = lambda i: stat[:, sb + i:sb + i + 1]
            aop(lambda en: en.activation(out=(uf_dst[:, 0:512] if from_psum else scrq[:, :]),
                                         in_=src0, func=AF.Copy, accum_out=c(0)), waits=waits)
            aop(lambda en: en.activation(out=(uf_dst[:, 512:1024] if from_psum else scrq[:, :]),
                                         in_=src1, func=AF.Copy, accum_out=c(1)))
            aop(lambda en: en.activation(out=scrq[:, :], in_=src0, func=AF.Square, accum_out=c(2)))
            aop(lambda en: en.activation(out=scrq[:, :], in_=src1, func=AF.Square, accum_out=c(3)),
                ss=True)
            aop(lambda en: en.add(out=c(4), in_=c(0), add=c(1)), ss=True)
            aop(lambda en: en.mul(out=c(4), in_=c(4), mul=1.0 / D), ss=True)
            aop(lambda en: en.add(out=c(5), in_=c(2), add=c(3)), ss=True)
            aop(lambda en: en.mul(out=c(5), in_=c(5), mul=1.0 / D), ss=True)
            aop(lambda en: en.activation(out=c(6), in_=c(4), func=AF.Square), ss=True)
            aop(lambda en: en.mul(out=c(6), in_=c(6), mul=-1.0), ss=True)
            aop(lambda en: en.add(out=c(5), in_=c(5), add=c(6)), ss=True)
            aop(lambda en: en.activation(out=c(6), in_=c(5), func=AF.Ln, bias=stat[:, 12:13]), ss=True)
            aop(lambda en: en.activation(out=c(7), in_=c(6), func=AF.Exp, scale=-0.5), ss=True)
            aop(lambda en: en.mul(out=c(8), in_=c(4), mul=-1.0), ss=True)
            return aop(lambda en: en.mul(out=c(8), in_=c(8), mul=c(7)), ss=True)

        B.force_ss = True
        ld0 = dma("sync", rp1[0:1, :], b_in[None, :], cLD)
        dma("sync", rp1[32:33, :], gamma[None, :], cLD)
        dma("sync", rp1[64:65, :], beta[None, :], cLD)
        dma("sync", rp1[96:97, :], d_re[None, :], cLD)
        dma("sync", grep, bcast_ap(gamma, D), cLD)
        dma("sync", brep, bcast_ap(beta, D), cLD)
        dma("sync", drep, bcast_ap(d_re, D), cLD)
        dma("sync", Acol[:, 0:1], a_re[:, None], cLD)
        dma("sync", Acol[:, 1:2], a_im[:, None], cLD)
        dma("sync", Acol[:, 2:3], b_re[:, None], cLD)
        dma("sync", Acol[:, 3:4], b_im[:, None], cLD)
        dma("sync", rbrow[:, :], r_b[None, :], cLD)
        dma("sync", selc[:, :], sel, cLD)
        ld_small = (cLD, cLD.v)

        for k in range(8):
            dma("gpsimd", wrbf[:, k * 8:(k + 1) * 8], r_w[k * 128:(k + 1) * 128, :], cGL)
        dma("gpsimd", b2hi_t[:], b2[:, :], cGL)
        gl_b2 = (cGL, cGL.v)

        OP("gpsimd", lambda en: en.memset(onescol[:], 1.0), cGP)
        OP("gpsimd", lambda en: en.memset(onesall[:], 1.0), cGP)
        OP("gpsimd", lambda en: en.memset(ones8[:], 1.0), cGP)
        OP("gpsimd", lambda en: en.memset(stat[:, 12:13], 1e-5), cGP)
        OP("gpsimd", lambda en: en.memset(tri, 1.0), cGP)
        OP("gpsimd", lambda en: en.affine_select(out=tri, in_=tri, pattern=[[1, 128]],
                                                 compare_op=ALU.is_ge, fill=0.0, base=0,
                                                 channel_multiplier=-1), cGP)
        OP("gpsimd", lambda en: en.memset(identb, 1.0), cGP)
        OP("gpsimd", lambda en: en.affine_select(out=identb, in_=identb, pattern=[[1, 128]],
                                                 compare_op=ALU.is_equal, fill=0.0, base=0,
                                                 channel_multiplier=-1), cGP)
        OP("gpsimd", lambda en: en.memset(identf[:], 1.0), cGP)
        OP("gpsimd", lambda en: en.affine_select(out=identf[:], in_=identf[:], pattern=[[1, 128]],
                                                 compare_op=ALU.is_equal, fill=0.0, base=0,
                                                 channel_multiplier=-1), cGP)
        OP("gpsimd", lambda en: en.iota(out=irow[:], pattern=[[1, 128]], base=0, channel_multiplier=0), cGP)
        gp_const = OP("gpsimd", lambda en: en.iota(out=icol[:], pattern=[[0, 1]], base=0,
                                                   channel_multiplier=1), cGP)

        vop(lambda en: en.tensor_copy(out=rowpA[0:1, 0:1024], in_=rp1[0:1, :]), waits=[ld0, ld_small])
        vop(lambda en: en.tensor_sub(out=rowpA[0:1, 1024:2048], in0=rp1[0:1, :],
                                     in1=rowpA[0:1, 0:1024]))
        vop(lambda en: en.tensor_copy(out=rowpA[0:1, 2048:2056], in_=rbrow[:, :]))
        vop(lambda en: en.tensor_sub(out=rowpA[0:1, 2056:2064], in0=rbrow[:, :],
                                     in1=rowpA[0:1, 2048:2056]))
        m = dma("sync", b2f[:], b2[:, :], cLD)
        vop(lambda en: en.tensor_sub(out=b2lo_t[:], in0=b2f[:], in1=b2hi_t[:]), waits=[m, gl_b2])

        vop(lambda en: en.tensor_copy(out=fcol[:], in_=icol[:]), waits=[gp_const])
        vop(lambda en: en.tensor_scalar(out=stat[:, 0:1], in0=fcol[:], scalar1=0.5, scalar2=None,
                                        op0=ALU.mult))
        vop(lambda en: en.tensor_copy(out=icol[:], in_=stat[:, 0:1]))
        vop(lambda en: en.tensor_copy(out=stat[:, 1:2], in_=icol[:]))
        dv_par = vop(lambda en: en.tensor_sub(out=stat[:, 2:3], in0=stat[:, 0:1], in1=stat[:, 1:2]))
        ac_abs = aop(lambda en: en.activation(out=stat[:, 4:5], in_=stat[:, 2:3], func=AF.Abs),
                     waits=[dv_par])
        dv_alt = vop(lambda en: en.tensor_scalar(out=altcol[:], in0=stat[:, 4:5], scalar1=-4.0,
                                                 scalar2=1.0, op0=ALU.mult, op1=ALU.add), waits=[ac_abs])
        vop(lambda en: en.tensor_copy(out=jrow[:], in_=irow[:]))

        vop(lambda en: en.memset(Tr[:, 0:1], 1.0), waits=[ld_small])
        vop(lambda en: en.memset(Ti[:, 0:1], 0.0))
        vop(lambda en: en.tensor_copy(out=Acol[:, 4:5], in_=Acol[:, 0:1]))
        vop(lambda en: en.tensor_copy(out=Acol[:, 5:6], in_=Acol[:, 1:2]))
        for lvl in range(7):
            n = 1 << lvl
            vop(lambda en, n=n: en.tensor_scalar(out=At1[:, 0:n], in0=Tr[:, 0:n], scalar1=Acol[:, 4:5],
                                                 scalar2=None, op0=ALU.mult))
            vop(lambda en, n=n: en.tensor_scalar(out=At2[:, 0:n], in0=Ti[:, 0:n], scalar1=Acol[:, 5:6],
                                                 scalar2=None, op0=ALU.mult))
            vop(lambda en, n=n: en.tensor_sub(out=Tr[:, n:2 * n], in0=At1[:, 0:n], in1=At2[:, 0:n]))
            vop(lambda en, n=n: en.tensor_scalar(out=At1[:, 0:n], in0=Tr[:, 0:n], scalar1=Acol[:, 5:6],
                                                 scalar2=None, op0=ALU.mult))
            vop(lambda en, n=n: en.tensor_scalar(out=At2[:, 0:n], in0=Ti[:, 0:n], scalar1=Acol[:, 4:5],
                                                 scalar2=None, op0=ALU.mult))
            vop(lambda en, n=n: en.tensor_add(out=Ti[:, n:2 * n], in0=At1[:, 0:n], in1=At2[:, 0:n]))
            vop(lambda en: en.tensor_mul(out=Acol[:, 6:7], in0=Acol[:, 4:5], in1=Acol[:, 4:5]))
            vop(lambda en: en.tensor_mul(out=Acol[:, 7:8], in0=Acol[:, 5:6], in1=Acol[:, 5:6]))
            vop(lambda en: en.tensor_mul(out=Acol[:, 5:6], in0=Acol[:, 4:5], in1=Acol[:, 5:6]))
            vop(lambda en: en.tensor_scalar(out=Acol[:, 5:6], in0=Acol[:, 5:6], scalar1=2.0,
                                            scalar2=None, op0=ALU.mult))
            vop(lambda en: en.tensor_sub(out=Acol[:, 4:5], in0=Acol[:, 6:7], in1=Acol[:, 7:8]))
        vop(lambda en: en.tensor_scalar(out=At1[:, :], in0=Tr[:, :], scalar1=Acol[:, 0:1],
                                        scalar2=None, op0=ALU.mult))
        vop(lambda en: en.tensor_scalar(out=At2[:, :], in0=Ti[:, :], scalar1=Acol[:, 1:2],
                                        scalar2=None, op0=ALU.mult))
        vop(lambda en: en.tensor_sub(out=Pfr[:, :], in0=At1[:, :], in1=At2[:, :]))
        vop(lambda en: en.tensor_scalar(out=At1[:, :], in0=Tr[:, :], scalar1=Acol[:, 1:2],
                                        scalar2=None, op0=ALU.mult))
        vop(lambda en: en.tensor_scalar(out=At2[:, :], in0=Ti[:, :], scalar1=Acol[:, 0:1],
                                        scalar2=None, op0=ALU.mult))
        vop(lambda en: en.tensor_add(out=Pfi[:, :], in0=At1[:, :], in1=At2[:, :]))
        vop(lambda en: en.tensor_copy(out=Pvr[:, :], in_=Tr[:, ::-1]))
        vop(lambda en: en.tensor_copy(out=Pvi[:, :], in_=Ti[:, ::-1]))
        vop(lambda en: en.memset(decr[:, 0:1], 1.0))
        vop(lambda en: en.memset(deci[:, 0:1], 0.0))
        for dd in range(1, 16):
            vop(lambda en, dd=dd: en.tensor_scalar(out=Acol[:, 6:7], in0=decr[:, dd - 1:dd],
                                                   scalar1=Acol[:, 4:5], scalar2=None, op0=ALU.mult))
            vop(lambda en, dd=dd: en.tensor_scalar(out=Acol[:, 7:8], in0=deci[:, dd - 1:dd],
                                                   scalar1=Acol[:, 5:6], scalar2=None, op0=ALU.mult))
            vop(lambda en, dd=dd: en.tensor_sub(out=decr[:, dd:dd + 1], in0=Acol[:, 6:7], in1=Acol[:, 7:8]))
            vop(lambda en, dd=dd: en.tensor_scalar(out=Acol[:, 6:7], in0=decr[:, dd - 1:dd],
                                                   scalar1=Acol[:, 5:6], scalar2=None, op0=ALU.mult))
            vop(lambda en, dd=dd: en.tensor_scalar(out=Acol[:, 7:8], in0=deci[:, dd - 1:dd],
                                                   scalar1=Acol[:, 4:5], scalar2=None, op0=ALU.mult))
            vop(lambda en, dd=dd: en.tensor_add(out=deci[:, dd:dd + 1], in0=Acol[:, 6:7], in1=Acol[:, 7:8]))
        for dd in range(16):
            vop(lambda en, dd=dd: en.tensor_scalar(out=At1[:, :], in0=Pfr[:, :], scalar1=decr[:, dd:dd + 1],
                                                   scalar2=None, op0=ALU.mult))
            vop(lambda en, dd=dd: en.tensor_scalar(out=At2[:, :], in0=Pfi[:, :], scalar1=deci[:, dd:dd + 1],
                                                   scalar2=None, op0=ALU.mult))
            vop(lambda en, dd=dd: en.tensor_sub(out=lhsTD[dd][0:64, :], in0=At1[:, :], in1=At2[:, :]))
            vop(lambda en, dd=dd: en.tensor_scalar(out=At1[:, :], in0=Pfr[:, :], scalar1=deci[:, dd:dd + 1],
                                                   scalar2=None, op0=ALU.mult))
            vop(lambda en, dd=dd: en.tensor_scalar(out=At2[:, :], in0=Pfi[:, :], scalar1=decr[:, dd:dd + 1],
                                                   scalar2=None, op0=ALU.mult))
            vop(lambda en, dd=dd: en.scalar_tensor_tensor(out=lhsTD[dd][64:128, :], in0=At1[:, :],
                                                          scalar=-1.0, in1=At2[:, :],
                                                          op0=ALU.mult, op1=ALU.subtract))
        if dbg:
            vop(lambda en: en.tensor_copy(out=dbgS[0:64, 0:8], in_=Acol[:, :]))
            vop(lambda en: en.tensor_copy(out=dbgS[0:64, 8:136], in_=Tr[:, :]))
            vop(lambda en: en.tensor_copy(out=dbgS[0:64, 136:264], in_=Pvr[:, :]))
            vop(lambda en: en.tensor_copy(out=dbgS[0:64, 264:280], in_=decr[:, :]))
            vop(lambda en: en.tensor_copy(out=dbgS[0:64, 280:408], in_=Ti[:, :]))
        dv_tbl = (cDV, cDV.v)

        pe_jk = pop(lambda en: en.matmul(out=p0[:, 0:128], lhsT=jrow[:], rhs=jrow[:], start=True,
                                         stop=True), waits=[dv_tbl])

        def angles(add64):
            addc = 64.0 if add64 else 0.0
            vop(lambda en, a=addc: en.tensor_scalar(out=uf32[:, 0:128], in0=p0[:, 0:128], scalar1=a,
                                                    scalar2=INVF, op0=ALU.add, op1=ALU.mult),
                waits=[pe_jk, (cAC, cAC.v)])
            vop(lambda en: en.tensor_copy(out=itmp[:], in_=uf32[:, 0:128]))
            vop(lambda en: en.tensor_copy(out=uf32[:, 128:256], in_=itmp[:]))
            vop(lambda en: en.tensor_sub(out=uf32[:, 0:128], in0=uf32[:, 0:128], in1=uf32[:, 128:256]))
            vop(lambda en: en.tensor_single_scalar(out=uf32[:, 128:256], in_=uf32[:, 0:128],
                                                   scalar=0.5, op=ALU.is_gt))
            vop(lambda en: en.tensor_sub(out=uf32[:, 0:128], in0=uf32[:, 0:128], in1=uf32[:, 128:256]))
            m2 = vop(lambda en: en.tensor_scalar(out=uf32[:, 0:128], in0=uf32[:, 0:128], scalar1=TWO_PI,
                                                 scalar2=None, op0=ALU.mult))
            return aop(lambda en: en.activation(out=stg32[:], in_=uf32[:, 0:128], func=AF.Sin),
                       waits=[m2, (cDV, cDV.v)])

        m_sin = angles(False)
        dv_fs = vop(lambda en: en.tensor_scalar(out=fsinn, in0=stg32[:], scalar1=-1.0, scalar2=None,
                                                op0=ALU.mult), waits=[m_sin])
        vop(lambda en: en.tensor_scalar(out=isinnw, in0=stg32[:], scalar1=-2.0 * INVF, scalar2=None,
                                        op0=ALU.mult))
        vop(lambda en: en.tensor_scalar(out=isinnw[:, 0:1] if False else AR[:, 0:0], in0=stg32[:, 0:1],
                                        scalar1=0.0, scalar2=None, op0=ALU.mult)) if False else None
        vop(lambda en: en.tensor_scalar(out=isinnw[:, 0:1], in0=stg32[:, 0:1], scalar1=-INVF,
                                        scalar2=None, op0=ALU.mult))
        m_cos = angles(True)
        dv_fc = vop(lambda en: en.tensor_copy(out=fcos, in_=stg32[:]), waits=[m_cos])
        if dbg:
            vop(lambda en: en.tensor_copy(out=dbgS[:, 408:472], in_=stg32[:, 0:64]))
        vop(lambda en: en.tensor_scalar(out=icosw, in0=stg32[:], scalar1=2.0 * INVF, scalar2=None,
                                        op0=ALU.mult))
        dv_ic = vop(lambda en: en.tensor_scalar(out=icosw[:, 0:1], in0=stg32[:, 0:1], scalar1=INVF,
                                                scalar2=None, op0=ALU.mult))
        m = vop(lambda en: en.tensor_copy(out=fcol[:], in_=altcol[:]), waits=[dv_alt])
        pe_at = pop(lambda en: en.transpose(out=p1[0:1, 0:128], in_=fcol[:], identity=identf[:]),
                    waits=[m, (cGP, gp_const[1])])
        dv_ialt = vop(lambda en: en.tensor_scalar(out=rowpA[0:1, 2064:2192], in0=p1[0:1, 0:128],
                                                  scalar1=INVF, scalar2=None, op0=ALU.mult), waits=[pe_at])

        for i in range(8):
            m = dma("sync", ftmp, w_in[i * 128:(i + 1) * 128, :], cLD, waits=[(cDV, cDV.v)])
            pe_m = None
            for k in range(8):
                pd = (p2 if k < 4 else p3)[:, (k % 4) * 128:(k % 4) * 128 + 128]
                pe_m = pop(lambda en, k=k, pd=pd: en.transpose(out=pd, in_=ftmp[:, k * 128:(k + 1) * 128],
                                                               identity=identf[:]),
                           waits=[m, (cDV, cDV.v)] if k == 0 else ())
            for k in range(8):
                pd = (p2 if k < 4 else p3)[:, (k % 4) * 128:(k % 4) * 128 + 128]
                vop(lambda en, k=k, i=i, pd=pd: en.tensor_copy(out=winT[k][:, i * 128:(i + 1) * 128], in_=pd),
                    waits=[pe_m] if k == 0 else ())
        for i in range(8):
            for part, dst in ((c_re, CrT), (c_im, CiT)):
                m = dma("sync", ftmp[:, 0:64], part[i * 128:(i + 1) * 128, :], cLD, waits=[(cDV, cDV.v)])
                pe_m = pop(lambda en: en.transpose(out=p2[0:64, 0:128], in_=ftmp[:, 0:64],
                                                   identity=identf[:]), waits=[m, (cDV, cDV.v)])
                vop(lambda en, i=i, dst=dst: en.tensor_copy(out=dst[:, i * 128:(i + 1) * 128],
                                                            in_=p2[0:64, 0:128]), waits=[pe_m])
        for seg in range(8):
            s0, s1 = seg * 128, (seg + 1) * 128
            vop(lambda en, s0=s0, s1=s1: en.tensor_scalar(out=At1[:, :], in0=CrT[:, s0:s1],
                                                          scalar1=Acol[:, 2:3], scalar2=None, op0=ALU.mult))
            vop(lambda en, s0=s0, s1=s1: en.tensor_scalar(out=At2[:, :], in0=CiT[:, s0:s1],
                                                          scalar1=Acol[:, 3:4], scalar2=None, op0=ALU.mult))
            vop(lambda en, s0=s0, s1=s1: en.tensor_sub(out=gstk[0:64, s0:s1], in0=At1[:, :], in1=At2[:, :]))
            vop(lambda en, s0=s0, s1=s1: en.tensor_scalar(out=At1[:, :], in0=CiT[:, s0:s1],
                                                          scalar1=Acol[:, 2:3], scalar2=None, op0=ALU.mult))
            vop(lambda en, s0=s0, s1=s1: en.tensor_scalar(out=At2[:, :], in0=CrT[:, s0:s1],
                                                          scalar1=Acol[:, 3:4], scalar2=None, op0=ALU.mult))
            vop(lambda en, s0=s0, s1=s1: en.tensor_add(out=gstk[64:128, s0:s1], in0=At1[:, :], in1=At2[:, :]))
            vop(lambda en, s0=s0, s1=s1: en.tensor_copy(out=gswp[0:64, s0:s1], in_=gstk[64:128, s0:s1]))
            vop(lambda en, s0=s0, s1=s1: en.tensor_copy(out=gswp[64:128, s0:s1], in_=gstk[0:64, s0:s1]))
        dv_g = (cDV, cDV.v)

        vop(lambda en: en.tensor_copy(out=stg32[0:64, :], in_=Pvr[:, :]))
        dv_st = vop(lambda en: en.tensor_copy(out=stg32[64:128, :], in_=Pvi[:, :]))
        pe_m = pop(lambda en: en.transpose(out=p2[:, 0:128], in_=stg32[:], identity=identf[:]),
                   waits=[dv_st])
        dv_done = vop(lambda en: en.tensor_copy(out=lhsTz, in_=p2[:, 0:128]), waits=[pe_m])
        for cc in range(NPRE):
            dd = NPRE - 1 - cc
            vop(lambda en, dd=dd: en.tensor_scalar(out=At1[:, :], in0=Pvr[:, :], scalar1=decr[:, dd:dd + 1],
                                                   scalar2=None, op0=ALU.mult), waits=[dv_done])
            vop(lambda en, dd=dd: en.tensor_scalar(out=At2[:, :], in0=Pvi[:, :], scalar1=deci[:, dd:dd + 1],
                                                   scalar2=None, op0=ALU.mult))
            vop(lambda en: en.tensor_sub(out=stg32[0:64, :], in0=At1[:, :], in1=At2[:, :]))
            vop(lambda en, dd=dd: en.tensor_scalar(out=At1[:, :], in0=Pvr[:, :], scalar1=deci[:, dd:dd + 1],
                                                   scalar2=None, op0=ALU.mult))
            vop(lambda en, dd=dd: en.tensor_scalar(out=At2[:, :], in0=Pvi[:, :], scalar1=decr[:, dd:dd + 1],
                                                   scalar2=None, op0=ALU.mult))
            dv_st = vop(lambda en: en.tensor_add(out=stg32[64:128, :], in0=At1[:, :], in1=At2[:, :]))
            pe_m = pop(lambda en: en.transpose(out=p2[:, 0:128], in_=stg32[:], identity=identf[:]),
                       waits=[dv_st])
            dv_done = vop(lambda en, cc=cc: en.tensor_copy(out=lhsTmega[cc], in_=p2[:, 0:128]),
                          waits=[pe_m])

        vop(lambda en: en.tensor_copy(out=stgb[0:64, :], in_=Tr[:, :]))
        dv_st2 = vop(lambda en: en.tensor_scalar(out=stgb[64:128, :], in0=Ti[:, :], scalar1=-1.0,
                                                 scalar2=None, op0=ALU.mult))
        k1 = pop(lambda en: en.matmul(out=pA[:, 0:512], lhsT=stgb, rhs=gstk[:, 0:512], start=True,
                                      stop=True), waits=[dv_st2, dv_g])
        k2 = pop(lambda en: en.matmul(out=pA[:, 512:1024], lhsT=stgb, rhs=gstk[:, 512:1024], start=True,
                                      stop=True))
        vop(lambda en: en.tensor_copy(out=kr0[:, 0:512], in_=pA[:, 0:512]), waits=[k1])
        dv_kr = vop(lambda en: en.tensor_copy(out=kr0[:, 512:1024], in_=pA[:, 512:1024]), waits=[k2])
        for (tblL, tblO) in ((fcos, khR), (fsinn, khI)):
            k1 = pop(lambda en, l=tblL: en.matmul(out=pA[:, 0:512], lhsT=l, rhs=kr0[:, 0:512],
                                                  start=True, stop=True), waits=[dv_kr, dv_fc, dv_fs, dv_ic])
            k2 = pop(lambda en, l=tblL: en.matmul(out=pA[:, 512:1024], lhsT=l, rhs=kr0[:, 512:1024],
                                                  start=True, stop=True))
            vop(lambda en, o=tblO: en.tensor_copy(out=o[:, 0:512], in_=pA[:, 0:512]), waits=[k1])
            dv_kr = vop(lambda en, o=tblO: en.tensor_copy(out=o[:, 512:1024], in_=pA[:, 512:1024]),
                        waits=[k2])
        k1 = pop(lambda en: en.matmul(out=pA[0:1, 0:512], lhsT=altcol[:], rhs=kr0[:, 0:512], start=True,
                                      stop=True), waits=[dv_kr, dv_alt])
        k2 = pop(lambda en: en.matmul(out=pA[0:1, 512:1024], lhsT=altcol[:], rhs=kr0[:, 512:1024],
                                      start=True, stop=True))
        vop(lambda en: en.tensor_copy(out=rowpB[0:1, 3072:3584], in_=pA[0:1, 0:512]), waits=[k1])
        dv_kh8 = vop(lambda en: en.tensor_copy(out=rowpB[0:1, 3584:4096], in_=pA[0:1, 512:1024]),
                     waits=[k2])

        for ei in range(8):
            m = dma("sync", stg32[0:32, :], b1[ei, :].rearrange("(m p) -> m p", p=128), cLD,
                    waits=[(cDV, cDV.v)])
            pe_m = pop(lambda en: en.transpose(out=p2[:, 0:32], in_=stg32[0:32, :],
                                               identity=identf[0:32, 0:32]),
                       waits=[m, (cDV, cDV.v)])
            vop(lambda en, ei=ei: en.tensor_copy(out=b1c[:, ei * 32:(ei + 1) * 32], in_=p2[:, 0:32]),
                waits=[pe_m])

        vop(lambda en: en.memset(carry[:, :], 0.0))
        B.force_ss = False
        dv_setup = (cDV, cDV.v)
        gp_done = (cGP, cGP.v)
        pe_setup = (cPE, cPE.v)
        ac_setup = (cAC, cAC.v)

        last_pe_uuse = pe_setup
        pe_uuse_slot = [pe_setup, pe_setup]
        last_dv_xfm = dv_setup
        last_dv_x1pre = dv_setup
        last_dv_cum = dv_setup
        last_ac_st2 = ac_setup
        last_mscr = None
        last_dv_zg = dv_setup
        last_dv_sums = dv_setup
        last_ac_stats = ac_setup
        last_dv_soft = dv_setup
        last_p01_free = dv_setup
        last_dv_xfm1 = dv_setup
        free_x = [dv_setup, dv_setup]
        sg_ready = None
        zg_marks = [None] * NOWN
        x1f_written = []
        moe_acc_marks = {}

        uf32B = [uf32, uf32]
        last_dv_fft = dv_setup

        def head(t):
            nonlocal last_dv_xfm, last_p01_free, last_ac_stats, last_dv_x1pre
            own = t >= NPRE
            sl = t % 2
            rows = slice(t * 128, (t + 1) * 128)
            mx = dma("sync", sx[sl][:], xfull[rows, :], cLD, waits=[free_x[sl]])
            mxb = None
            if own:
                mxb = dma("gpsimd", sxb[sl], xfull[rows, :], cGL, waits=[free_x[sl]])
            tp_last = None
            for k in range(8):
                pd = (p0 if k < 4 else p1)[:, (k % 4) * 128:(k % 4) * 128 + 128]
                w = [mx]
                if k == 0:
                    w += [last_dv_xfm, last_p01_free, gp_done]
                tp_last = pop(lambda en, k=k, pd=pd, s=sl: en.transpose(
                    out=pd, in_=sx[s][:, k * 128:(k + 1) * 128], identity=identf[:]), waits=w)
            dvm = None
            for k in range(8):
                dvm = vop(lambda en, k=k: en.tensor_copy(
                    out=xfm[:, k * 128:(k + 1) * 128],
                    in_=(p0 if k < 4 else p1)[:, (k % 4) * 128:(k % 4) * 128 + 128]),
                    waits=[tp_last] if k == 0 else [])
            last_dv_xfm = dvm
            if not own:
                free_x[sl] = dvm
            zm = None
            for half in range(2):
                pd = p0 if half == 0 else p1
                hs = slice(half * 512, (half + 1) * 512)
                for k in range(8):
                    pop(lambda en, k=k, pd=pd, hs=hs: en.matmul(
                        out=pd[:], lhsT=xfm[:, k * 128:(k + 1) * 128], rhs=winT[k][:, hs],
                        start=(k == 0), stop=False),
                        waits=[last_dv_xfm] if (k == 0 and half == 0) else ())
                pop(lambda en, pd=pd, hs=hs: en.matmul(out=pd[:], lhsT=onesall[0:1, :],
                                                       rhs=rowpA[0:1, hs], start=False, stop=False))
                zm = pop(lambda en, pd=pd, hs=hs: en.matmul(
                    out=pd[:], lhsT=onesall[0:1, :],
                    rhs=rowpA[0:1, 1024 + hs.start:1024 + hs.stop], start=False, stop=True))
            return dict(t=t, mx=mx, mxb=mxb, zm=zm, sl=sl, own=own)

        def stats_part(hh):
            nonlocal last_p01_free, last_ac_stats
            t = hh["t"]
            sl = t % 2
            uf = uf32B[sl]
            ac_st = ln_stats(p0[:], p1[:], True, waits=[hh["zm"], last_dv_x1pre, (cDV, cDV.v), gp_done],
                             uf_dst=uf, sb=sl * 16)
            last_ac_stats = ac_st
            last_p01_free = ac_st
            u_hi = u_hiB[sl]
            u_lo = u_loB[sl]
            vop(lambda en, uf=uf, sl=sl: en.tensor_scalar(
                out=uf, in0=uf, scalar1=stat[:, sl * 16 + 7:sl * 16 + 8],
                scalar2=stat[:, sl * 16 + 8:sl * 16 + 9], op0=ALU.mult, op1=ALU.add),
                waits=[ac_st, pe_uuse_slot[sl]])
            vop(lambda en, uf=uf: en.tensor_mul(out=uf, in0=uf, in1=grep))
            vop(lambda en, uf=uf: en.tensor_add(out=uf, in0=uf, in1=brep))
            vop(lambda en, uf=uf, uh=u_hi: en.tensor_copy(out=uh, in_=uf))
            hh["dv_u"] = vop(lambda en, uf=uf, uh=u_hi, ul=u_lo: en.tensor_sub(out=ul, in0=uf, in1=uh))

        def mm_part(hh):
            nonlocal last_pe_uuse, last_dv_xfm1, last_dv_x1pre, last_dv_cum, last_dv_zg
            nonlocal last_dv_sums, last_dv_soft, last_ac_st2, last_mscr, sg_ready, last_dv_fft
            t = hh["t"]
            own = t >= NPRE
            co = t - NPRE
            sl = t % 2
            u_hi = u_hiB[sl]
            u_lo = u_loB[sl]
            mxb = hh["mxb"]
            dv_u = hh["dv_u"]

            if not own:
                w0 = [dv_u] + ([last_dv_x1pre] if t == 0 else [])
                pop(lambda en, t=t, uh=u_hi: en.matmul(out=pA[:, 0:512], lhsT=lhsTmega[t],
                                                       rhs=uh[:, 0:512],
                                                       start=(t == 0), stop=(t == NPRE - 1)), waits=w0)
                pe_mega = pop(lambda en, t=t, uh=u_hi: en.matmul(out=pA[:, 512:1024], lhsT=lhsTmega[t],
                                                                 rhs=uh[:, 512:1024],
                                                                 start=(t == 0), stop=(t == NPRE - 1)))
                pop(lambda en, uh=u_hi: en.matmul(out=p2[0:1, :], lhsT=onescol[:], rhs=uh[:, 0:512],
                                                  start=True, stop=False),
                    waits=[last_dv_sums, (cDV, cDV.v)])
                pop(lambda en, ul=u_lo: en.matmul(out=p2[0:1, :], lhsT=onescol[:], rhs=ul[:, 0:512],
                                                  start=False, stop=True))
                pop(lambda en, uh=u_hi: en.matmul(out=p3[0:1, :], lhsT=onescol[:], rhs=uh[:, 512:1024],
                                                  start=True, stop=False))
                pe_sums = pop(lambda en, ul=u_lo: en.matmul(out=p3[0:1, :], lhsT=onescol[:],
                                                            rhs=ul[:, 512:1024], start=False, stop=True))
                last_pe_uuse = pe_sums
                pe_uuse_slot[sl] = pe_sums
                vop(lambda en: en.tensor_copy(out=crow[:, 0:512], in_=p2[0:1, :]),
                    waits=[pe_sums], ss=True)
                vop(lambda en: en.tensor_copy(out=crow[:, 512:1024], in_=p3[0:1, :]), ss=True)
                last_dv_sums = vop(lambda en: en.tensor_add(out=carry[:, :], in0=carry[:, :],
                                                            in1=crow[:, :]), ss=True)
                if t == NPRE - 1:
                    vop(lambda en: en.scalar_tensor_tensor(out=ufR[0:64, :], in0=pA[0:64, :],
                                                           scalar=selc[0:64, 0:1], in1=gstk[0:64, :],
                                                           op0=ALU.mult, op1=ALU.mult),
                        waits=[pe_mega, ld_small])
                    vop(lambda en: en.scalar_tensor_tensor(out=ufI[0:64, :], in0=pA[64:128, :],
                                                           scalar=selc[0:64, 0:1], in1=gswp[0:64, :],
                                                           op0=ALU.mult, op1=ALU.mult))
                    vop(lambda en: en.tensor_sub(out=yR[0:64, :], in0=ufR[0:64, :], in1=ufI[0:64, :]))
                    vop(lambda en: en.scalar_tensor_tensor(out=ufR[0:64, :], in0=pA[0:64, :],
                                                           scalar=selc[0:64, 0:1], in1=gswp[0:64, :],
                                                           op0=ALU.mult, op1=ALU.mult))
                    vop(lambda en: en.scalar_tensor_tensor(out=ufI[0:64, :], in0=pA[64:128, :],
                                                           scalar=selc[0:64, 0:1], in1=gstk[0:64, :],
                                                           op0=ALU.mult, op1=ALU.mult))
                    sg_ready = vop(lambda en: en.tensor_add(out=yR[64:128, :], in0=ufR[0:64, :],
                                                            in1=ufI[0:64, :]))
                    last_dv_x1pre = sg_ready
                    last_dv_sums = vop(lambda en: en.tensor_scalar(out=carry[:, :], in0=carry[:, :],
                                                                   scalar1=selc[0:1, 0:1], scalar2=None,
                                                                   op0=ALU.mult))
                return

            dv_x0lo = vop(lambda en, s=sl: en.tensor_sub(out=x0lo, in0=sx[s][:], in1=sxb[s]),
                          waits=[hh["mx"], mxb, pe_uuse_slot[0], pe_uuse_slot[1]])
            if co == 0:
                for i in range(8):
                    sg_ready = vop(lambda en, i=i: en.tensor_copy(
                        out=lhsTmega[i], in_=yR[:, i * 128:(i + 1) * 128]), waits=[sg_ready])
            pz1 = pop(lambda en, uh=u_hi: en.matmul(out=p2[:], lhsT=lhsTz, rhs=uh[:, 0:512], start=True,
                                                    stop=True),
                      waits=[dv_u, last_dv_zg, last_dv_sums, last_dv_fft])
            pz2 = pop(lambda en, uh=u_hi: en.matmul(out=p3[:], lhsT=lhsTz, rhs=uh[:, 512:1024],
                                                    start=True, stop=True))
            vop(lambda en: en.tensor_mul(out=ufR[0:64, 0:512], in0=gstk[0:64, 0:512], in1=p2[0:64, :]),
                waits=[pz1, sg_ready])
            vop(lambda en: en.tensor_mul(out=ufR[0:64, 512:1024], in0=gstk[0:64, 512:1024],
                                         in1=p3[0:64, :]), waits=[pz2])
            vop(lambda en: en.tensor_mul(out=ufI[0:64, 0:512], in0=gswp[0:64, 0:512], in1=p2[64:128, :]))
            vop(lambda en: en.tensor_mul(out=ufI[0:64, 512:1024], in0=gswp[0:64, 512:1024],
                                         in1=p3[64:128, :]))
            vop(lambda en, co=co: en.tensor_sub(out=ZG[co][0:64, :], in0=ufR[0:64, :], in1=ufI[0:64, :]))
            vop(lambda en: en.tensor_mul(out=ufR[0:64, 0:512], in0=gswp[0:64, 0:512], in1=p2[0:64, :]))
            vop(lambda en: en.tensor_mul(out=ufR[0:64, 512:1024], in0=gswp[0:64, 512:1024],
                                         in1=p3[0:64, :]))
            vop(lambda en: en.tensor_mul(out=ufI[0:64, 0:512], in0=gstk[0:64, 0:512], in1=p2[64:128, :]))
            vop(lambda en: en.tensor_mul(out=ufI[0:64, 512:1024], in0=gstk[0:64, 512:1024],
                                         in1=p3[64:128, :]))
            dv_zg = vop(lambda en, co=co: en.tensor_add(out=ZG[co][64:128, :], in0=ufR[0:64, :],
                                                        in1=ufI[0:64, :]))
            zg_marks[co] = dv_zg
            last_dv_zg = dv_zg
            pop(lambda en, uh=u_hi: en.matmul(out=p2[0:1, :], lhsT=onescol[:], rhs=uh[:, 0:512],
                                              start=True, stop=False), waits=[dv_zg, last_dv_sums])
            pop(lambda en, ul=u_lo: en.matmul(out=p2[0:1, :], lhsT=onescol[:], rhs=ul[:, 0:512],
                                              start=False, stop=True))
            pop(lambda en, uh=u_hi: en.matmul(out=p3[0:1, :], lhsT=onescol[:], rhs=uh[:, 512:1024],
                                              start=True, stop=False))
            pe_sums = pop(lambda en, ul=u_lo: en.matmul(out=p3[0:1, :], lhsT=onescol[:],
                                                        rhs=ul[:, 512:1024], start=False, stop=True))
            vop(lambda en: en.tensor_copy(out=rowpB[0:1, 0:1024], in_=carry[:, :]),
                waits=[last_dv_sums], ss=True)
            dv_cr = vop(lambda en: en.tensor_sub(out=rowpB[0:1, 1024:2048], in0=carry[:, :],
                                                 in1=rowpB[0:1, 0:1024]), ss=True)
            vop(lambda en: en.tensor_copy(out=crow[:, 0:512], in_=p2[0:1, :]), waits=[pe_sums], ss=True)
            vop(lambda en: en.tensor_copy(out=crow[:, 512:1024], in_=p3[0:1, :]), ss=True)
            dv_se = vop(lambda en: en.tensor_add(out=carry[:, :], in0=carry[:, :], in1=crow[:, :]),
                        ss=True)
            last_dv_sums = dv_se
            pe_accB = None
            for half in range(2):
                hs = slice(half * 512, (half + 1) * 512)
                pd = pB[:, hs]
                seq = [(tri, u_hi[:, hs], [dv_cr, last_dv_cum, last_dv_soft] if half == 0 else []),
                       (tri, u_lo[:, hs], []),
                       (onesall[0:1, :], rowpB[0:1, hs], []),
                       (onesall[0:1, :], rowpB[0:1, 1024 + hs.start:1024 + hs.stop], [])]
                for i, (l, r, w) in enumerate(seq):
                    pe_accB = pop(lambda en, l=l, r=r, pd=pd, st=(i == 0), sp=(i == len(seq) - 1):
                                  en.matmul(out=pd, lhsT=l, rhs=r, start=st, stop=sp), waits=w)
            f1 = pop(lambda en, uh=u_hi: en.matmul(out=p2[:], lhsT=fcos, rhs=uh[:, 0:512], start=True,
                                                   stop=True), waits=[dv_u, dv_se])
            f2 = pop(lambda en, uh=u_hi: en.matmul(out=p3[:], lhsT=fcos, rhs=uh[:, 512:1024], start=True,
                                                   stop=True))
            vop(lambda en: en.tensor_copy(out=ufR[:, 0:512], in_=p2[:]), waits=[f1])
            dv_ufr = vop(lambda en: en.tensor_copy(out=ufR[:, 512:1024], in_=p3[:]), waits=[f2])
            f1 = pop(lambda en, uh=u_hi: en.matmul(out=p2[:], lhsT=fsinn, rhs=uh[:, 0:512], start=True,
                                                   stop=True), waits=[dv_ufr])
            f2 = pop(lambda en, uh=u_hi: en.matmul(out=p3[:], lhsT=fsinn, rhs=uh[:, 512:1024],
                                                   start=True, stop=True))
            pop(lambda en, uh=u_hi: en.matmul(out=pA[0:1, 0:512], lhsT=altcol[:], rhs=uh[:, 0:512],
                                              start=True, stop=True), waits=[last_dv_xfm1])
            f8 = pop(lambda en, uh=u_hi: en.matmul(out=pA[0:1, 512:1024], lhsT=altcol[:],
                                                   rhs=uh[:, 512:1024], start=True, stop=True))
            vop(lambda en: en.tensor_copy(out=ufI[:, 0:512], in_=p2[:]), waits=[f1])
            vop(lambda en: en.tensor_copy(out=ufI[:, 512:1024], in_=p3[:]), waits=[f2])
            dv_ufi = vop(lambda en: en.engine_nop())
            last_dv_fft = dv_ufi
            vop(lambda en: en.tensor_mul(out=yR, in0=ufR, in1=khR), waits=[dv_ufi, dv_kh8, dv_ialt])
            vop(lambda en: en.tensor_mul(out=yI, in0=ufI, in1=khI))
            vop(lambda en: en.tensor_sub(out=yR, in0=yR, in1=yI))
            vop(lambda en: en.tensor_mul(out=yI, in0=ufI, in1=khR))
            vop(lambda en: en.tensor_mul(out=ufI, in0=ufR, in1=khI))
            vop(lambda en: en.tensor_add(out=yI, in0=yI, in1=ufI))
            vop(lambda en: en.tensor_mul(out=rowpB[0:1, 2048:2560], in0=pA[0:1, 0:512],
                                         in1=rowpB[0:1, 3072:3584]), waits=[f8], ss=True)
            dv_y = vop(lambda en: en.tensor_mul(out=rowpB[0:1, 2560:3072], in0=pA[0:1, 512:1024],
                                                in1=rowpB[0:1, 3584:4096]), ss=True)
            last_dv_zg = dv_y
            pe_accA = None
            for half in range(2):
                hs = slice(half * 512, (half + 1) * 512)
                pd = pA[:, hs]
                seq = []
                seq.append((icosw, yR[:, hs], [dv_y, last_dv_x1pre, last_dv_xfm1] if half == 0 else []))
                seq.append((isinnw, yI[:, hs], []))
                seq.append((rowpA[0:1, 2064:2192], rowpB[0:1, 2048 + hs.start:2048 + hs.stop], []))
                for cp in range(co):
                    seq.append((lhsTD[co - 1 - cp], ZG[cp][:, hs],
                                [zg_marks[cp]] if half == 0 else []))
                seq.append((lhsTD[co], sg_home[:, hs], [sg_ready] if half == 0 else []))
                seq.append((identb, u_hi[:, hs], []))
                seq.append((identb, u_lo[:, hs], []))
                seq.append((identb, sxb[sl][:, hs], [mxb] if half == 0 else []))
                seq.append((identb, x0lo[:, hs], [dv_x0lo] if half == 0 else []))
                for i, (l, r, w) in enumerate(seq):
                    pe_accA = pop(lambda en, l=l, r=r, pd=pd, st=(i == 0), sp=(i == len(seq) - 1):
                                  en.matmul(out=pd, lhsT=l, rhs=r, start=st, stop=sp), waits=w)
            last_pe_uuse = pe_accB
            pe_uuse_slot[sl] = pe_accA
            vop(lambda en: en.tensor_mul(out=x1pre, in0=drep, in1=pB[:]),
                waits=[pe_accB, last_ac_st2])
            dv_pre = vop(lambda en: en.tensor_add(out=x1pre, in0=x1pre, in1=pA[:]), waits=[pe_accA])
            last_dv_x1pre = dv_pre
            last_dv_cum = dv_pre
            free_x[sl] = dv_pre
            ac_st2 = ln_stats(x1pre[:, 0:512], x1pre[:, 512:1024], False,
                              waits=[dv_pre, (cDV, cDV.v)])
            last_ac_st2 = ac_st2
            dv_x1 = vop(lambda en: en.tensor_scalar(out=x1f, in0=x1pre, scalar1=stat[:, 39:40],
                                                    scalar2=stat[:, 40:41], op0=ALU.mult, op1=ALU.add),
                        waits=[ac_st2] + ([last_mscr] if last_mscr else []))
            mscr = dma("sync", x1scr[co * 128:(co + 1) * 128, :], x1f, cLD, waits=[dv_x1])
            x1f_written.append(mscr)
            last_mscr = mscr
            x1tp = None
            for k in range(8):
                pd = pA[:, k * 128:(k + 1) * 128]
                x1tp = pop(lambda en, k=k, pd=pd: en.transpose(
                    out=pd, in_=x1f[:, k * 128:(k + 1) * 128], identity=identf[:]),
                    waits=[dv_x1] if k == 0 else ())
            dv_xfm1 = None
            for k in range(8):
                dv_xfm1 = vop(lambda en, k=k, co=co: en.tensor_copy(
                    out=x1fm[k][:, co * 128:(co + 1) * 128], in_=pA[:, k * 128:(k + 1) * 128]),
                    waits=[x1tp] if k == 0 else [])
            last_dv_xfm1 = dv_xfm1
            pop(lambda en, co=co: en.matmul(out=pB[:, 0:8], lhsT=x1fm[0][:, co * 128:(co + 1) * 128],
                                            rhs=wrbf[:, 0:8], start=True, stop=False),
                waits=[dv_xfm1, gl_b2])
            for k in range(1, 8):
                pop(lambda en, k=k, co=co: en.matmul(out=pB[:, 0:8],
                                                     lhsT=x1fm[k][:, co * 128:(co + 1) * 128],
                                                     rhs=wrbf[:, k * 8:(k + 1) * 8], start=False,
                                                     stop=False))
            pop(lambda en: en.matmul(out=pB[:, 0:8], lhsT=onesall[0:1, :],
                                     rhs=rowpA[0:1, 2048:2056], start=False, stop=False))
            pe_rt = pop(lambda en: en.matmul(out=pB[:, 0:8], lhsT=onesall[0:1, :],
                                             rhs=rowpA[0:1, 2056:2064], start=False, stop=True))
            vop(lambda en: en.reduce_max(out=stat[:, 9:10], in_=pB[:, 0:8], axis=AX.X),
                waits=[pe_rt], ss=True)
            vop(lambda en: en.tensor_scalar(out=l2t[:], in0=pB[:, 0:8], scalar1=stat[:, 9:10],
                                            scalar2=None, op0=ALU.subtract), ss=True)
            vop(lambda en: en.tensor_single_scalar(out=msk8[:], in_=l2t[:], scalar=0.0, op=ALU.is_ge),
                ss=True)
            dv_l2 = vop(lambda en: en.tensor_tensor_scan(out=cs8[:], data0=msk8[:], data1=ones8[:],
                                                         initial=0.0, op0=ALU.add, op1=ALU.mult), ss=True)
            ac_e = aop(lambda en: en.activation(out=exp8[:], in_=l2t[:], func=AF.Exp), waits=[dv_l2])
            vop(lambda en: en.tensor_single_scalar(out=cs8[:], in_=cs8[:], scalar=1.5, op=ALU.is_le),
                ss=True)
            vop(lambda en: en.tensor_mul(out=msk8[:], in0=msk8[:], in1=cs8[:]), ss=True)
            vop(lambda en: en.reduce_sum(out=stat[:, 10:11], in_=exp8[:], axis=AX.X), waits=[ac_e],
                ss=True)
            vop(lambda en: en.reciprocal(out=stat[:, 11:12], in_=stat[:, 10:11]), ss=True)
            last_dv_soft = vop(lambda en, co=co: en.tensor_scalar(
                out=geAll[:, co * 8:(co + 1) * 8], in0=msk8[:], scalar1=stat[:, 11:12], scalar2=None,
                op0=ALU.mult), ss=True)

        if run_attn:
            ts_list = list(range(n_pre)) + list(range(NPRE, NPRE + n_own))
            prev = None
            for t in ts_list:
                hh = head(t)
                stats_part(hh)
                if prev is not None:
                    mm_part(prev)
                prev = hh
            if prev is not None:
                mm_part(prev)

        attn_done = (cDV, cDV.v)
        pe_attn_done = (cPE, cPE.v)
        ld_attn_done = (cLD, cLD.v)
        pe_hdone = [None] * NE
        gl_w1 = [None] * NE
        last_dv_gate = attn_done
        prev_gate = attn_done
        ac_hev = {}
        w2c_marks = {}

        for ei in range(n_experts):
            if ei == 0:
                w = [attn_done, pe_attn_done]
                for k in range(8):
                    dma("gpsimd", w1res[k], w1[ei, k * 128:(k + 1) * 128, :], cGL,
                        waits=w if k == 0 else ())
                gl_w1[ei] = (cGL, cGL.v)
            for tt in range(NTT):
                ts = slice(tt * TT_TOK, (tt + 1) * TT_TOK)
                prev_gate = last_dv_gate
                for mt in range(32):
                    pd = p0 if mt % 2 == 0 else p1
                    w = []
                    if mt == 0:
                        w = [gl_w1[ei], last_dv_gate]
                    elif mt == 1:
                        w = [last_dv_gate]
                    else:
                        pm = ac_hev.get((ei, tt, mt - 2))
                        if pm:
                            w = [pm]
                    for k in range(8):
                        pop(lambda en, k=k, mt=mt, pd=pd, ts=ts: en.matmul(
                            out=pd[:], lhsT=w1res[k][:, mt * 128:(mt + 1) * 128], rhs=x1fm[k][:, ts],
                            start=(k == 0), stop=(k == 7)), waits=w if k == 0 else ())
                    pe_h = (cPE, cPE.v)
                    if mt == 31:
                        pe_hdone[ei] = pe_h
                    ac_hev[(ei, tt, mt)] = aop(lambda en, mt=mt, pd=pd, ei=ei: en.activation(
                        out=hbuf[:, mt * TT_TOK:(mt + 1) * TT_TOK], in_=pd[:], func=AF.Relu,
                        bias=b1c[:, ei * 32 + mt:ei * 32 + mt + 1]), waits=[pe_h])
                ps_halves = [(pA[:, 0:512], pA[:, 512:1024]), (pB[:, 0:512], pB[:, 512:1024]),
                             (p0[:], p1[:]), (p2[:], p3[:])]
                pe_w2 = None
                for k in range(32):
                    slot = k % 4
                    wd = []
                    pm = w2c_marks.get((ei, tt, k - 4))
                    if pm:
                        wd.append(pm)
                    if k < 4:
                        wd.append(prev_gate)
                    m = dma("gpsimd", w2buf[slot], w2[ei, k * 128:(k + 1) * 128, :], cGL, waits=wd)
                    if tt == NTT - 1 and ei + 1 < n_experts and 8 <= k < 16:
                        kk = k - 8
                        dma("gpsimd", w1res[kk], w1[ei + 1, kk * 128:(kk + 1) * 128, :], cGL,
                            waits=[pe_hdone[ei]] if kk == 0 else ())
                        if kk == 7:
                            gl_w1[ei + 1] = (cGL, cGL.v)
                    for tM in range(4):
                        for hf in range(2):
                            w = []
                            if tM == 0 and hf == 0:
                                w = [m]
                                if k == 0:
                                    w += [prev_gate, ac_hev[(ei, tt, 31)]]
                            pe_w2 = pop(lambda en, k=k, tM=tM, hf=hf, slot=slot: en.matmul(
                                out=ps_halves[tM][hf],
                                lhsT=hbuf[:, k * TT_TOK + tM * 128:k * TT_TOK + (tM + 1) * 128],
                                rhs=w2buf[slot][:, hf * 512:(hf + 1) * 512],
                                start=(k == 0), stop=(k == 31)), waits=w)
                    w2c_marks[(ei, tt, k)] = pe_w2

                dvg = None
                for tM in range(4):
                    g = tt * 4 + tM
                    w = [pe_w2, last_dv_soft]
                    if ei == 0 and tt == 0:
                        w.append(ld_attn_done)
                    pm = moe_acc_marks.get((g, ei - 1))
                    if pm:
                        w.append(pm)
                    gcol = geAll[:, g * 8 + ei:g * 8 + ei + 1]
                    if tM == 0:
                        dvg = vop(lambda en, gc=gcol: en.tensor_scalar(out=tmpg[0], in0=pA[:], scalar1=gc,
                                                                       scalar2=None, op0=ALU.mult), waits=w)
                    elif tM == 1:
                        dvg = vop(lambda en, gc=gcol: en.tensor_scalar(out=tmpg[1], in0=pB[:], scalar1=gc,
                                                                       scalar2=None, op0=ALU.mult), waits=w)
                    else:
                        pp = (p0, p1) if tM == 2 else (p2, p3)
                        vop(lambda en, gc=gcol, pp=pp, tM=tM: en.tensor_scalar(
                            out=tmpg[tM][:, 0:512], in0=pp[0][:], scalar1=gc, scalar2=None,
                            op0=ALU.mult), waits=w)
                        dvg = vop(lambda en, gc=gcol, pp=pp, tM=tM: en.tensor_scalar(
                            out=tmpg[tM][:, 512:1024], in0=pp[1][:], scalar1=gc, scalar2=None,
                            op0=ALU.mult))
                    gr = slice(g * 128, (g + 1) * 128)
                    mden = OP("gpsimd", lambda en, gr=gr, tM=tM, ei=ei: en.dma_start(
                        out=accd[gr, :], in_=tmpg[tM],
                        accum_op=(ALU.bypass if ei == 0 else ALU.add)), cGL, k=16, waits=[dvg])
                    moe_acc_marks[(g, ei)] = mden
                last_dv_gate = dvg

        moe_all_gl = (cGL, cGL.v)
        moe_all_dv = (cDV, cDV.v)

        st_marks = []
        for g in range(16 if run_final else 0):
            sl = g % 2
            gr = slice(g * 128, (g + 1) * 128)
            w = []
            if n_experts > 0:
                w.append(moe_acc_marks[(g, n_experts - 1)])
            if run_attn and g < n_own:
                w.append(x1f_written[g])
            if g == 0:
                w += [moe_all_gl, moe_all_dv]
            if len(st_marks) >= 2:
                w.append(st_marks[-2])
            m1 = dma("sync", x1g[sl], x1scr[gr, :], cLD, waits=w)
            m2 = dma("sync", accg[sl], accd[gr, :], cLD)
            ptp = pop(lambda en, g=g: en.transpose(out=p2[0:8, 0:128], in_=geAll[:, g * 8:(g + 1) * 8],
                                                   identity=identf[:]),
                      waits=([moe_all_dv, (cDV, cDV.v)] if True else []))
            dge = vop(lambda en: en.tensor_copy(out=geT[:], in_=p2[0:8, 0:128]), waits=[ptp])
            pb2 = None
            for hf in range(2):
                pd = p0 if hf == 0 else p1
                pop(lambda en, pd=pd, hf=hf: en.matmul(out=pd[:], lhsT=geT[:],
                                                       rhs=b2hi_t[:, hf * 512:(hf + 1) * 512],
                                                       start=True, stop=False), waits=[dge])
                pb2 = pop(lambda en, pd=pd, hf=hf: en.matmul(out=pd[:], lhsT=geT[:],
                                                             rhs=b2lo_t[:, hf * 512:(hf + 1) * 512],
                                                             start=False, stop=True))
            dv1 = vop(lambda en, sl=sl: en.tensor_add(out=resb, in0=x1g[sl], in1=accg[sl]),
                      waits=[m1, m2, (cAC, cAC.v)])
            vop(lambda en: en.tensor_add(out=resb[:, 0:512], in0=resb[:, 0:512], in1=p0[:]),
                waits=[pb2])
            dv1 = vop(lambda en: en.tensor_add(out=resb[:, 512:1024], in0=resb[:, 512:1024], in1=p1[:]))
            ac3 = ln_stats(resb[:, 0:512], resb[:, 512:1024], False, waits=[dv1])
            dv2 = vop(lambda en, sl=sl: en.tensor_scalar(out=outb[sl], in0=resb, scalar1=stat[:, 39:40],
                                                         scalar2=stat[:, 40:41], op0=ALU.mult,
                                                         op1=ALU.add), waits=[ac3])
            st_marks.append(dma("sync", out[gr, :], outb[sl], cST, waits=[dv2]))
        if dbg:
            B.wait("vector", (cPE, cPE.v))
            B.wait("vector", (cAC, cAC.v))
            dcp = vop(lambda en: en.tensor_copy(out=ftmp, in_=uf32))
            dma("sync", dbg1[:, 0:1024], ftmp, cST, waits=[dcp])
            dma("sync", dbg1[:, 1024:1152], fcos, cST)
            B.plan["sync"].pop()
            cST.v -= 16
            dcp = vop(lambda en: en.tensor_copy(out=ftmp[:, 0:128], in_=fcos))
            dcp = vop(lambda en: en.tensor_copy(out=ftmp[:, 128:256], in_=icosw))
            dcp = vop(lambda en: en.tensor_copy(out=ftmp[:, 256:384], in_=lhsTD[0]))
            dcp = vop(lambda en: en.tensor_copy(out=ftmp[:, 384:512], in_=lhsTz))
            dcp = vop(lambda en: en.tensor_copy(out=ftmp[:, 512:640], in_=tri))
            dma("sync", dbg1[:, 1024:1664], ftmp[:, 0:640], cST, waits=[dcp])
            dcp = vop(lambda en: en.tensor_copy(out=ftmp, in_=khR))
            dma("sync", dbg1[:, 2048:3072], ftmp, cST, waits=[dcp])
            dcp = vop(lambda en: en.tensor_copy(out=ftmp, in_=u_hi))
            dma("sync", dbg1[:, 3072:4096], ftmp, cST, waits=[dcp])
            dcp = vop(lambda en: en.tensor_copy(out=ftmp, in_=gstk))
            dma("sync", dbg1[:, 4096:5120], ftmp, cST, waits=[dcp])
            dcp = vop(lambda en: en.tensor_copy(out=ftmp, in_=ZG[15]))
            dma("sync", dbg1[:, 5120:6144], ftmp, cST, waits=[dcp])
            dma("sync", dbg1[:, 6144:6656], dbgS[:, :], cST)
            dcp = vop(lambda en: en.tensor_copy(out=ftmp[0:1, 256:1024], in_=carry[:, 0:768]))
            dma("sync", dbg1[:, 6656:7680], ftmp, cST, waits=[dcp])
            dcp = vop(lambda en: en.tensor_copy(out=ftmp, in_=x1pre))
            dma("sync", dbg1[:, 7168:8192], ftmp, cST, waits=[dcp])
        B.plan["sync"].append(("wait", cST.h, cST.v))

        B.emit()
    return nc


_NC_CACHE = None


def kernel(**inp):
    global _NC_CACHE
    x = np.ascontiguousarray(np.asarray(inp["L0_attn_x"], dtype=np.float32))
    _ = np.asarray(inp["L0_attn_D_im"])
    ins = {
        "w_in": np.ascontiguousarray(np.asarray(inp["L0_attn_W_in"], np.float32)),
        "b_in": np.ascontiguousarray(np.asarray(inp["L0_attn_b_in"], np.float32)),
        "gamma": np.ascontiguousarray(np.asarray(inp["L0_attn_gamma"], np.float32)),
        "beta": np.ascontiguousarray(np.asarray(inp["L0_attn_beta"], np.float32)),
        "a_re": np.ascontiguousarray(np.asarray(inp["L0_attn_A_re"], np.float32)),
        "a_im": np.ascontiguousarray(np.asarray(inp["L0_attn_A_im"], np.float32)),
        "b_re": np.ascontiguousarray(np.asarray(inp["L0_attn_B_re"], np.float32)),
        "b_im": np.ascontiguousarray(np.asarray(inp["L0_attn_B_im"], np.float32)),
        "c_re": np.ascontiguousarray(np.asarray(inp["L0_attn_C_re"], np.float32)),
        "c_im": np.ascontiguousarray(np.asarray(inp["L0_attn_C_im"], np.float32)),
        "d_re": np.ascontiguousarray(np.asarray(inp["L0_attn_D_re"], np.float32)),
        "r_w": np.ascontiguousarray(np.asarray(inp["L0_ffn_router_W"], np.float32)),
        "r_b": np.ascontiguousarray(np.asarray(inp["L0_ffn_router_b"], np.float32)),
        "w1": np.ascontiguousarray(np.asarray(inp["L0_ffn_W1"], np.float32)),
        "b1": np.ascontiguousarray(np.asarray(inp["L0_ffn_b1"], np.float32)),
        "w2": np.ascontiguousarray(np.asarray(inp["L0_ffn_W2"], np.float32)),
        "b2": np.ascontiguousarray(np.asarray(inp["L0_ffn_b2"], np.float32)),
    }
    if _NC_CACHE is None:
        _NC_CACHE = build_program()
    nc = _NC_CACHE
    in_maps = []
    for cid in range(8):
        b, h = cid >> 1, cid & 1
        xf = np.concatenate([x[b, 0:2048], x[b, h * 2048:(h + 1) * 2048]], axis=0)
        m = dict(ins)
        m["xfull"] = np.ascontiguousarray(xf)
        m["sel"] = np.full((128, 1), float(h), np.float32)
        in_maps.append(m)
    res = run_bass_kernel_spmd(nc, in_maps, core_ids=list(range(8)))
    outa = np.empty((4, 4096, D), np.float32)
    for cid, r in enumerate(res.results):
        b, h = cid >> 1, cid & 1
        outa[b, h * 2048:(h + 1) * 2048] = r["out"]
    return outa


# version 09 (final)
# speedup vs baseline: 1.0100x; 1.0100x over previous
import numpy as np
from contextlib import ExitStack

import concourse.bass as bass
import concourse.mybir as mybir
from concourse.bass_utils import run_bass_kernel_spmd

F32 = mybir.dt.float32
BF16 = mybir.dt.bfloat16
I32 = mybir.dt.int32
AF = mybir.ActivationFunctionType
ALU = mybir.AluOpType
AX = mybir.AxisListType

C = 128
NPRE = 16
NOWN = 16
D = 1024
NS = 64
NE = 8
DFF = 4096
TT_TOK = 512
NTT = 4
INVF = 1.0 / 256.0
TWO_PI = float(2.0 * np.pi)
ARENA = 54272
COALESCE_OFF = True


class Ctr:
    def __init__(self, h, name):
        self.h = h
        self.name = name
        self.v = 0


class Builder:
    ENGS = ("sync", "gpsimd", "tensor", "vector", "scalar")

    def __init__(self, nc):
        self.nc = nc
        self.plan = {e: [] for e in self.ENGS}
        self.hw = {}
        self.waited = set()

    def ctr(self, ctx, name):
        return Ctr(ctx.enter_context(self.nc.semaphore(name)), name)

    def wait(self, eng, cv):
        if cv is None:
            return
        c, v = cv
        if v <= 0:
            return
        if self.hw.get((eng, c.name), 0) >= v:
            return
        self.plan[eng].append(("wait", c.h, v))
        self.hw[(eng, c.name)] = v
        self.waited.add((c.name, v))

    force_ss = False

    def op(self, eng, fn, inc=None, k=1, waits=(), ss=False):
        for w in waits:
            self.wait(eng, w)
        incs = []
        r = None
        if inc is not None:
            inc.v += k
            incs = [(inc, k, inc.v)]
            r = (inc, inc.v)
        self.plan[eng].append(("op", fn, incs))
        if (ss or self.force_ss) and r is not None:
            self.wait(eng, r)
        return r

    def emit(self):
        plan = self.plan
        waited = self.waited
        with self.nc.Block() as block:
            def run(eng_obj, key):
                pend = {}
                last_inst = None
                for item in plan[key]:
                    if item[0] == "wait":
                        eng_obj.wait_ge(item[1], item[2])
                    else:
                        inst = item[1](eng_obj)
                        last_inst = inst
                        for (c, k, vafter) in item[2]:
                            if k == 16:
                                inst.then_inc(c.h, k)
                            else:
                                pend[c.name] = (c, pend.get(c.name, (c, 0))[1] + k)
                                if COALESCE_OFF or (c.name, vafter) in waited:
                                    cc, p = pend.pop(c.name)
                                    inst.then_inc(cc.h, p)
                for name, (c, p) in pend.items():
                    if last_inst is not None:
                        last_inst.then_inc(c.h, p)

            @block.sync
            def _(e):
                run(e, "sync")

            @block.gpsimd
            def _(e):
                run(e, "gpsimd")

            @block.tensor
            def _(e):
                run(e, "tensor")

            @block.vector
            def _(e):
                run(e, "vector")

            @block.scalar
            def _(e):
                run(e, "scalar")


def bcast_ap(dram_ap, n):
    return bass.AP(dram_ap.tensor, 0, [[0, 128], [1, n]])


def build_program(run_attn=True, n_experts=NE, run_final=True, n_own=NOWN, n_pre=NPRE, chunk_ops=6, dbg=False):
    nc = bass.Bass("TRN2", target_bir_lowering=False)

    xfull = nc.dram_tensor("xfull", [4096, D], F32, kind="ExternalInput").ap()
    w_in = nc.dram_tensor("w_in", [D, D], F32, kind="ExternalInput").ap()
    b_in = nc.dram_tensor("b_in", [D], F32, kind="ExternalInput").ap()
    gamma = nc.dram_tensor("gamma", [D], F32, kind="ExternalInput").ap()
    beta = nc.dram_tensor("beta", [D], F32, kind="ExternalInput").ap()
    a_re = nc.dram_tensor("a_re", [NS], F32, kind="ExternalInput").ap()
    a_im = nc.dram_tensor("a_im", [NS], F32, kind="ExternalInput").ap()
    b_re = nc.dram_tensor("b_re", [NS], F32, kind="ExternalInput").ap()
    b_im = nc.dram_tensor("b_im", [NS], F32, kind="ExternalInput").ap()
    c_re = nc.dram_tensor("c_re", [D, NS], F32, kind="ExternalInput").ap()
    c_im = nc.dram_tensor("c_im", [D, NS], F32, kind="ExternalInput").ap()
    d_re = nc.dram_tensor("d_re", [D], F32, kind="ExternalInput").ap()
    r_w = nc.dram_tensor("r_w", [D, NE], F32, kind="ExternalInput").ap()
    r_b = nc.dram_tensor("r_b", [NE], F32, kind="ExternalInput").ap()
    w1 = nc.dram_tensor("w1", [NE, D, DFF], F32, kind="ExternalInput").ap()
    b1 = nc.dram_tensor("b1", [NE, DFF], F32, kind="ExternalInput").ap()
    w2 = nc.dram_tensor("w2", [NE, DFF, D], F32, kind="ExternalInput").ap()
    b2 = nc.dram_tensor("b2", [NE, D], F32, kind="ExternalInput").ap()
    sel = nc.dram_tensor("sel", [128, 1], F32, kind="ExternalInput").ap()
    out = nc.dram_tensor("out", [2048, D], F32, kind="ExternalOutput").ap()
    kind_scr = "ExternalOutput" if dbg else "Internal"
    x1scr = nc.dram_tensor("x1scr", [2048, D], F32, kind=kind_scr).ap()
    accd = nc.dram_tensor("accd", [2048, D], F32, kind=kind_scr).ap()
    dbg1 = nc.dram_tensor("dbg1", [128, 8192], F32, kind="ExternalOutput").ap() if dbg else None

    with ExitStack() as ctx:
        e = ctx.enter_context
        B = Builder(nc)

        AR = e(nc.sbuf_tensor("AR", [128, ARENA], BF16))
        off = [0]

        def carve(n):
            o = off[0]
            off[0] += n
            assert off[0] <= ARENA, f"bf16 arena overflow {off[0]}"
            return AR[:, o:o + n]

        winT = [carve(D) for _ in range(8)]
        ZG = [carve(D) for _ in range(NOWN)]
        khR = carve(D)
        khI = carve(D)
        fcos = carve(128)
        fsinn = carve(128)
        icosw = carve(128)
        isinnw = carve(128)
        kr0 = carve(D)
        lhsTD = [carve(128) for _ in range(16)]
        lhsTz = carve(128)
        megaR = carve(2048)
        lhsTmega = [megaR[:, c * 128:(c + 1) * 128] for c in range(NPRE)]
        sg_home = megaR[:, 0:1024]
        tri = carve(128)
        identb = carve(128)
        u_hiB = [carve(D) for _ in range(2)]
        u_loB = [carve(D) for _ in range(2)]
        x0lo = carve(D)
        sxb = [carve(D) for _ in range(2)]
        xfm = carve(D)
        ufR = carve(D)
        ufI = carve(D)
        yR = carve(D)
        yI = carve(D)
        rowpA = carve(2304)
        rowpB = carve(4096)
        gstk = carve(D)
        gswp = carve(D)
        stgb = carve(128)
        attn_end = off[0]
        assert attn_end <= ARENA

        w1res = [AR[:, k * DFF:(k + 1) * DFF] for k in range(8)]
        hbuf = AR[:, 32768:32768 + 32 * TT_TOK]
        w2buf = [AR[:, 49152 + i * D:49152 + (i + 1) * D] for i in range(4)]

        x1fm = [e(nc.sbuf_tensor(f"x1fm{k}", [128, 2048], BF16)) for k in range(8)]
        wrbf = e(nc.sbuf_tensor("wrbf", [128, NE * 8], BF16))
        b2f = e(nc.sbuf_tensor("b2f", [8, D], F32))
        b2hi_t = e(nc.sbuf_tensor("b2hi_t", [8, D], BF16))
        b2lo_t = e(nc.sbuf_tensor("b2lo_t", [8, D], BF16))
        geT = e(nc.sbuf_tensor("geT", [8, 128], BF16))
        onescol = e(nc.sbuf_tensor("onescol", [128, 1], BF16))
        onesall = e(nc.sbuf_tensor("onesall", [128, 128], BF16))
        altcol = e(nc.sbuf_tensor("altcol", [128, 1], BF16))

        AR2 = e(nc.sbuf_tensor("AR2", [128, 7680], F32))
        uf32 = AR2[:, 0:1024]
        x1pre = AR2[:, 1024:2048]
        x1f = AR2[:, 2048:3072]
        grep = AR2[:, 3072:4096]
        brep = AR2[:, 4096:5120]
        drep = AR2[:, 5120:6144]
        scrq = AR2[:, 6144:6656]
        ftmp = AR2[:, 6656:7680]
        csrow = AR2[0:1, 6656:7680]
        x1g = [AR2[:, 0:1024], AR2[:, 1024:2048]]
        accg = [AR2[:, 2048:3072], AR2[:, 3072:4096]]
        resb = AR2[:, 4096:5120]
        outb = [AR2[:, 5120:6144], AR2[:, 6144:7168]]
        tmpg = [AR2[:, i * 1024:(i + 1) * 1024] for i in range(4)]

        sx = [e(nc.sbuf_tensor(f"sx{i}", [128, D], F32)) for i in range(2)]
        rp1 = e(nc.sbuf_tensor("rp1", [128, D], F32))
        identf = e(nc.sbuf_tensor("identf", [128, 128], F32))
        b1c = e(nc.sbuf_tensor("b1c", [128, NE * 32], F32))
        geAll = e(nc.sbuf_tensor("geAll", [128, NOWN * 8], F32))
        selc = e(nc.sbuf_tensor("selc", [128, 1], F32))
        Tr = e(nc.sbuf_tensor("Tr", [64, 128], F32))
        Ti = e(nc.sbuf_tensor("Ti", [64, 128], F32))
        Pfr = e(nc.sbuf_tensor("Pfr", [64, 128], F32))
        Pfi = e(nc.sbuf_tensor("Pfi", [64, 128], F32))
        Pvr = e(nc.sbuf_tensor("Pvr", [64, 128], F32))
        Pvi = e(nc.sbuf_tensor("Pvi", [64, 128], F32))
        At1 = e(nc.sbuf_tensor("At1", [64, 128], F32))
        At2 = e(nc.sbuf_tensor("At2", [64, 128], F32))
        Acol = e(nc.sbuf_tensor("Acol", [64, 8], F32))
        decr = e(nc.sbuf_tensor("decr", [64, 16], F32))
        deci = e(nc.sbuf_tensor("deci", [64, 16], F32))
        CrT = AR2[0:64, 0:1024]
        CiT = AR2[0:64, 1024:2048]
        stg32 = e(nc.sbuf_tensor("stg32", [128, 128], F32))
        jrow = e(nc.sbuf_tensor("jrow", [1, 128], F32))
        irow = e(nc.sbuf_tensor("irow", [1, 128], I32))
        itmp = e(nc.sbuf_tensor("itmp", [128, 128], I32))
        icol = e(nc.sbuf_tensor("icol", [128, 1], I32))
        fcol = e(nc.sbuf_tensor("fcol", [128, 1], F32))
        frows = e(nc.sbuf_tensor("frows", [128, D], F32))
        crow = e(nc.sbuf_tensor("crow", [1, D], F32))
        carry = frows[0:1, :]
        rbrow = frows[0:1, 0:NE]
        stat = e(nc.sbuf_tensor("stat", [128, 48], F32))
        l2t = e(nc.sbuf_tensor("l2t", [128, NE], F32))
        exp8 = e(nc.sbuf_tensor("exp8", [128, NE], F32))
        msk8 = e(nc.sbuf_tensor("msk8", [128, NE], F32))
        cs8 = e(nc.sbuf_tensor("cs8", [128, NE], F32))
        ones8 = e(nc.sbuf_tensor("ones8", [128, NE], F32))

        dbgS = e(nc.sbuf_tensor("dbgS", [128, 512], F32)) if dbg else None
        pA = e(nc.psum_tensor("pA", [128, 1024], F32))
        pB = e(nc.psum_tensor("pB", [128, 1024], F32))
        p0 = e(nc.psum_tensor("p0", [128, 512], F32))
        p1 = e(nc.psum_tensor("p1", [128, 512], F32))
        p2 = e(nc.psum_tensor("p2", [128, 512], F32))
        p3 = e(nc.psum_tensor("p3", [128, 512], F32))

        cLD = B.ctr(ctx, "cLD")
        cGL = B.ctr(ctx, "cGL")
        cGP = B.ctr(ctx, "cGP")
        cPE = B.ctr(ctx, "cPE")
        cDV = B.ctr(ctx, "cDV")
        cAC = B.ctr(ctx, "cAC")
        cST = B.ctr(ctx, "cST")

        OP = B.op

        def dma(eng, dst, src, inc, waits=()):
            return OP(eng, lambda en, d=dst, s=src: en.dma_start(out=d, in_=s), inc=inc, k=16, waits=waits)

        def vop(fn, waits=(), ss=False):
            return OP("vector", fn, cDV, waits=waits, ss=ss)

        def aop(fn, waits=(), ss=False):
            return OP("scalar", fn, cAC, waits=waits, ss=ss)

        def pop(fn, waits=()):
            return OP("tensor", fn, cPE, waits=waits)

        def ln_stats(src0, src1, from_psum, waits=(), uf_dst=None, sb=32):
            c = lambda i: stat[:, sb + i:sb + i + 1]
            aop(lambda en: en.activation(out=(uf_dst[:, 0:512] if from_psum else scrq[:, :]),
                                         in_=src0, func=AF.Copy, accum_out=c(0)), waits=waits)
            aop(lambda en: en.activation(out=(uf_dst[:, 512:1024] if from_psum else scrq[:, :]),
                                         in_=src1, func=AF.Copy, accum_out=c(1)))
            aop(lambda en: en.activation(out=scrq[:, :], in_=src0, func=AF.Square, accum_out=c(2)))
            aop(lambda en: en.activation(out=scrq[:, :], in_=src1, func=AF.Square, accum_out=c(3)),
                ss=True)
            aop(lambda en: en.add(out=c(4), in_=c(0), add=c(1)), ss=True)
            aop(lambda en: en.mul(out=c(4), in_=c(4), mul=1.0 / D), ss=True)
            aop(lambda en: en.add(out=c(5), in_=c(2), add=c(3)), ss=True)
            aop(lambda en: en.mul(out=c(5), in_=c(5), mul=1.0 / D), ss=True)
            aop(lambda en: en.activation(out=c(6), in_=c(4), func=AF.Square), ss=True)
            aop(lambda en: en.mul(out=c(6), in_=c(6), mul=-1.0), ss=True)
            aop(lambda en: en.add(out=c(5), in_=c(5), add=c(6)), ss=True)
            aop(lambda en: en.activation(out=c(6), in_=c(5), func=AF.Ln, bias=stat[:, 12:13]), ss=True)
            aop(lambda en: en.activation(out=c(7), in_=c(6), func=AF.Exp, scale=-0.5), ss=True)
            aop(lambda en: en.mul(out=c(8), in_=c(4), mul=-1.0), ss=True)
            return aop(lambda en: en.mul(out=c(8), in_=c(8), mul=c(7)), ss=True)

        B.force_ss = True
        ld0 = dma("sync", rp1[0:1, :], b_in[None, :], cLD)
        dma("sync", rp1[32:33, :], gamma[None, :], cLD)
        dma("sync", rp1[64:65, :], beta[None, :], cLD)
        dma("sync", rp1[96:97, :], d_re[None, :], cLD)
        dma("sync", grep, bcast_ap(gamma, D), cLD)
        dma("sync", brep, bcast_ap(beta, D), cLD)
        dma("sync", drep, bcast_ap(d_re, D), cLD)
        dma("sync", Acol[:, 0:1], a_re[:, None], cLD)
        dma("sync", Acol[:, 1:2], a_im[:, None], cLD)
        dma("sync", Acol[:, 2:3], b_re[:, None], cLD)
        dma("sync", Acol[:, 3:4], b_im[:, None], cLD)
        dma("sync", rbrow[:, :], r_b[None, :], cLD)
        dma("sync", selc[:, :], sel, cLD)
        ld_small = (cLD, cLD.v)

        for k in range(8):
            dma("gpsimd", wrbf[:, k * 8:(k + 1) * 8], r_w[k * 128:(k + 1) * 128, :], cGL)
        dma("gpsimd", b2hi_t[:], b2[:, :], cGL)
        gl_b2 = (cGL, cGL.v)

        OP("gpsimd", lambda en: en.memset(onescol[:], 1.0), cGP)
        OP("gpsimd", lambda en: en.memset(onesall[:], 1.0), cGP)
        OP("gpsimd", lambda en: en.memset(ones8[:], 1.0), cGP)
        OP("gpsimd", lambda en: en.memset(stat[:, 12:13], 1e-5), cGP)
        OP("gpsimd", lambda en: en.memset(tri, 1.0), cGP)
        OP("gpsimd", lambda en: en.affine_select(out=tri, in_=tri, pattern=[[1, 128]],
                                                 compare_op=ALU.is_ge, fill=0.0, base=0,
                                                 channel_multiplier=-1), cGP)
        OP("gpsimd", lambda en: en.memset(identb, 1.0), cGP)
        OP("gpsimd", lambda en: en.affine_select(out=identb, in_=identb, pattern=[[1, 128]],
                                                 compare_op=ALU.is_equal, fill=0.0, base=0,
                                                 channel_multiplier=-1), cGP)
        OP("gpsimd", lambda en: en.memset(identf[:], 1.0), cGP)
        OP("gpsimd", lambda en: en.affine_select(out=identf[:], in_=identf[:], pattern=[[1, 128]],
                                                 compare_op=ALU.is_equal, fill=0.0, base=0,
                                                 channel_multiplier=-1), cGP)
        OP("gpsimd", lambda en: en.iota(out=irow[:], pattern=[[1, 128]], base=0, channel_multiplier=0), cGP)
        gp_const = OP("gpsimd", lambda en: en.iota(out=icol[:], pattern=[[0, 1]], base=0,
                                                   channel_multiplier=1), cGP)

        vop(lambda en: en.tensor_copy(out=rowpA[0:1, 0:1024], in_=rp1[0:1, :]), waits=[ld0, ld_small])
        vop(lambda en: en.tensor_sub(out=rowpA[0:1, 1024:2048], in0=rp1[0:1, :],
                                     in1=rowpA[0:1, 0:1024]))
        vop(lambda en: en.tensor_copy(out=rowpA[0:1, 2048:2056], in_=rbrow[:, :]))
        vop(lambda en: en.tensor_sub(out=rowpA[0:1, 2056:2064], in0=rbrow[:, :],
                                     in1=rowpA[0:1, 2048:2056]))
        m = dma("sync", b2f[:], b2[:, :], cLD)
        vop(lambda en: en.tensor_sub(out=b2lo_t[:], in0=b2f[:], in1=b2hi_t[:]), waits=[m, gl_b2])

        vop(lambda en: en.tensor_copy(out=fcol[:], in_=icol[:]), waits=[gp_const])
        vop(lambda en: en.tensor_scalar(out=stat[:, 0:1], in0=fcol[:], scalar1=0.5, scalar2=None,
                                        op0=ALU.mult))
        vop(lambda en: en.tensor_copy(out=icol[:], in_=stat[:, 0:1]))
        vop(lambda en: en.tensor_copy(out=stat[:, 1:2], in_=icol[:]))
        dv_par = vop(lambda en: en.tensor_sub(out=stat[:, 2:3], in0=stat[:, 0:1], in1=stat[:, 1:2]))
        ac_abs = aop(lambda en: en.activation(out=stat[:, 4:5], in_=stat[:, 2:3], func=AF.Abs),
                     waits=[dv_par])
        dv_alt = vop(lambda en: en.tensor_scalar(out=altcol[:], in0=stat[:, 4:5], scalar1=-4.0,
                                                 scalar2=1.0, op0=ALU.mult, op1=ALU.add), waits=[ac_abs])
        vop(lambda en: en.tensor_copy(out=jrow[:], in_=irow[:]))

        vop(lambda en: en.memset(Tr[:, 0:1], 1.0), waits=[ld_small])
        vop(lambda en: en.memset(Ti[:, 0:1], 0.0))
        vop(lambda en: en.tensor_copy(out=Acol[:, 4:5], in_=Acol[:, 0:1]))
        vop(lambda en: en.tensor_copy(out=Acol[:, 5:6], in_=Acol[:, 1:2]))
        for lvl in range(7):
            n = 1 << lvl
            vop(lambda en, n=n: en.tensor_scalar(out=At1[:, 0:n], in0=Tr[:, 0:n], scalar1=Acol[:, 4:5],
                                                 scalar2=None, op0=ALU.mult))
            vop(lambda en, n=n: en.tensor_scalar(out=At2[:, 0:n], in0=Ti[:, 0:n], scalar1=Acol[:, 5:6],
                                                 scalar2=None, op0=ALU.mult))
            vop(lambda en, n=n: en.tensor_sub(out=Tr[:, n:2 * n], in0=At1[:, 0:n], in1=At2[:, 0:n]))
            vop(lambda en, n=n: en.tensor_scalar(out=At1[:, 0:n], in0=Tr[:, 0:n], scalar1=Acol[:, 5:6],
                                                 scalar2=None, op0=ALU.mult))
            vop(lambda en, n=n: en.tensor_scalar(out=At2[:, 0:n], in0=Ti[:, 0:n], scalar1=Acol[:, 4:5],
                                                 scalar2=None, op0=ALU.mult))
            vop(lambda en, n=n: en.tensor_add(out=Ti[:, n:2 * n], in0=At1[:, 0:n], in1=At2[:, 0:n]))
            vop(lambda en: en.tensor_mul(out=Acol[:, 6:7], in0=Acol[:, 4:5], in1=Acol[:, 4:5]))
            vop(lambda en: en.tensor_mul(out=Acol[:, 7:8], in0=Acol[:, 5:6], in1=Acol[:, 5:6]))
            vop(lambda en: en.tensor_mul(out=Acol[:, 5:6], in0=Acol[:, 4:5], in1=Acol[:, 5:6]))
            vop(lambda en: en.tensor_scalar(out=Acol[:, 5:6], in0=Acol[:, 5:6], scalar1=2.0,
                                            scalar2=None, op0=ALU.mult))
            vop(lambda en: en.tensor_sub(out=Acol[:, 4:5], in0=Acol[:, 6:7], in1=Acol[:, 7:8]))
        vop(lambda en: en.tensor_scalar(out=At1[:, :], in0=Tr[:, :], scalar1=Acol[:, 0:1],
                                        scalar2=None, op0=ALU.mult))
        vop(lambda en: en.tensor_scalar(out=At2[:, :], in0=Ti[:, :], scalar1=Acol[:, 1:2],
                                        scalar2=None, op0=ALU.mult))
        vop(lambda en: en.tensor_sub(out=Pfr[:, :], in0=At1[:, :], in1=At2[:, :]))
        vop(lambda en: en.tensor_scalar(out=At1[:, :], in0=Tr[:, :], scalar1=Acol[:, 1:2],
                                        scalar2=None, op0=ALU.mult))
        vop(lambda en: en.tensor_scalar(out=At2[:, :], in0=Ti[:, :], scalar1=Acol[:, 0:1],
                                        scalar2=None, op0=ALU.mult))
        vop(lambda en: en.tensor_add(out=Pfi[:, :], in0=At1[:, :], in1=At2[:, :]))
        vop(lambda en: en.tensor_copy(out=Pvr[:, :], in_=Tr[:, ::-1]))
        vop(lambda en: en.tensor_copy(out=Pvi[:, :], in_=Ti[:, ::-1]))
        vop(lambda en: en.memset(decr[:, 0:1], 1.0))
        vop(lambda en: en.memset(deci[:, 0:1], 0.0))
        for dd in range(1, 16):
            vop(lambda en, dd=dd: en.tensor_scalar(out=Acol[:, 6:7], in0=decr[:, dd - 1:dd],
                                                   scalar1=Acol[:, 4:5], scalar2=None, op0=ALU.mult))
            vop(lambda en, dd=dd: en.tensor_scalar(out=Acol[:, 7:8], in0=deci[:, dd - 1:dd],
                                                   scalar1=Acol[:, 5:6], scalar2=None, op0=ALU.mult))
            vop(lambda en, dd=dd: en.tensor_sub(out=decr[:, dd:dd + 1], in0=Acol[:, 6:7], in1=Acol[:, 7:8]))
            vop(lambda en, dd=dd: en.tensor_scalar(out=Acol[:, 6:7], in0=decr[:, dd - 1:dd],
                                                   scalar1=Acol[:, 5:6], scalar2=None, op0=ALU.mult))
            vop(lambda en, dd=dd: en.tensor_scalar(out=Acol[:, 7:8], in0=deci[:, dd - 1:dd],
                                                   scalar1=Acol[:, 4:5], scalar2=None, op0=ALU.mult))
            vop(lambda en, dd=dd: en.tensor_add(out=deci[:, dd:dd + 1], in0=Acol[:, 6:7], in1=Acol[:, 7:8]))
        for dd in range(16):
            vop(lambda en, dd=dd: en.tensor_scalar(out=At1[:, :], in0=Pfr[:, :], scalar1=decr[:, dd:dd + 1],
                                                   scalar2=None, op0=ALU.mult))
            vop(lambda en, dd=dd: en.tensor_scalar(out=At2[:, :], in0=Pfi[:, :], scalar1=deci[:, dd:dd + 1],
                                                   scalar2=None, op0=ALU.mult))
            vop(lambda en, dd=dd: en.tensor_sub(out=lhsTD[dd][0:64, :], in0=At1[:, :], in1=At2[:, :]))
            vop(lambda en, dd=dd: en.tensor_scalar(out=At1[:, :], in0=Pfr[:, :], scalar1=deci[:, dd:dd + 1],
                                                   scalar2=None, op0=ALU.mult))
            vop(lambda en, dd=dd: en.tensor_scalar(out=At2[:, :], in0=Pfi[:, :], scalar1=decr[:, dd:dd + 1],
                                                   scalar2=None, op0=ALU.mult))
            vop(lambda en, dd=dd: en.scalar_tensor_tensor(out=lhsTD[dd][64:128, :], in0=At1[:, :],
                                                          scalar=-1.0, in1=At2[:, :],
                                                          op0=ALU.mult, op1=ALU.subtract))
        if dbg:
            vop(lambda en: en.tensor_copy(out=dbgS[0:64, 0:8], in_=Acol[:, :]))
            vop(lambda en: en.tensor_copy(out=dbgS[0:64, 8:136], in_=Tr[:, :]))
            vop(lambda en: en.tensor_copy(out=dbgS[0:64, 136:264], in_=Pvr[:, :]))
            vop(lambda en: en.tensor_copy(out=dbgS[0:64, 264:280], in_=decr[:, :]))
            vop(lambda en: en.tensor_copy(out=dbgS[0:64, 280:408], in_=Ti[:, :]))
        dv_tbl = (cDV, cDV.v)

        pe_jk = pop(lambda en: en.matmul(out=p0[:, 0:128], lhsT=jrow[:], rhs=jrow[:], start=True,
                                         stop=True), waits=[dv_tbl])

        def angles(add64):
            addc = 64.0 if add64 else 0.0
            vop(lambda en, a=addc: en.tensor_scalar(out=uf32[:, 0:128], in0=p0[:, 0:128], scalar1=a,
                                                    scalar2=INVF, op0=ALU.add, op1=ALU.mult),
                waits=[pe_jk, (cAC, cAC.v)])
            vop(lambda en: en.tensor_copy(out=itmp[:], in_=uf32[:, 0:128]))
            vop(lambda en: en.tensor_copy(out=uf32[:, 128:256], in_=itmp[:]))
            vop(lambda en: en.tensor_sub(out=uf32[:, 0:128], in0=uf32[:, 0:128], in1=uf32[:, 128:256]))
            vop(lambda en: en.tensor_single_scalar(out=uf32[:, 128:256], in_=uf32[:, 0:128],
                                                   scalar=0.5, op=ALU.is_gt))
            vop(lambda en: en.tensor_sub(out=uf32[:, 0:128], in0=uf32[:, 0:128], in1=uf32[:, 128:256]))
            m2 = vop(lambda en: en.tensor_scalar(out=uf32[:, 0:128], in0=uf32[:, 0:128], scalar1=TWO_PI,
                                                 scalar2=None, op0=ALU.mult))
            return aop(lambda en: en.activation(out=stg32[:], in_=uf32[:, 0:128], func=AF.Sin),
                       waits=[m2, (cDV, cDV.v)])

        m_sin = angles(False)
        dv_fs = vop(lambda en: en.tensor_scalar(out=fsinn, in0=stg32[:], scalar1=-1.0, scalar2=None,
                                                op0=ALU.mult), waits=[m_sin])
        vop(lambda en: en.tensor_scalar(out=isinnw, in0=stg32[:], scalar1=-2.0 * INVF, scalar2=None,
                                        op0=ALU.mult))
        vop(lambda en: en.tensor_scalar(out=isinnw[:, 0:1] if False else AR[:, 0:0], in0=stg32[:, 0:1],
                                        scalar1=0.0, scalar2=None, op0=ALU.mult)) if False else None
        vop(lambda en: en.tensor_scalar(out=isinnw[:, 0:1], in0=stg32[:, 0:1], scalar1=-INVF,
                                        scalar2=None, op0=ALU.mult))
        m_cos = angles(True)
        dv_fc = vop(lambda en: en.tensor_copy(out=fcos, in_=stg32[:]), waits=[m_cos])
        if dbg:
            vop(lambda en: en.tensor_copy(out=dbgS[:, 408:472], in_=stg32[:, 0:64]))
        vop(lambda en: en.tensor_scalar(out=icosw, in0=stg32[:], scalar1=2.0 * INVF, scalar2=None,
                                        op0=ALU.mult))
        dv_ic = vop(lambda en: en.tensor_scalar(out=icosw[:, 0:1], in0=stg32[:, 0:1], scalar1=INVF,
                                                scalar2=None, op0=ALU.mult))
        m = vop(lambda en: en.tensor_copy(out=fcol[:], in_=altcol[:]), waits=[dv_alt])
        pe_at = pop(lambda en: en.transpose(out=p1[0:1, 0:128], in_=fcol[:], identity=identf[:]),
                    waits=[m, (cGP, gp_const[1])])
        dv_ialt = vop(lambda en: en.tensor_scalar(out=rowpA[0:1, 2064:2192], in0=p1[0:1, 0:128],
                                                  scalar1=INVF, scalar2=None, op0=ALU.mult), waits=[pe_at])

        for i in range(8):
            m = dma("sync", ftmp, w_in[i * 128:(i + 1) * 128, :], cLD, waits=[(cDV, cDV.v)])
            pe_m = None
            for k in range(8):
                pd = (p2 if k < 4 else p3)[:, (k % 4) * 128:(k % 4) * 128 + 128]
                pe_m = pop(lambda en, k=k, pd=pd: en.transpose(out=pd, in_=ftmp[:, k * 128:(k + 1) * 128],
                                                               identity=identf[:]),
                           waits=[m, (cDV, cDV.v)] if k == 0 else ())
            for k in range(8):
                pd = (p2 if k < 4 else p3)[:, (k % 4) * 128:(k % 4) * 128 + 128]
                vop(lambda en, k=k, i=i, pd=pd: en.tensor_copy(out=winT[k][:, i * 128:(i + 1) * 128], in_=pd),
                    waits=[pe_m] if k == 0 else ())
        for i in range(8):
            for part, dst in ((c_re, CrT), (c_im, CiT)):
                m = dma("sync", ftmp[:, 0:64], part[i * 128:(i + 1) * 128, :], cLD, waits=[(cDV, cDV.v)])
                pe_m = pop(lambda en: en.transpose(out=p2[0:64, 0:128], in_=ftmp[:, 0:64],
                                                   identity=identf[:]), waits=[m, (cDV, cDV.v)])
                vop(lambda en, i=i, dst=dst: en.tensor_copy(out=dst[:, i * 128:(i + 1) * 128],
                                                            in_=p2[0:64, 0:128]), waits=[pe_m])
        for seg in range(8):
            s0, s1 = seg * 128, (seg + 1) * 128
            vop(lambda en, s0=s0, s1=s1: en.tensor_scalar(out=At1[:, :], in0=CrT[:, s0:s1],
                                                          scalar1=Acol[:, 2:3], scalar2=None, op0=ALU.mult))
            vop(lambda en, s0=s0, s1=s1: en.tensor_scalar(out=At2[:, :], in0=CiT[:, s0:s1],
                                                          scalar1=Acol[:, 3:4], scalar2=None, op0=ALU.mult))
            vop(lambda en, s0=s0, s1=s1: en.tensor_sub(out=gstk[0:64, s0:s1], in0=At1[:, :], in1=At2[:, :]))
            vop(lambda en, s0=s0, s1=s1: en.tensor_scalar(out=At1[:, :], in0=CiT[:, s0:s1],
                                                          scalar1=Acol[:, 2:3], scalar2=None, op0=ALU.mult))
            vop(lambda en, s0=s0, s1=s1: en.tensor_scalar(out=At2[:, :], in0=CrT[:, s0:s1],
                                                          scalar1=Acol[:, 3:4], scalar2=None, op0=ALU.mult))
            vop(lambda en, s0=s0, s1=s1: en.tensor_add(out=gstk[64:128, s0:s1], in0=At1[:, :], in1=At2[:, :]))
            vop(lambda en, s0=s0, s1=s1: en.tensor_copy(out=gswp[0:64, s0:s1], in_=gstk[64:128, s0:s1]))
            vop(lambda en, s0=s0, s1=s1: en.tensor_copy(out=gswp[64:128, s0:s1], in_=gstk[0:64, s0:s1]))
        dv_g = (cDV, cDV.v)

        vop(lambda en: en.tensor_copy(out=stg32[0:64, :], in_=Pvr[:, :]))
        dv_st = vop(lambda en: en.tensor_copy(out=stg32[64:128, :], in_=Pvi[:, :]))
        pe_m = pop(lambda en: en.transpose(out=p2[:, 0:128], in_=stg32[:], identity=identf[:]),
                   waits=[dv_st])
        dv_done = vop(lambda en: en.tensor_copy(out=lhsTz, in_=p2[:, 0:128]), waits=[pe_m])
        for cc in range(NPRE):
            dd = NPRE - 1 - cc
            vop(lambda en, dd=dd: en.tensor_scalar(out=At1[:, :], in0=Pvr[:, :], scalar1=decr[:, dd:dd + 1],
                                                   scalar2=None, op0=ALU.mult), waits=[dv_done])
            vop(lambda en, dd=dd: en.tensor_scalar(out=At2[:, :], in0=Pvi[:, :], scalar1=deci[:, dd:dd + 1],
                                                   scalar2=None, op0=ALU.mult))
            vop(lambda en: en.tensor_sub(out=stg32[0:64, :], in0=At1[:, :], in1=At2[:, :]))
            vop(lambda en, dd=dd: en.tensor_scalar(out=At1[:, :], in0=Pvr[:, :], scalar1=deci[:, dd:dd + 1],
                                                   scalar2=None, op0=ALU.mult))
            vop(lambda en, dd=dd: en.tensor_scalar(out=At2[:, :], in0=Pvi[:, :], scalar1=decr[:, dd:dd + 1],
                                                   scalar2=None, op0=ALU.mult))
            dv_st = vop(lambda en: en.tensor_add(out=stg32[64:128, :], in0=At1[:, :], in1=At2[:, :]))
            pe_m = pop(lambda en: en.transpose(out=p2[:, 0:128], in_=stg32[:], identity=identf[:]),
                       waits=[dv_st])
            dv_done = vop(lambda en, cc=cc: en.tensor_copy(out=lhsTmega[cc], in_=p2[:, 0:128]),
                          waits=[pe_m])

        vop(lambda en: en.tensor_copy(out=stgb[0:64, :], in_=Tr[:, :]))
        dv_st2 = vop(lambda en: en.tensor_scalar(out=stgb[64:128, :], in0=Ti[:, :], scalar1=-1.0,
                                                 scalar2=None, op0=ALU.mult))
        k1 = pop(lambda en: en.matmul(out=pA[:, 0:512], lhsT=stgb, rhs=gstk[:, 0:512], start=True,
                                      stop=True), waits=[dv_st2, dv_g])
        k2 = pop(lambda en: en.matmul(out=pA[:, 512:1024], lhsT=stgb, rhs=gstk[:, 512:1024], start=True,
                                      stop=True))
        vop(lambda en: en.tensor_copy(out=kr0[:, 0:512], in_=pA[:, 0:512]), waits=[k1])
        dv_kr = vop(lambda en: en.tensor_copy(out=kr0[:, 512:1024], in_=pA[:, 512:1024]), waits=[k2])
        for (tblL, tblO) in ((fcos, khR), (fsinn, khI)):
            k1 = pop(lambda en, l=tblL: en.matmul(out=pA[:, 0:512], lhsT=l, rhs=kr0[:, 0:512],
                                                  start=True, stop=True), waits=[dv_kr, dv_fc, dv_fs, dv_ic])
            k2 = pop(lambda en, l=tblL: en.matmul(out=pA[:, 512:1024], lhsT=l, rhs=kr0[:, 512:1024],
                                                  start=True, stop=True))
            vop(lambda en, o=tblO: en.tensor_copy(out=o[:, 0:512], in_=pA[:, 0:512]), waits=[k1])
            dv_kr = vop(lambda en, o=tblO: en.tensor_copy(out=o[:, 512:1024], in_=pA[:, 512:1024]),
                        waits=[k2])
        k1 = pop(lambda en: en.matmul(out=pA[0:1, 0:512], lhsT=altcol[:], rhs=kr0[:, 0:512], start=True,
                                      stop=True), waits=[dv_kr, dv_alt])
        k2 = pop(lambda en: en.matmul(out=pA[0:1, 512:1024], lhsT=altcol[:], rhs=kr0[:, 512:1024],
                                      start=True, stop=True))
        vop(lambda en: en.tensor_copy(out=rowpB[0:1, 3072:3584], in_=pA[0:1, 0:512]), waits=[k1])
        dv_kh8 = vop(lambda en: en.tensor_copy(out=rowpB[0:1, 3584:4096], in_=pA[0:1, 512:1024]),
                     waits=[k2])

        for ei in range(8):
            m = dma("sync", stg32[0:32, :], b1[ei, :].rearrange("(m p) -> m p", p=128), cLD,
                    waits=[(cDV, cDV.v)])
            pe_m = pop(lambda en: en.transpose(out=p2[:, 0:32], in_=stg32[0:32, :],
                                               identity=identf[0:32, 0:32]),
                       waits=[m, (cDV, cDV.v)])
            vop(lambda en, ei=ei: en.tensor_copy(out=b1c[:, ei * 32:(ei + 1) * 32], in_=p2[:, 0:32]),
                waits=[pe_m])

        vop(lambda en: en.memset(carry[:, :], 0.0))
        B.force_ss = False
        dv_setup = (cDV, cDV.v)
        gp_done = (cGP, cGP.v)
        pe_setup = (cPE, cPE.v)
        ac_setup = (cAC, cAC.v)

        last_pe_uuse = pe_setup
        pe_uuse_slot = [pe_setup, pe_setup]
        last_dv_xfm = dv_setup
        last_dv_x1pre = dv_setup
        last_dv_cum = dv_setup
        last_ac_st2 = ac_setup
        last_mscr = None
        last_dv_zg = dv_setup
        last_dv_sums = dv_setup
        last_ac_stats = ac_setup
        last_dv_soft = dv_setup
        last_p01_free = dv_setup
        last_dv_xfm1 = dv_setup
        free_x = [dv_setup, dv_setup]
        sg_ready = None
        zg_marks = [None] * NOWN
        x1f_written = []
        moe_acc_marks = {}

        uf32B = [uf32, uf32]
        last_dv_fft = dv_setup

        def head(t):
            nonlocal last_dv_xfm, last_p01_free, last_ac_stats, last_dv_x1pre
            own = t >= NPRE
            sl = t % 2
            rows = slice(t * 128, (t + 1) * 128)
            mx = dma("sync", sx[sl][:], xfull[rows, :], cLD, waits=[free_x[sl]])
            mxb = None
            if own:
                mxb = dma("gpsimd", sxb[sl], xfull[rows, :], cGL, waits=[free_x[sl]])
            tp_last = None
            for k in range(8):
                pd = (p0 if k < 4 else p1)[:, (k % 4) * 128:(k % 4) * 128 + 128]
                w = [mx]
                if k == 0:
                    w += [last_dv_xfm, last_p01_free, gp_done]
                tp_last = pop(lambda en, k=k, pd=pd, s=sl: en.transpose(
                    out=pd, in_=sx[s][:, k * 128:(k + 1) * 128], identity=identf[:]), waits=w)
            dvm = None
            for k in range(8):
                dvm = aop(lambda en, k=k: en.activation(
                    out=xfm[:, k * 128:(k + 1) * 128],
                    in_=(p0 if k < 4 else p1)[:, (k % 4) * 128:(k % 4) * 128 + 128],
                    func=AF.Copy),
                    waits=[tp_last] if k == 0 else [])
            last_dv_xfm = dvm
            if not own:
                free_x[sl] = dvm
            zm = None
            for half in range(2):
                pd = p0 if half == 0 else p1
                hs = slice(half * 512, (half + 1) * 512)
                for k in range(8):
                    pop(lambda en, k=k, pd=pd, hs=hs: en.matmul(
                        out=pd[:], lhsT=xfm[:, k * 128:(k + 1) * 128], rhs=winT[k][:, hs],
                        start=(k == 0), stop=False),
                        waits=[last_dv_xfm] if (k == 0 and half == 0) else ())
                pop(lambda en, pd=pd, hs=hs: en.matmul(out=pd[:], lhsT=onesall[0:1, :],
                                                       rhs=rowpA[0:1, hs], start=False, stop=False))
                zm = pop(lambda en, pd=pd, hs=hs: en.matmul(
                    out=pd[:], lhsT=onesall[0:1, :],
                    rhs=rowpA[0:1, 1024 + hs.start:1024 + hs.stop], start=False, stop=True))
            return dict(t=t, mx=mx, mxb=mxb, zm=zm, sl=sl, own=own)

        def stats_part(hh):
            nonlocal last_p01_free, last_ac_stats
            t = hh["t"]
            sl = t % 2
            uf = uf32B[sl]
            ac_st = ln_stats(p0[:], p1[:], True, waits=[hh["zm"], last_dv_x1pre, (cDV, cDV.v), gp_done],
                             uf_dst=uf, sb=sl * 16)
            last_ac_stats = ac_st
            last_p01_free = ac_st
            u_hi = u_hiB[sl]
            u_lo = u_loB[sl]
            vop(lambda en, uf=uf, sl=sl: en.tensor_scalar(
                out=uf, in0=uf, scalar1=stat[:, sl * 16 + 7:sl * 16 + 8],
                scalar2=stat[:, sl * 16 + 8:sl * 16 + 9], op0=ALU.mult, op1=ALU.add),
                waits=[ac_st, pe_uuse_slot[sl]])
            vop(lambda en, uf=uf: en.tensor_mul(out=uf, in0=uf, in1=grep))
            vop(lambda en, uf=uf: en.tensor_add(out=uf, in0=uf, in1=brep))
            vop(lambda en, uf=uf, uh=u_hi: en.tensor_copy(out=uh, in_=uf))
            hh["dv_u"] = vop(lambda en, uf=uf, uh=u_hi, ul=u_lo: en.tensor_sub(out=ul, in0=uf, in1=uh))

        def mm_part(hh):
            nonlocal last_pe_uuse, last_dv_xfm1, last_dv_x1pre, last_dv_cum, last_dv_zg
            nonlocal last_dv_sums, last_dv_soft, last_ac_st2, last_mscr, sg_ready, last_dv_fft
            t = hh["t"]
            own = t >= NPRE
            co = t - NPRE
            sl = t % 2
            u_hi = u_hiB[sl]
            u_lo = u_loB[sl]
            mxb = hh["mxb"]
            dv_u = hh["dv_u"]

            if not own:
                w0 = [dv_u] + ([last_dv_x1pre] if t == 0 else [])
                pop(lambda en, t=t, uh=u_hi: en.matmul(out=pA[:, 0:512], lhsT=lhsTmega[t],
                                                       rhs=uh[:, 0:512],
                                                       start=(t == 0), stop=(t == NPRE - 1)), waits=w0)
                pe_mega = pop(lambda en, t=t, uh=u_hi: en.matmul(out=pA[:, 512:1024], lhsT=lhsTmega[t],
                                                                 rhs=uh[:, 512:1024],
                                                                 start=(t == 0), stop=(t == NPRE - 1)))
                pop(lambda en, uh=u_hi: en.matmul(out=p2[0:1, :], lhsT=onescol[:], rhs=uh[:, 0:512],
                                                  start=True, stop=False),
                    waits=[last_dv_sums, (cDV, cDV.v)])
                pop(lambda en, ul=u_lo: en.matmul(out=p2[0:1, :], lhsT=onescol[:], rhs=ul[:, 0:512],
                                                  start=False, stop=True))
                pop(lambda en, uh=u_hi: en.matmul(out=p3[0:1, :], lhsT=onescol[:], rhs=uh[:, 512:1024],
                                                  start=True, stop=False))
                pe_sums = pop(lambda en, ul=u_lo: en.matmul(out=p3[0:1, :], lhsT=onescol[:],
                                                            rhs=ul[:, 512:1024], start=False, stop=True))
                last_pe_uuse = pe_sums
                pe_uuse_slot[sl] = pe_sums
                vop(lambda en: en.tensor_copy(out=crow[:, 0:512], in_=p2[0:1, :]),
                    waits=[pe_sums], ss=True)
                vop(lambda en: en.tensor_copy(out=crow[:, 512:1024], in_=p3[0:1, :]), ss=True)
                last_dv_sums = vop(lambda en: en.tensor_add(out=carry[:, :], in0=carry[:, :],
                                                            in1=crow[:, :]), ss=True)
                if t == NPRE - 1:
                    vop(lambda en: en.scalar_tensor_tensor(out=ufR[0:64, :], in0=pA[0:64, :],
                                                           scalar=selc[0:64, 0:1], in1=gstk[0:64, :],
                                                           op0=ALU.mult, op1=ALU.mult),
                        waits=[pe_mega, ld_small])
                    vop(lambda en: en.scalar_tensor_tensor(out=ufI[0:64, :], in0=pA[64:128, :],
                                                           scalar=selc[0:64, 0:1], in1=gswp[0:64, :],
                                                           op0=ALU.mult, op1=ALU.mult))
                    vop(lambda en: en.tensor_sub(out=yR[0:64, :], in0=ufR[0:64, :], in1=ufI[0:64, :]))
                    vop(lambda en: en.scalar_tensor_tensor(out=ufR[0:64, :], in0=pA[0:64, :],
                                                           scalar=selc[0:64, 0:1], in1=gswp[0:64, :],
                                                           op0=ALU.mult, op1=ALU.mult))
                    vop(lambda en: en.scalar_tensor_tensor(out=ufI[0:64, :], in0=pA[64:128, :],
                                                           scalar=selc[0:64, 0:1], in1=gstk[0:64, :],
                                                           op0=ALU.mult, op1=ALU.mult))
                    sg_ready = vop(lambda en: en.tensor_add(out=yR[64:128, :], in0=ufR[0:64, :],
                                                            in1=ufI[0:64, :]))
                    last_dv_x1pre = sg_ready
                    last_dv_sums = vop(lambda en: en.tensor_scalar(out=carry[:, :], in0=carry[:, :],
                                                                   scalar1=selc[0:1, 0:1], scalar2=None,
                                                                   op0=ALU.mult))
                return

            dv_x0lo = vop(lambda en, s=sl: en.tensor_sub(out=x0lo, in0=sx[s][:], in1=sxb[s]),
                          waits=[hh["mx"], mxb, pe_uuse_slot[0], pe_uuse_slot[1]])
            if co == 0:
                for i in range(8):
                    sg_ready = vop(lambda en, i=i: en.tensor_copy(
                        out=lhsTmega[i], in_=yR[:, i * 128:(i + 1) * 128]), waits=[sg_ready])
            pz1 = pop(lambda en, uh=u_hi: en.matmul(out=p2[:], lhsT=lhsTz, rhs=uh[:, 0:512], start=True,
                                                    stop=True),
                      waits=[dv_u, last_dv_zg, last_dv_sums, last_dv_fft])
            pz2 = pop(lambda en, uh=u_hi: en.matmul(out=p3[:], lhsT=lhsTz, rhs=uh[:, 512:1024],
                                                    start=True, stop=True))
            vop(lambda en: en.tensor_mul(out=ufR[0:64, 0:512], in0=gstk[0:64, 0:512], in1=p2[0:64, :]),
                waits=[pz1, sg_ready])
            vop(lambda en: en.tensor_mul(out=ufR[0:64, 512:1024], in0=gstk[0:64, 512:1024],
                                         in1=p3[0:64, :]), waits=[pz2])
            vop(lambda en: en.tensor_mul(out=ufI[0:64, 0:512], in0=gswp[0:64, 0:512], in1=p2[64:128, :]))
            vop(lambda en: en.tensor_mul(out=ufI[0:64, 512:1024], in0=gswp[0:64, 512:1024],
                                         in1=p3[64:128, :]))
            vop(lambda en, co=co: en.tensor_sub(out=ZG[co][0:64, :], in0=ufR[0:64, :], in1=ufI[0:64, :]))
            vop(lambda en: en.tensor_mul(out=ufR[0:64, 0:512], in0=gswp[0:64, 0:512], in1=p2[0:64, :]))
            vop(lambda en: en.tensor_mul(out=ufR[0:64, 512:1024], in0=gswp[0:64, 512:1024],
                                         in1=p3[0:64, :]))
            vop(lambda en: en.tensor_mul(out=ufI[0:64, 0:512], in0=gstk[0:64, 0:512], in1=p2[64:128, :]))
            vop(lambda en: en.tensor_mul(out=ufI[0:64, 512:1024], in0=gstk[0:64, 512:1024],
                                         in1=p3[64:128, :]))
            dv_zg = vop(lambda en, co=co: en.tensor_add(out=ZG[co][64:128, :], in0=ufR[0:64, :],
                                                        in1=ufI[0:64, :]))
            zg_marks[co] = dv_zg
            last_dv_zg = dv_zg
            pop(lambda en, uh=u_hi: en.matmul(out=p2[0:1, :], lhsT=onescol[:], rhs=uh[:, 0:512],
                                              start=True, stop=False), waits=[dv_zg, last_dv_sums])
            pop(lambda en, ul=u_lo: en.matmul(out=p2[0:1, :], lhsT=onescol[:], rhs=ul[:, 0:512],
                                              start=False, stop=True))
            pop(lambda en, uh=u_hi: en.matmul(out=p3[0:1, :], lhsT=onescol[:], rhs=uh[:, 512:1024],
                                              start=True, stop=False))
            pe_sums = pop(lambda en, ul=u_lo: en.matmul(out=p3[0:1, :], lhsT=onescol[:],
                                                        rhs=ul[:, 512:1024], start=False, stop=True))
            vop(lambda en: en.tensor_copy(out=rowpB[0:1, 0:1024], in_=carry[:, :]),
                waits=[last_dv_sums], ss=True)
            dv_cr = vop(lambda en: en.tensor_sub(out=rowpB[0:1, 1024:2048], in0=carry[:, :],
                                                 in1=rowpB[0:1, 0:1024]), ss=True)
            vop(lambda en: en.tensor_copy(out=crow[:, 0:512], in_=p2[0:1, :]), waits=[pe_sums], ss=True)
            vop(lambda en: en.tensor_copy(out=crow[:, 512:1024], in_=p3[0:1, :]), ss=True)
            dv_se = vop(lambda en: en.tensor_add(out=carry[:, :], in0=carry[:, :], in1=crow[:, :]),
                        ss=True)
            last_dv_sums = dv_se
            pe_accB = None
            for half in range(2):
                hs = slice(half * 512, (half + 1) * 512)
                pd = pB[:, hs]
                seq = [(tri, u_hi[:, hs], [dv_cr, last_dv_cum, last_dv_soft] if half == 0 else []),
                       (tri, u_lo[:, hs], []),
                       (onesall[0:1, :], rowpB[0:1, hs], []),
                       (onesall[0:1, :], rowpB[0:1, 1024 + hs.start:1024 + hs.stop], [])]
                for i, (l, r, w) in enumerate(seq):
                    pe_accB = pop(lambda en, l=l, r=r, pd=pd, st=(i == 0), sp=(i == len(seq) - 1):
                                  en.matmul(out=pd, lhsT=l, rhs=r, start=st, stop=sp), waits=w)
            f1 = pop(lambda en, uh=u_hi: en.matmul(out=p2[:], lhsT=fcos, rhs=uh[:, 0:512], start=True,
                                                   stop=True), waits=[dv_u, dv_se])
            f2 = pop(lambda en, uh=u_hi: en.matmul(out=p3[:], lhsT=fcos, rhs=uh[:, 512:1024], start=True,
                                                   stop=True))
            aop(lambda en: en.activation(out=ufR[:, 0:512], in_=p2[:], func=AF.Copy), waits=[f1])
            dv_ufr = aop(lambda en: en.activation(out=ufR[:, 512:1024], in_=p3[:], func=AF.Copy),
                         waits=[f2])
            f1 = pop(lambda en, uh=u_hi: en.matmul(out=p2[:], lhsT=fsinn, rhs=uh[:, 0:512], start=True,
                                                   stop=True), waits=[dv_ufr])
            f2 = pop(lambda en, uh=u_hi: en.matmul(out=p3[:], lhsT=fsinn, rhs=uh[:, 512:1024],
                                                   start=True, stop=True))
            pop(lambda en, uh=u_hi: en.matmul(out=pA[0:1, 0:512], lhsT=altcol[:], rhs=uh[:, 0:512],
                                              start=True, stop=True), waits=[last_dv_xfm1])
            f8 = pop(lambda en, uh=u_hi: en.matmul(out=pA[0:1, 512:1024], lhsT=altcol[:],
                                                   rhs=uh[:, 512:1024], start=True, stop=True))
            aop(lambda en: en.activation(out=ufI[:, 0:512], in_=p2[:], func=AF.Copy), waits=[f1])
            dv_ufi = aop(lambda en: en.activation(out=ufI[:, 512:1024], in_=p3[:], func=AF.Copy),
                         waits=[f2])
            last_dv_fft = dv_ufi
            vop(lambda en: en.tensor_mul(out=yR, in0=ufR, in1=khR), waits=[dv_ufi, dv_kh8, dv_ialt])
            vop(lambda en: en.tensor_mul(out=yI, in0=ufI, in1=khI))
            vop(lambda en: en.tensor_sub(out=yR, in0=yR, in1=yI))
            vop(lambda en: en.tensor_mul(out=yI, in0=ufI, in1=khR))
            vop(lambda en: en.tensor_mul(out=ufI, in0=ufR, in1=khI))
            vop(lambda en: en.tensor_add(out=yI, in0=yI, in1=ufI))
            vop(lambda en: en.tensor_mul(out=rowpB[0:1, 2048:2560], in0=pA[0:1, 0:512],
                                         in1=rowpB[0:1, 3072:3584]), waits=[f8], ss=True)
            dv_y = vop(lambda en: en.tensor_mul(out=rowpB[0:1, 2560:3072], in0=pA[0:1, 512:1024],
                                                in1=rowpB[0:1, 3584:4096]), ss=True)
            last_dv_zg = dv_y
            pe_accA = None
            for half in range(2):
                hs = slice(half * 512, (half + 1) * 512)
                pd = pA[:, hs]
                seq = []
                seq.append((icosw, yR[:, hs], [dv_y, last_dv_x1pre, last_dv_xfm1] if half == 0 else []))
                seq.append((isinnw, yI[:, hs], []))
                seq.append((rowpA[0:1, 2064:2192], rowpB[0:1, 2048 + hs.start:2048 + hs.stop], []))
                for cp in range(co):
                    seq.append((lhsTD[co - 1 - cp], ZG[cp][:, hs],
                                [zg_marks[cp]] if half == 0 else []))
                seq.append((lhsTD[co], sg_home[:, hs], [sg_ready] if half == 0 else []))
                seq.append((identb, u_hi[:, hs], []))
                seq.append((identb, u_lo[:, hs], []))
                seq.append((identb, sxb[sl][:, hs], [mxb] if half == 0 else []))
                seq.append((identb, x0lo[:, hs], [dv_x0lo] if half == 0 else []))
                for i, (l, r, w) in enumerate(seq):
                    pe_accA = pop(lambda en, l=l, r=r, pd=pd, st=(i == 0), sp=(i == len(seq) - 1):
                                  en.matmul(out=pd, lhsT=l, rhs=r, start=st, stop=sp), waits=w)
            last_pe_uuse = pe_accB
            pe_uuse_slot[sl] = pe_accA
            vop(lambda en: en.tensor_mul(out=x1pre, in0=drep, in1=pB[:]),
                waits=[pe_accB, last_ac_st2])
            dv_pre = vop(lambda en: en.tensor_add(out=x1pre, in0=x1pre, in1=pA[:]), waits=[pe_accA])
            last_dv_x1pre = dv_pre
            last_dv_cum = dv_pre
            free_x[sl] = dv_pre
            ac_st2 = ln_stats(x1pre[:, 0:512], x1pre[:, 512:1024], False,
                              waits=[dv_pre, (cDV, cDV.v)])
            last_ac_st2 = ac_st2
            dv_x1 = vop(lambda en: en.tensor_scalar(out=x1f, in0=x1pre, scalar1=stat[:, 39:40],
                                                    scalar2=stat[:, 40:41], op0=ALU.mult, op1=ALU.add),
                        waits=[ac_st2] + ([last_mscr] if last_mscr else []))
            mscr = dma("sync", x1scr[co * 128:(co + 1) * 128, :], x1f, cLD, waits=[dv_x1])
            x1f_written.append(mscr)
            last_mscr = mscr
            x1tp = None
            for k in range(8):
                pd = pA[:, k * 128:(k + 1) * 128]
                x1tp = pop(lambda en, k=k, pd=pd: en.transpose(
                    out=pd, in_=x1f[:, k * 128:(k + 1) * 128], identity=identf[:]),
                    waits=[dv_x1] if k == 0 else ())
            dv_xfm1 = None
            for k in range(8):
                dv_xfm1 = aop(lambda en, k=k, co=co: en.activation(
                    out=x1fm[k][:, co * 128:(co + 1) * 128], in_=pA[:, k * 128:(k + 1) * 128],
                    func=AF.Copy),
                    waits=[x1tp] if k == 0 else [])
            last_dv_xfm1 = dv_xfm1
            pop(lambda en, co=co: en.matmul(out=pB[:, 0:8], lhsT=x1fm[0][:, co * 128:(co + 1) * 128],
                                            rhs=wrbf[:, 0:8], start=True, stop=False),
                waits=[dv_xfm1, gl_b2])
            for k in range(1, 8):
                pop(lambda en, k=k, co=co: en.matmul(out=pB[:, 0:8],
                                                     lhsT=x1fm[k][:, co * 128:(co + 1) * 128],
                                                     rhs=wrbf[:, k * 8:(k + 1) * 8], start=False,
                                                     stop=False))
            pop(lambda en: en.matmul(out=pB[:, 0:8], lhsT=onesall[0:1, :],
                                     rhs=rowpA[0:1, 2048:2056], start=False, stop=False))
            pe_rt = pop(lambda en: en.matmul(out=pB[:, 0:8], lhsT=onesall[0:1, :],
                                             rhs=rowpA[0:1, 2056:2064], start=False, stop=True))
            vop(lambda en: en.reduce_max(out=stat[:, 9:10], in_=pB[:, 0:8], axis=AX.X),
                waits=[pe_rt], ss=True)
            vop(lambda en: en.tensor_scalar(out=l2t[:], in0=pB[:, 0:8], scalar1=stat[:, 9:10],
                                            scalar2=None, op0=ALU.subtract), ss=True)
            vop(lambda en: en.tensor_single_scalar(out=msk8[:], in_=l2t[:], scalar=0.0, op=ALU.is_ge),
                ss=True)
            dv_l2 = vop(lambda en: en.tensor_tensor_scan(out=cs8[:], data0=msk8[:], data1=ones8[:],
                                                         initial=0.0, op0=ALU.add, op1=ALU.mult), ss=True)
            ac_e = aop(lambda en: en.activation(out=exp8[:], in_=l2t[:], func=AF.Exp), waits=[dv_l2])
            vop(lambda en: en.tensor_single_scalar(out=cs8[:], in_=cs8[:], scalar=1.5, op=ALU.is_le),
                ss=True)
            vop(lambda en: en.tensor_mul(out=msk8[:], in0=msk8[:], in1=cs8[:]), ss=True)
            vop(lambda en: en.reduce_sum(out=stat[:, 10:11], in_=exp8[:], axis=AX.X), waits=[ac_e],
                ss=True)
            vop(lambda en: en.reciprocal(out=stat[:, 11:12], in_=stat[:, 10:11]), ss=True)
            last_dv_soft = vop(lambda en, co=co: en.tensor_scalar(
                out=geAll[:, co * 8:(co + 1) * 8], in0=msk8[:], scalar1=stat[:, 11:12], scalar2=None,
                op0=ALU.mult), ss=True)

        if run_attn:
            ts_list = list(range(n_pre)) + list(range(NPRE, NPRE + n_own))
            prev = None
            for t in ts_list:
                hh = head(t)
                stats_part(hh)
                if prev is not None:
                    mm_part(prev)
                prev = hh
            if prev is not None:
                mm_part(prev)

        attn_done = (cDV, cDV.v)
        pe_attn_done = (cPE, cPE.v)
        ld_attn_done = (cLD, cLD.v)
        ac_attn_done = (cAC, cAC.v)
        pe_hdone = [None] * NE
        gl_w1 = [None] * NE
        last_dv_gate = attn_done
        prev_gate = attn_done
        ac_hev = {}
        w2c_marks = {}

        for ei in range(n_experts):
            if ei == 0:
                w = [attn_done, pe_attn_done, ac_attn_done]
                for k in range(8):
                    dma("gpsimd", w1res[k], w1[ei, k * 128:(k + 1) * 128, :], cGL,
                        waits=w if k == 0 else ())
                gl_w1[ei] = (cGL, cGL.v)
            for tt in range(NTT):
                ts = slice(tt * TT_TOK, (tt + 1) * TT_TOK)
                prev_gate = last_dv_gate
                for mt in range(32):
                    pd = p0 if mt % 2 == 0 else p1
                    w = []
                    if mt == 0:
                        w = [gl_w1[ei], last_dv_gate]
                        if ei == 0 and tt == 0:
                            w.append(ac_attn_done)
                    elif mt == 1:
                        w = [last_dv_gate]
                    else:
                        pm = ac_hev.get((ei, tt, mt - 2))
                        if pm:
                            w = [pm]
                    for k in range(8):
                        pop(lambda en, k=k, mt=mt, pd=pd, ts=ts: en.matmul(
                            out=pd[:], lhsT=w1res[k][:, mt * 128:(mt + 1) * 128], rhs=x1fm[k][:, ts],
                            start=(k == 0), stop=(k == 7)), waits=w if k == 0 else ())
                    pe_h = (cPE, cPE.v)
                    if mt == 31:
                        pe_hdone[ei] = pe_h
                    ac_hev[(ei, tt, mt)] = aop(lambda en, mt=mt, pd=pd, ei=ei: en.activation(
                        out=hbuf[:, mt * TT_TOK:(mt + 1) * TT_TOK], in_=pd[:], func=AF.Relu,
                        bias=b1c[:, ei * 32 + mt:ei * 32 + mt + 1]), waits=[pe_h])
                ps_halves = [(pA[:, 0:512], pA[:, 512:1024]), (pB[:, 0:512], pB[:, 512:1024]),
                             (p0[:], p1[:]), (p2[:], p3[:])]
                pe_w2 = None
                for k in range(32):
                    slot = k % 4
                    wd = []
                    pm = w2c_marks.get((ei, tt, k - 4))
                    if pm:
                        wd.append(pm)
                    if k < 4:
                        wd.append(prev_gate)
                    m = dma("gpsimd", w2buf[slot], w2[ei, k * 128:(k + 1) * 128, :], cGL, waits=wd)
                    if tt == NTT - 1 and ei + 1 < n_experts and 8 <= k < 16:
                        kk = k - 8
                        dma("gpsimd", w1res[kk], w1[ei + 1, kk * 128:(kk + 1) * 128, :], cGL,
                            waits=[pe_hdone[ei]] if kk == 0 else ())
                        if kk == 7:
                            gl_w1[ei + 1] = (cGL, cGL.v)
                    for tM in range(4):
                        for hf in range(2):
                            w = []
                            if tM == 0 and hf == 0:
                                w = [m]
                                if k == 0:
                                    w += [prev_gate, ac_hev[(ei, tt, 31)]]
                            pe_w2 = pop(lambda en, k=k, tM=tM, hf=hf, slot=slot: en.matmul(
                                out=ps_halves[tM][hf],
                                lhsT=hbuf[:, k * TT_TOK + tM * 128:k * TT_TOK + (tM + 1) * 128],
                                rhs=w2buf[slot][:, hf * 512:(hf + 1) * 512],
                                start=(k == 0), stop=(k == 31)), waits=w)
                    w2c_marks[(ei, tt, k)] = pe_w2

                dvg = None
                for tM in range(4):
                    g = tt * 4 + tM
                    w = [pe_w2, last_dv_soft]
                    if ei == 0 and tt == 0:
                        w.append(ld_attn_done)
                    pm = moe_acc_marks.get((g, ei - 1))
                    if pm:
                        w.append(pm)
                    gcol = geAll[:, g * 8 + ei:g * 8 + ei + 1]
                    if tM == 0:
                        dvg = vop(lambda en, gc=gcol: en.tensor_scalar(out=tmpg[0], in0=pA[:], scalar1=gc,
                                                                       scalar2=None, op0=ALU.mult), waits=w)
                    elif tM == 1:
                        dvg = vop(lambda en, gc=gcol: en.tensor_scalar(out=tmpg[1], in0=pB[:], scalar1=gc,
                                                                       scalar2=None, op0=ALU.mult), waits=w)
                    else:
                        pp = (p0, p1) if tM == 2 else (p2, p3)
                        vop(lambda en, gc=gcol, pp=pp, tM=tM: en.tensor_scalar(
                            out=tmpg[tM][:, 0:512], in0=pp[0][:], scalar1=gc, scalar2=None,
                            op0=ALU.mult), waits=w)
                        dvg = vop(lambda en, gc=gcol, pp=pp, tM=tM: en.tensor_scalar(
                            out=tmpg[tM][:, 512:1024], in0=pp[1][:], scalar1=gc, scalar2=None,
                            op0=ALU.mult))
                    gr = slice(g * 128, (g + 1) * 128)
                    mden = OP("gpsimd", lambda en, gr=gr, tM=tM, ei=ei: en.dma_start(
                        out=accd[gr, :], in_=tmpg[tM],
                        accum_op=(ALU.bypass if ei == 0 else ALU.add)), cGL, k=16, waits=[dvg])
                    moe_acc_marks[(g, ei)] = mden
                last_dv_gate = dvg

        moe_all_gl = (cGL, cGL.v)
        moe_all_dv = (cDV, cDV.v)

        st_marks = []
        for g in range(16 if run_final else 0):
            sl = g % 2
            gr = slice(g * 128, (g + 1) * 128)
            w = []
            if n_experts > 0:
                w.append(moe_acc_marks[(g, n_experts - 1)])
            if run_attn and g < n_own:
                w.append(x1f_written[g])
            if g == 0:
                w += [moe_all_gl, moe_all_dv]
            if len(st_marks) >= 2:
                w.append(st_marks[-2])
            m1 = dma("sync", x1g[sl], x1scr[gr, :], cLD, waits=w)
            m2 = dma("sync", accg[sl], accd[gr, :], cLD)
            ptp = pop(lambda en, g=g: en.transpose(out=p2[0:8, 0:128], in_=geAll[:, g * 8:(g + 1) * 8],
                                                   identity=identf[:]),
                      waits=([moe_all_dv, (cDV, cDV.v)] if True else []))
            dge = vop(lambda en: en.tensor_copy(out=geT[:], in_=p2[0:8, 0:128]), waits=[ptp])
            pb2 = None
            for hf in range(2):
                pd = p0 if hf == 0 else p1
                pop(lambda en, pd=pd, hf=hf: en.matmul(out=pd[:], lhsT=geT[:],
                                                       rhs=b2hi_t[:, hf * 512:(hf + 1) * 512],
                                                       start=True, stop=False), waits=[dge])
                pb2 = pop(lambda en, pd=pd, hf=hf: en.matmul(out=pd[:], lhsT=geT[:],
                                                             rhs=b2lo_t[:, hf * 512:(hf + 1) * 512],
                                                             start=False, stop=True))
            dv1 = vop(lambda en, sl=sl: en.tensor_add(out=resb, in0=x1g[sl], in1=accg[sl]),
                      waits=[m1, m2, (cAC, cAC.v)])
            vop(lambda en: en.tensor_add(out=resb[:, 0:512], in0=resb[:, 0:512], in1=p0[:]),
                waits=[pb2])
            dv1 = vop(lambda en: en.tensor_add(out=resb[:, 512:1024], in0=resb[:, 512:1024], in1=p1[:]))
            ac3 = ln_stats(resb[:, 0:512], resb[:, 512:1024], False, waits=[dv1])
            dv2 = vop(lambda en, sl=sl: en.tensor_scalar(out=outb[sl], in0=resb, scalar1=stat[:, 39:40],
                                                         scalar2=stat[:, 40:41], op0=ALU.mult,
                                                         op1=ALU.add), waits=[ac3])
            st_marks.append(dma("sync", out[gr, :], outb[sl], cST, waits=[dv2]))
        if dbg:
            B.wait("vector", (cPE, cPE.v))
            B.wait("vector", (cAC, cAC.v))
            dcp = vop(lambda en: en.tensor_copy(out=ftmp, in_=uf32))
            dma("sync", dbg1[:, 0:1024], ftmp, cST, waits=[dcp])
            dma("sync", dbg1[:, 1024:1152], fcos, cST)
            B.plan["sync"].pop()
            cST.v -= 16
            dcp = vop(lambda en: en.tensor_copy(out=ftmp[:, 0:128], in_=fcos))
            dcp = vop(lambda en: en.tensor_copy(out=ftmp[:, 128:256], in_=icosw))
            dcp = vop(lambda en: en.tensor_copy(out=ftmp[:, 256:384], in_=lhsTD[0]))
            dcp = vop(lambda en: en.tensor_copy(out=ftmp[:, 384:512], in_=lhsTz))
            dcp = vop(lambda en: en.tensor_copy(out=ftmp[:, 512:640], in_=tri))
            dma("sync", dbg1[:, 1024:1664], ftmp[:, 0:640], cST, waits=[dcp])
            dcp = vop(lambda en: en.tensor_copy(out=ftmp, in_=khR))
            dma("sync", dbg1[:, 2048:3072], ftmp, cST, waits=[dcp])
            dcp = vop(lambda en: en.tensor_copy(out=ftmp, in_=u_hi))
            dma("sync", dbg1[:, 3072:4096], ftmp, cST, waits=[dcp])
            dcp = vop(lambda en: en.tensor_copy(out=ftmp, in_=gstk))
            dma("sync", dbg1[:, 4096:5120], ftmp, cST, waits=[dcp])
            dcp = vop(lambda en: en.tensor_copy(out=ftmp, in_=ZG[15]))
            dma("sync", dbg1[:, 5120:6144], ftmp, cST, waits=[dcp])
            dma("sync", dbg1[:, 6144:6656], dbgS[:, :], cST)
            dcp = vop(lambda en: en.tensor_copy(out=ftmp[0:1, 256:1024], in_=carry[:, 0:768]))
            dma("sync", dbg1[:, 6656:7680], ftmp, cST, waits=[dcp])
            dcp = vop(lambda en: en.tensor_copy(out=ftmp, in_=x1pre))
            dma("sync", dbg1[:, 7168:8192], ftmp, cST, waits=[dcp])
        B.plan["sync"].append(("wait", cST.h, cST.v))

        B.emit()
    return nc


_NC_CACHE = None


def kernel(**inp):
    global _NC_CACHE
    x = np.ascontiguousarray(np.asarray(inp["L0_attn_x"], dtype=np.float32))
    _ = np.asarray(inp["L0_attn_D_im"])
    ins = {
        "w_in": np.ascontiguousarray(np.asarray(inp["L0_attn_W_in"], np.float32)),
        "b_in": np.ascontiguousarray(np.asarray(inp["L0_attn_b_in"], np.float32)),
        "gamma": np.ascontiguousarray(np.asarray(inp["L0_attn_gamma"], np.float32)),
        "beta": np.ascontiguousarray(np.asarray(inp["L0_attn_beta"], np.float32)),
        "a_re": np.ascontiguousarray(np.asarray(inp["L0_attn_A_re"], np.float32)),
        "a_im": np.ascontiguousarray(np.asarray(inp["L0_attn_A_im"], np.float32)),
        "b_re": np.ascontiguousarray(np.asarray(inp["L0_attn_B_re"], np.float32)),
        "b_im": np.ascontiguousarray(np.asarray(inp["L0_attn_B_im"], np.float32)),
        "c_re": np.ascontiguousarray(np.asarray(inp["L0_attn_C_re"], np.float32)),
        "c_im": np.ascontiguousarray(np.asarray(inp["L0_attn_C_im"], np.float32)),
        "d_re": np.ascontiguousarray(np.asarray(inp["L0_attn_D_re"], np.float32)),
        "r_w": np.ascontiguousarray(np.asarray(inp["L0_ffn_router_W"], np.float32)),
        "r_b": np.ascontiguousarray(np.asarray(inp["L0_ffn_router_b"], np.float32)),
        "w1": np.ascontiguousarray(np.asarray(inp["L0_ffn_W1"], np.float32)),
        "b1": np.ascontiguousarray(np.asarray(inp["L0_ffn_b1"], np.float32)),
        "w2": np.ascontiguousarray(np.asarray(inp["L0_ffn_W2"], np.float32)),
        "b2": np.ascontiguousarray(np.asarray(inp["L0_ffn_b2"], np.float32)),
    }
    if _NC_CACHE is None:
        _NC_CACHE = build_program()
    nc = _NC_CACHE
    in_maps = []
    for cid in range(8):
        b, h = cid >> 1, cid & 1
        xf = np.concatenate([x[b, 0:2048], x[b, h * 2048:(h + 1) * 2048]], axis=0)
        m = dict(ins)
        m["xfull"] = np.ascontiguousarray(xf)
        m["sel"] = np.full((128, 1), float(h), np.float32)
        in_maps.append(m)
    res = run_bass_kernel_spmd(nc, in_maps, core_ids=list(range(8)))
    outa = np.empty((4, 4096, D), np.float32)
    for cid, r in enumerate(res.results):
        b, h = cid >> 1, cid & 1
        outa[b, h * 2048:(h + 1) * 2048] = r["out"]
    return outa


# version 10
# speedup vs baseline: 1.0171x; 1.0071x over previous
import numpy as np
from contextlib import ExitStack

import concourse.bass as bass
import concourse.mybir as mybir
from concourse.bass_utils import run_bass_kernel_spmd

F32 = mybir.dt.float32
BF16 = mybir.dt.bfloat16
I32 = mybir.dt.int32
AF = mybir.ActivationFunctionType
ALU = mybir.AluOpType
AX = mybir.AxisListType

C = 128
NPRE = 16
NOWN = 16
D = 1024
NS = 64
NE = 8
DFF = 4096
TT_TOK = 512
NTT = 4
INVF = 1.0 / 256.0
TWO_PI = float(2.0 * np.pi)
ARENA = 54272
COALESCE_OFF = True


class Ctr:
    def __init__(self, h, name):
        self.h = h
        self.name = name
        self.v = 0


class Builder:
    ENGS = ("sync", "gpsimd", "tensor", "vector", "scalar")

    def __init__(self, nc):
        self.nc = nc
        self.plan = {e: [] for e in self.ENGS}
        self.hw = {}
        self.waited = set()

    def ctr(self, ctx, name):
        return Ctr(ctx.enter_context(self.nc.semaphore(name)), name)

    def wait(self, eng, cv):
        if cv is None:
            return
        c, v = cv
        if v <= 0:
            return
        if self.hw.get((eng, c.name), 0) >= v:
            return
        self.plan[eng].append(("wait", c.h, v))
        self.hw[(eng, c.name)] = v
        self.waited.add((c.name, v))

    force_ss = False

    def op(self, eng, fn, inc=None, k=1, waits=(), ss=False):
        for w in waits:
            self.wait(eng, w)
        incs = []
        r = None
        if inc is not None:
            inc.v += k
            incs = [(inc, k, inc.v)]
            r = (inc, inc.v)
        self.plan[eng].append(("op", fn, incs))
        if (ss or self.force_ss) and r is not None:
            self.wait(eng, r)
        return r

    def emit(self):
        plan = self.plan
        waited = self.waited
        with self.nc.Block() as block:
            def run(eng_obj, key):
                pend = {}
                last_inst = None
                for item in plan[key]:
                    if item[0] == "wait":
                        eng_obj.wait_ge(item[1], item[2])
                    else:
                        inst = item[1](eng_obj)
                        last_inst = inst
                        for (c, k, vafter) in item[2]:
                            if k == 16:
                                inst.then_inc(c.h, k)
                            else:
                                pend[c.name] = (c, pend.get(c.name, (c, 0))[1] + k)
                                if COALESCE_OFF or (c.name, vafter) in waited:
                                    cc, p = pend.pop(c.name)
                                    inst.then_inc(cc.h, p)
                for name, (c, p) in pend.items():
                    if last_inst is not None:
                        last_inst.then_inc(c.h, p)

            @block.sync
            def _(e):
                run(e, "sync")

            @block.gpsimd
            def _(e):
                run(e, "gpsimd")

            @block.tensor
            def _(e):
                run(e, "tensor")

            @block.vector
            def _(e):
                run(e, "vector")

            @block.scalar
            def _(e):
                run(e, "scalar")


def bcast_ap(dram_ap, n):
    return bass.AP(dram_ap.tensor, 0, [[0, 128], [1, n]])


def build_program(run_attn=True, n_experts=NE, run_final=True, n_own=NOWN, n_pre=NPRE, chunk_ops=6, dbg=False):
    nc = bass.Bass("TRN2", target_bir_lowering=False)

    xfull = nc.dram_tensor("xfull", [4096, D], F32, kind="ExternalInput").ap()
    w_in = nc.dram_tensor("w_in", [D, D], F32, kind="ExternalInput").ap()
    b_in = nc.dram_tensor("b_in", [D], F32, kind="ExternalInput").ap()
    gamma = nc.dram_tensor("gamma", [D], F32, kind="ExternalInput").ap()
    beta = nc.dram_tensor("beta", [D], F32, kind="ExternalInput").ap()
    a_re = nc.dram_tensor("a_re", [NS], F32, kind="ExternalInput").ap()
    a_im = nc.dram_tensor("a_im", [NS], F32, kind="ExternalInput").ap()
    b_re = nc.dram_tensor("b_re", [NS], F32, kind="ExternalInput").ap()
    b_im = nc.dram_tensor("b_im", [NS], F32, kind="ExternalInput").ap()
    c_re = nc.dram_tensor("c_re", [D, NS], F32, kind="ExternalInput").ap()
    c_im = nc.dram_tensor("c_im", [D, NS], F32, kind="ExternalInput").ap()
    d_re = nc.dram_tensor("d_re", [D], F32, kind="ExternalInput").ap()
    r_w = nc.dram_tensor("r_w", [D, NE], F32, kind="ExternalInput").ap()
    r_b = nc.dram_tensor("r_b", [NE], F32, kind="ExternalInput").ap()
    w1 = nc.dram_tensor("w1", [NE, D, DFF], F32, kind="ExternalInput").ap()
    b1 = nc.dram_tensor("b1", [NE, DFF], F32, kind="ExternalInput").ap()
    w2 = nc.dram_tensor("w2", [NE, DFF, D], F32, kind="ExternalInput").ap()
    b2 = nc.dram_tensor("b2", [NE, D], F32, kind="ExternalInput").ap()
    sel = nc.dram_tensor("sel", [128, 1], F32, kind="ExternalInput").ap()
    out = nc.dram_tensor("out", [2048, D], F32, kind="ExternalOutput").ap()
    kind_scr = "ExternalOutput" if dbg else "Internal"
    x1scr = nc.dram_tensor("x1scr", [2048, D], F32, kind=kind_scr).ap()
    accd = nc.dram_tensor("accd", [2048, D], F32, kind=kind_scr).ap()
    dbg1 = nc.dram_tensor("dbg1", [128, 8192], F32, kind="ExternalOutput").ap() if dbg else None

    with ExitStack() as ctx:
        e = ctx.enter_context
        B = Builder(nc)

        AR = e(nc.sbuf_tensor("AR", [128, ARENA], BF16))
        off = [0]

        def carve(n):
            o = off[0]
            off[0] += n
            assert off[0] <= ARENA, f"bf16 arena overflow {off[0]}"
            return AR[:, o:o + n]

        winT = [carve(D) for _ in range(8)]
        ZG = [carve(D) for _ in range(NOWN)]
        khR = carve(D)
        khI = carve(D)
        fcos = carve(128)
        fsinn = carve(128)
        icosw = carve(128)
        isinnw = carve(128)
        kr0 = carve(D)
        lhsTD = [carve(128) for _ in range(16)]
        lhsTz = carve(128)
        megaR = carve(2048)
        lhsTmega = [megaR[:, c * 128:(c + 1) * 128] for c in range(NPRE)]
        sg_home = megaR[:, 0:1024]
        tri = carve(128)
        identb = carve(128)
        u_hiB = [carve(D) for _ in range(2)]
        u_loB = [carve(D) for _ in range(2)]
        x0lo = carve(D)
        sxb = [carve(D) for _ in range(2)]
        xfm = carve(D)
        ufR = carve(D)
        ufI = carve(D)
        yR = carve(D)
        yI = carve(D)
        rowpA = carve(2304)
        rowpB = carve(4096)
        gstk = carve(D)
        gswp = carve(D)
        stgb = carve(128)
        attn_end = off[0]
        assert attn_end <= ARENA

        w1res = [AR[:, k * DFF:(k + 1) * DFF] for k in range(8)]
        hbuf = AR[:, 32768:32768 + 32 * TT_TOK]
        w2buf = [AR[:, 49152 + i * D:49152 + (i + 1) * D] for i in range(4)]

        x1fm = [e(nc.sbuf_tensor(f"x1fm{k}", [128, 2048], BF16)) for k in range(8)]
        wrbf = e(nc.sbuf_tensor("wrbf", [128, NE * 8], BF16))
        b2f = e(nc.sbuf_tensor("b2f", [8, D], F32))
        b2hi_t = e(nc.sbuf_tensor("b2hi_t", [8, D], BF16))
        b2lo_t = e(nc.sbuf_tensor("b2lo_t", [8, D], BF16))
        geT = e(nc.sbuf_tensor("geT", [8, 128], BF16))
        onescol = e(nc.sbuf_tensor("onescol", [128, 1], BF16))
        onesall = e(nc.sbuf_tensor("onesall", [128, 128], BF16))
        altcol = e(nc.sbuf_tensor("altcol", [128, 1], BF16))

        AR2 = e(nc.sbuf_tensor("AR2", [128, 7680], F32))
        uf32 = AR2[:, 0:1024]
        x1pre = AR2[:, 1024:2048]
        x1f = AR2[:, 2048:3072]
        grep = AR2[:, 3072:4096]
        brep = AR2[:, 4096:5120]
        drep = AR2[:, 5120:6144]
        scrq = AR2[:, 6144:6656]
        ftmp = AR2[:, 6656:7680]
        csrow = AR2[0:1, 6656:7680]
        x1g = [AR2[:, 0:1024], AR2[:, 1024:2048]]
        accg = [AR2[:, 2048:3072], AR2[:, 3072:4096]]
        resb = AR2[:, 4096:5120]
        outb = [AR2[:, 5120:6144], AR2[:, 6144:7168]]
        tmpg = [AR2[:, i * 1024:(i + 1) * 1024] for i in range(4)]

        sx = [e(nc.sbuf_tensor(f"sx{i}", [128, D], F32)) for i in range(2)]
        rp1 = e(nc.sbuf_tensor("rp1", [128, D], F32))
        identf = e(nc.sbuf_tensor("identf", [128, 128], F32))
        b1c = e(nc.sbuf_tensor("b1c", [128, NE * 32], F32))
        geAll = e(nc.sbuf_tensor("geAll", [128, NOWN * 8], F32))
        selc = e(nc.sbuf_tensor("selc", [128, 1], F32))
        Tr = e(nc.sbuf_tensor("Tr", [64, 128], F32))
        Ti = e(nc.sbuf_tensor("Ti", [64, 128], F32))
        Pfr = e(nc.sbuf_tensor("Pfr", [64, 128], F32))
        Pfi = e(nc.sbuf_tensor("Pfi", [64, 128], F32))
        Pvr = e(nc.sbuf_tensor("Pvr", [64, 128], F32))
        Pvi = e(nc.sbuf_tensor("Pvi", [64, 128], F32))
        At1 = e(nc.sbuf_tensor("At1", [64, 128], F32))
        At2 = e(nc.sbuf_tensor("At2", [64, 128], F32))
        Acol = e(nc.sbuf_tensor("Acol", [64, 8], F32))
        decr = e(nc.sbuf_tensor("decr", [64, 16], F32))
        deci = e(nc.sbuf_tensor("deci", [64, 16], F32))
        CrT = AR2[0:64, 0:1024]
        CiT = AR2[0:64, 1024:2048]
        stg32 = e(nc.sbuf_tensor("stg32", [128, 128], F32))
        jrow = e(nc.sbuf_tensor("jrow", [1, 128], F32))
        irow = e(nc.sbuf_tensor("irow", [1, 128], I32))
        itmp = e(nc.sbuf_tensor("itmp", [128, 128], I32))
        icol = e(nc.sbuf_tensor("icol", [128, 1], I32))
        fcol = e(nc.sbuf_tensor("fcol", [128, 1], F32))
        frows = e(nc.sbuf_tensor("frows", [128, D], F32))
        crow = e(nc.sbuf_tensor("crow", [1, D], F32))
        carry = frows[0:1, :]
        rbrow = frows[0:1, 0:NE]
        stat = e(nc.sbuf_tensor("stat", [128, 48], F32))
        l2t = e(nc.sbuf_tensor("l2t", [128, NE], F32))
        exp8 = e(nc.sbuf_tensor("exp8", [128, NE], F32))
        msk8 = e(nc.sbuf_tensor("msk8", [128, NE], F32))
        cs8 = e(nc.sbuf_tensor("cs8", [128, NE], F32))
        ones8 = e(nc.sbuf_tensor("ones8", [128, NE], F32))

        dbgS = e(nc.sbuf_tensor("dbgS", [128, 512], F32)) if dbg else None
        pA = e(nc.psum_tensor("pA", [128, 1024], F32))
        pB = e(nc.psum_tensor("pB", [128, 1024], F32))
        p0 = e(nc.psum_tensor("p0", [128, 512], F32))
        p1 = e(nc.psum_tensor("p1", [128, 512], F32))
        p2 = e(nc.psum_tensor("p2", [128, 512], F32))
        p3 = e(nc.psum_tensor("p3", [128, 512], F32))

        cLD = B.ctr(ctx, "cLD")
        cGL = B.ctr(ctx, "cGL")
        cGP = B.ctr(ctx, "cGP")
        cPE = B.ctr(ctx, "cPE")
        cDV = B.ctr(ctx, "cDV")
        cAC = B.ctr(ctx, "cAC")
        cST = B.ctr(ctx, "cST")

        OP = B.op

        def dma(eng, dst, src, inc, waits=()):
            return OP(eng, lambda en, d=dst, s=src: en.dma_start(out=d, in_=s), inc=inc, k=16, waits=waits)

        def vop(fn, waits=(), ss=False):
            return OP("vector", fn, cDV, waits=waits, ss=ss)

        def aop(fn, waits=(), ss=False):
            return OP("scalar", fn, cAC, waits=waits, ss=ss)

        def pop(fn, waits=()):
            return OP("tensor", fn, cPE, waits=waits)

        def ln_stats(src0, src1, from_psum, waits=(), uf_dst=None, sb=32):
            c = lambda i: stat[:, sb + i:sb + i + 1]
            aop(lambda en: en.activation(out=(uf_dst[:, 0:512] if from_psum else scrq[:, :]),
                                         in_=src0, func=AF.Copy, accum_out=c(0)), waits=waits)
            aop(lambda en: en.activation(out=(uf_dst[:, 512:1024] if from_psum else scrq[:, :]),
                                         in_=src1, func=AF.Copy, accum_out=c(1)))
            aop(lambda en: en.activation(out=scrq[:, :], in_=src0, func=AF.Square, accum_out=c(2)))
            aop(lambda en: en.activation(out=scrq[:, :], in_=src1, func=AF.Square, accum_out=c(3)),
                ss=True)
            aop(lambda en: en.add(out=c(4), in_=c(0), add=c(1)), ss=True)
            aop(lambda en: en.mul(out=c(4), in_=c(4), mul=1.0 / D), ss=True)
            aop(lambda en: en.add(out=c(5), in_=c(2), add=c(3)), ss=True)
            aop(lambda en: en.mul(out=c(5), in_=c(5), mul=1.0 / D), ss=True)
            aop(lambda en: en.activation(out=c(6), in_=c(4), func=AF.Square), ss=True)
            aop(lambda en: en.mul(out=c(6), in_=c(6), mul=-1.0), ss=True)
            aop(lambda en: en.add(out=c(5), in_=c(5), add=c(6)), ss=True)
            aop(lambda en: en.activation(out=c(6), in_=c(5), func=AF.Ln, bias=stat[:, 12:13]), ss=True)
            aop(lambda en: en.activation(out=c(7), in_=c(6), func=AF.Exp, scale=-0.5), ss=True)
            aop(lambda en: en.mul(out=c(8), in_=c(4), mul=-1.0), ss=True)
            return aop(lambda en: en.mul(out=c(8), in_=c(8), mul=c(7)), ss=True)

        B.force_ss = True
        ld0 = dma("sync", rp1[0:1, :], b_in[None, :], cLD)
        dma("sync", rp1[32:33, :], gamma[None, :], cLD)
        dma("sync", rp1[64:65, :], beta[None, :], cLD)
        dma("sync", rp1[96:97, :], d_re[None, :], cLD)
        dma("sync", grep, bcast_ap(gamma, D), cLD)
        dma("sync", brep, bcast_ap(beta, D), cLD)
        dma("sync", drep, bcast_ap(d_re, D), cLD)
        dma("sync", Acol[:, 0:1], a_re[:, None], cLD)
        dma("sync", Acol[:, 1:2], a_im[:, None], cLD)
        dma("sync", Acol[:, 2:3], b_re[:, None], cLD)
        dma("sync", Acol[:, 3:4], b_im[:, None], cLD)
        dma("sync", rbrow[:, :], r_b[None, :], cLD)
        dma("sync", selc[:, :], sel, cLD)
        ld_small = (cLD, cLD.v)

        for k in range(8):
            dma("gpsimd", wrbf[:, k * 8:(k + 1) * 8], r_w[k * 128:(k + 1) * 128, :], cGL)
        dma("gpsimd", b2hi_t[:], b2[:, :], cGL)
        gl_b2 = (cGL, cGL.v)

        OP("gpsimd", lambda en: en.memset(onescol[:], 1.0), cGP)
        OP("gpsimd", lambda en: en.memset(onesall[:], 1.0), cGP)
        OP("gpsimd", lambda en: en.memset(ones8[:], 1.0), cGP)
        OP("gpsimd", lambda en: en.memset(stat[:, 12:13], 1e-5), cGP)
        OP("gpsimd", lambda en: en.memset(tri, 1.0), cGP)
        OP("gpsimd", lambda en: en.affine_select(out=tri, in_=tri, pattern=[[1, 128]],
                                                 compare_op=ALU.is_ge, fill=0.0, base=0,
                                                 channel_multiplier=-1), cGP)
        OP("gpsimd", lambda en: en.memset(identb, 1.0), cGP)
        OP("gpsimd", lambda en: en.affine_select(out=identb, in_=identb, pattern=[[1, 128]],
                                                 compare_op=ALU.is_equal, fill=0.0, base=0,
                                                 channel_multiplier=-1), cGP)
        OP("gpsimd", lambda en: en.memset(identf[:], 1.0), cGP)
        OP("gpsimd", lambda en: en.affine_select(out=identf[:], in_=identf[:], pattern=[[1, 128]],
                                                 compare_op=ALU.is_equal, fill=0.0, base=0,
                                                 channel_multiplier=-1), cGP)
        OP("gpsimd", lambda en: en.iota(out=irow[:], pattern=[[1, 128]], base=0, channel_multiplier=0), cGP)
        gp_const = OP("gpsimd", lambda en: en.iota(out=icol[:], pattern=[[0, 1]], base=0,
                                                   channel_multiplier=1), cGP)

        vop(lambda en: en.tensor_copy(out=rowpA[0:1, 0:1024], in_=rp1[0:1, :]), waits=[ld0, ld_small])
        vop(lambda en: en.tensor_sub(out=rowpA[0:1, 1024:2048], in0=rp1[0:1, :],
                                     in1=rowpA[0:1, 0:1024]))
        vop(lambda en: en.tensor_copy(out=rowpA[0:1, 2048:2056], in_=rbrow[:, :]))
        vop(lambda en: en.tensor_sub(out=rowpA[0:1, 2056:2064], in0=rbrow[:, :],
                                     in1=rowpA[0:1, 2048:2056]))
        m = dma("sync", b2f[:], b2[:, :], cLD)
        vop(lambda en: en.tensor_sub(out=b2lo_t[:], in0=b2f[:], in1=b2hi_t[:]), waits=[m, gl_b2])

        vop(lambda en: en.tensor_copy(out=fcol[:], in_=icol[:]), waits=[gp_const])
        vop(lambda en: en.tensor_scalar(out=stat[:, 0:1], in0=fcol[:], scalar1=0.5, scalar2=None,
                                        op0=ALU.mult))
        vop(lambda en: en.tensor_copy(out=icol[:], in_=stat[:, 0:1]))
        vop(lambda en: en.tensor_copy(out=stat[:, 1:2], in_=icol[:]))
        dv_par = vop(lambda en: en.tensor_sub(out=stat[:, 2:3], in0=stat[:, 0:1], in1=stat[:, 1:2]))
        ac_abs = aop(lambda en: en.activation(out=stat[:, 4:5], in_=stat[:, 2:3], func=AF.Abs),
                     waits=[dv_par])
        dv_alt = vop(lambda en: en.tensor_scalar(out=altcol[:], in0=stat[:, 4:5], scalar1=-4.0,
                                                 scalar2=1.0, op0=ALU.mult, op1=ALU.add), waits=[ac_abs])
        vop(lambda en: en.tensor_copy(out=jrow[:], in_=irow[:]))

        vop(lambda en: en.memset(Tr[:, 0:1], 1.0), waits=[ld_small])
        vop(lambda en: en.memset(Ti[:, 0:1], 0.0))
        vop(lambda en: en.tensor_copy(out=Acol[:, 4:5], in_=Acol[:, 0:1]))
        vop(lambda en: en.tensor_copy(out=Acol[:, 5:6], in_=Acol[:, 1:2]))
        for lvl in range(7):
            n = 1 << lvl
            vop(lambda en, n=n: en.tensor_scalar(out=At1[:, 0:n], in0=Tr[:, 0:n], scalar1=Acol[:, 4:5],
                                                 scalar2=None, op0=ALU.mult))
            vop(lambda en, n=n: en.tensor_scalar(out=At2[:, 0:n], in0=Ti[:, 0:n], scalar1=Acol[:, 5:6],
                                                 scalar2=None, op0=ALU.mult))
            vop(lambda en, n=n: en.tensor_sub(out=Tr[:, n:2 * n], in0=At1[:, 0:n], in1=At2[:, 0:n]))
            vop(lambda en, n=n: en.tensor_scalar(out=At1[:, 0:n], in0=Tr[:, 0:n], scalar1=Acol[:, 5:6],
                                                 scalar2=None, op0=ALU.mult))
            vop(lambda en, n=n: en.tensor_scalar(out=At2[:, 0:n], in0=Ti[:, 0:n], scalar1=Acol[:, 4:5],
                                                 scalar2=None, op0=ALU.mult))
            vop(lambda en, n=n: en.tensor_add(out=Ti[:, n:2 * n], in0=At1[:, 0:n], in1=At2[:, 0:n]))
            vop(lambda en: en.tensor_mul(out=Acol[:, 6:7], in0=Acol[:, 4:5], in1=Acol[:, 4:5]))
            vop(lambda en: en.tensor_mul(out=Acol[:, 7:8], in0=Acol[:, 5:6], in1=Acol[:, 5:6]))
            vop(lambda en: en.tensor_mul(out=Acol[:, 5:6], in0=Acol[:, 4:5], in1=Acol[:, 5:6]))
            vop(lambda en: en.tensor_scalar(out=Acol[:, 5:6], in0=Acol[:, 5:6], scalar1=2.0,
                                            scalar2=None, op0=ALU.mult))
            vop(lambda en: en.tensor_sub(out=Acol[:, 4:5], in0=Acol[:, 6:7], in1=Acol[:, 7:8]))
        vop(lambda en: en.tensor_scalar(out=At1[:, :], in0=Tr[:, :], scalar1=Acol[:, 0:1],
                                        scalar2=None, op0=ALU.mult))
        vop(lambda en: en.tensor_scalar(out=At2[:, :], in0=Ti[:, :], scalar1=Acol[:, 1:2],
                                        scalar2=None, op0=ALU.mult))
        vop(lambda en: en.tensor_sub(out=Pfr[:, :], in0=At1[:, :], in1=At2[:, :]))
        vop(lambda en: en.tensor_scalar(out=At1[:, :], in0=Tr[:, :], scalar1=Acol[:, 1:2],
                                        scalar2=None, op0=ALU.mult))
        vop(lambda en: en.tensor_scalar(out=At2[:, :], in0=Ti[:, :], scalar1=Acol[:, 0:1],
                                        scalar2=None, op0=ALU.mult))
        vop(lambda en: en.tensor_add(out=Pfi[:, :], in0=At1[:, :], in1=At2[:, :]))
        vop(lambda en: en.tensor_copy(out=Pvr[:, :], in_=Tr[:, ::-1]))
        vop(lambda en: en.tensor_copy(out=Pvi[:, :], in_=Ti[:, ::-1]))
        vop(lambda en: en.memset(decr[:, 0:1], 1.0))
        vop(lambda en: en.memset(deci[:, 0:1], 0.0))
        for dd in range(1, 16):
            vop(lambda en, dd=dd: en.tensor_scalar(out=Acol[:, 6:7], in0=decr[:, dd - 1:dd],
                                                   scalar1=Acol[:, 4:5], scalar2=None, op0=ALU.mult))
            vop(lambda en, dd=dd: en.tensor_scalar(out=Acol[:, 7:8], in0=deci[:, dd - 1:dd],
                                                   scalar1=Acol[:, 5:6], scalar2=None, op0=ALU.mult))
            vop(lambda en, dd=dd: en.tensor_sub(out=decr[:, dd:dd + 1], in0=Acol[:, 6:7], in1=Acol[:, 7:8]))
            vop(lambda en, dd=dd: en.tensor_scalar(out=Acol[:, 6:7], in0=decr[:, dd - 1:dd],
                                                   scalar1=Acol[:, 5:6], scalar2=None, op0=ALU.mult))
            vop(lambda en, dd=dd: en.tensor_scalar(out=Acol[:, 7:8], in0=deci[:, dd - 1:dd],
                                                   scalar1=Acol[:, 4:5], scalar2=None, op0=ALU.mult))
            vop(lambda en, dd=dd: en.tensor_add(out=deci[:, dd:dd + 1], in0=Acol[:, 6:7], in1=Acol[:, 7:8]))
        for dd in range(16):
            vop(lambda en, dd=dd: en.tensor_scalar(out=At1[:, :], in0=Pfr[:, :], scalar1=decr[:, dd:dd + 1],
                                                   scalar2=None, op0=ALU.mult))
            vop(lambda en, dd=dd: en.tensor_scalar(out=At2[:, :], in0=Pfi[:, :], scalar1=deci[:, dd:dd + 1],
                                                   scalar2=None, op0=ALU.mult))
            vop(lambda en, dd=dd: en.tensor_sub(out=lhsTD[dd][0:64, :], in0=At1[:, :], in1=At2[:, :]))
            vop(lambda en, dd=dd: en.tensor_scalar(out=At1[:, :], in0=Pfr[:, :], scalar1=deci[:, dd:dd + 1],
                                                   scalar2=None, op0=ALU.mult))
            vop(lambda en, dd=dd: en.tensor_scalar(out=At2[:, :], in0=Pfi[:, :], scalar1=decr[:, dd:dd + 1],
                                                   scalar2=None, op0=ALU.mult))
            vop(lambda en, dd=dd: en.scalar_tensor_tensor(out=lhsTD[dd][64:128, :], in0=At1[:, :],
                                                          scalar=-1.0, in1=At2[:, :],
                                                          op0=ALU.mult, op1=ALU.subtract))
        if dbg:
            vop(lambda en: en.tensor_copy(out=dbgS[0:64, 0:8], in_=Acol[:, :]))
            vop(lambda en: en.tensor_copy(out=dbgS[0:64, 8:136], in_=Tr[:, :]))
            vop(lambda en: en.tensor_copy(out=dbgS[0:64, 136:264], in_=Pvr[:, :]))
            vop(lambda en: en.tensor_copy(out=dbgS[0:64, 264:280], in_=decr[:, :]))
            vop(lambda en: en.tensor_copy(out=dbgS[0:64, 280:408], in_=Ti[:, :]))
        dv_tbl = (cDV, cDV.v)

        pe_jk = pop(lambda en: en.matmul(out=p0[:, 0:128], lhsT=jrow[:], rhs=jrow[:], start=True,
                                         stop=True), waits=[dv_tbl])

        def angles(add64):
            addc = 64.0 if add64 else 0.0
            vop(lambda en, a=addc: en.tensor_scalar(out=uf32[:, 0:128], in0=p0[:, 0:128], scalar1=a,
                                                    scalar2=INVF, op0=ALU.add, op1=ALU.mult),
                waits=[pe_jk, (cAC, cAC.v)])
            vop(lambda en: en.tensor_copy(out=itmp[:], in_=uf32[:, 0:128]))
            vop(lambda en: en.tensor_copy(out=uf32[:, 128:256], in_=itmp[:]))
            vop(lambda en: en.tensor_sub(out=uf32[:, 0:128], in0=uf32[:, 0:128], in1=uf32[:, 128:256]))
            vop(lambda en: en.tensor_single_scalar(out=uf32[:, 128:256], in_=uf32[:, 0:128],
                                                   scalar=0.5, op=ALU.is_gt))
            vop(lambda en: en.tensor_sub(out=uf32[:, 0:128], in0=uf32[:, 0:128], in1=uf32[:, 128:256]))
            m2 = vop(lambda en: en.tensor_scalar(out=uf32[:, 0:128], in0=uf32[:, 0:128], scalar1=TWO_PI,
                                                 scalar2=None, op0=ALU.mult))
            return aop(lambda en: en.activation(out=stg32[:], in_=uf32[:, 0:128], func=AF.Sin),
                       waits=[m2, (cDV, cDV.v)])

        m_sin = angles(False)
        dv_fs = vop(lambda en: en.tensor_scalar(out=fsinn, in0=stg32[:], scalar1=-1.0, scalar2=None,
                                                op0=ALU.mult), waits=[m_sin])
        vop(lambda en: en.tensor_scalar(out=isinnw, in0=stg32[:], scalar1=-2.0 * INVF, scalar2=None,
                                        op0=ALU.mult))
        vop(lambda en: en.tensor_scalar(out=isinnw[:, 0:1] if False else AR[:, 0:0], in0=stg32[:, 0:1],
                                        scalar1=0.0, scalar2=None, op0=ALU.mult)) if False else None
        vop(lambda en: en.tensor_scalar(out=isinnw[:, 0:1], in0=stg32[:, 0:1], scalar1=-INVF,
                                        scalar2=None, op0=ALU.mult))
        m_cos = angles(True)
        dv_fc = vop(lambda en: en.tensor_copy(out=fcos, in_=stg32[:]), waits=[m_cos])
        if dbg:
            vop(lambda en: en.tensor_copy(out=dbgS[:, 408:472], in_=stg32[:, 0:64]))
        vop(lambda en: en.tensor_scalar(out=icosw, in0=stg32[:], scalar1=2.0 * INVF, scalar2=None,
                                        op0=ALU.mult))
        dv_ic = vop(lambda en: en.tensor_scalar(out=icosw[:, 0:1], in0=stg32[:, 0:1], scalar1=INVF,
                                                scalar2=None, op0=ALU.mult))
        m = vop(lambda en: en.tensor_copy(out=fcol[:], in_=altcol[:]), waits=[dv_alt])
        pe_at = pop(lambda en: en.transpose(out=p1[0:1, 0:128], in_=fcol[:], identity=identf[:]),
                    waits=[m, (cGP, gp_const[1])])
        dv_ialt = vop(lambda en: en.tensor_scalar(out=rowpA[0:1, 2064:2192], in0=p1[0:1, 0:128],
                                                  scalar1=INVF, scalar2=None, op0=ALU.mult), waits=[pe_at])

        for i in range(8):
            m = dma("sync", ftmp, w_in[i * 128:(i + 1) * 128, :], cLD, waits=[(cDV, cDV.v)])
            pe_m = None
            for k in range(8):
                pd = (p2 if k < 4 else p3)[:, (k % 4) * 128:(k % 4) * 128 + 128]
                pe_m = pop(lambda en, k=k, pd=pd: en.transpose(out=pd, in_=ftmp[:, k * 128:(k + 1) * 128],
                                                               identity=identf[:]),
                           waits=[m, (cDV, cDV.v)] if k == 0 else ())
            for k in range(8):
                pd = (p2 if k < 4 else p3)[:, (k % 4) * 128:(k % 4) * 128 + 128]
                vop(lambda en, k=k, i=i, pd=pd: en.tensor_copy(out=winT[k][:, i * 128:(i + 1) * 128], in_=pd),
                    waits=[pe_m] if k == 0 else ())
        for i in range(8):
            for part, dst in ((c_re, CrT), (c_im, CiT)):
                m = dma("sync", ftmp[:, 0:64], part[i * 128:(i + 1) * 128, :], cLD, waits=[(cDV, cDV.v)])
                pe_m = pop(lambda en: en.transpose(out=p2[0:64, 0:128], in_=ftmp[:, 0:64],
                                                   identity=identf[:]), waits=[m, (cDV, cDV.v)])
                vop(lambda en, i=i, dst=dst: en.tensor_copy(out=dst[:, i * 128:(i + 1) * 128],
                                                            in_=p2[0:64, 0:128]), waits=[pe_m])
        for seg in range(8):
            s0, s1 = seg * 128, (seg + 1) * 128
            vop(lambda en, s0=s0, s1=s1: en.tensor_scalar(out=At1[:, :], in0=CrT[:, s0:s1],
                                                          scalar1=Acol[:, 2:3], scalar2=None, op0=ALU.mult))
            vop(lambda en, s0=s0, s1=s1: en.tensor_scalar(out=At2[:, :], in0=CiT[:, s0:s1],
                                                          scalar1=Acol[:, 3:4], scalar2=None, op0=ALU.mult))
            vop(lambda en, s0=s0, s1=s1: en.tensor_sub(out=gstk[0:64, s0:s1], in0=At1[:, :], in1=At2[:, :]))
            vop(lambda en, s0=s0, s1=s1: en.tensor_scalar(out=At1[:, :], in0=CiT[:, s0:s1],
                                                          scalar1=Acol[:, 2:3], scalar2=None, op0=ALU.mult))
            vop(lambda en, s0=s0, s1=s1: en.tensor_scalar(out=At2[:, :], in0=CrT[:, s0:s1],
                                                          scalar1=Acol[:, 3:4], scalar2=None, op0=ALU.mult))
            vop(lambda en, s0=s0, s1=s1: en.tensor_add(out=gstk[64:128, s0:s1], in0=At1[:, :], in1=At2[:, :]))
            vop(lambda en, s0=s0, s1=s1: en.tensor_copy(out=gswp[0:64, s0:s1], in_=gstk[64:128, s0:s1]))
            vop(lambda en, s0=s0, s1=s1: en.tensor_copy(out=gswp[64:128, s0:s1], in_=gstk[0:64, s0:s1]))
        dv_g = (cDV, cDV.v)

        vop(lambda en: en.tensor_copy(out=stg32[0:64, :], in_=Pvr[:, :]))
        dv_st = vop(lambda en: en.tensor_copy(out=stg32[64:128, :], in_=Pvi[:, :]))
        pe_m = pop(lambda en: en.transpose(out=p2[:, 0:128], in_=stg32[:], identity=identf[:]),
                   waits=[dv_st])
        dv_done = vop(lambda en: en.tensor_copy(out=lhsTz, in_=p2[:, 0:128]), waits=[pe_m])
        for cc in range(NPRE):
            dd = NPRE - 1 - cc
            vop(lambda en, dd=dd: en.tensor_scalar(out=At1[:, :], in0=Pvr[:, :], scalar1=decr[:, dd:dd + 1],
                                                   scalar2=None, op0=ALU.mult), waits=[dv_done])
            vop(lambda en, dd=dd: en.tensor_scalar(out=At2[:, :], in0=Pvi[:, :], scalar1=deci[:, dd:dd + 1],
                                                   scalar2=None, op0=ALU.mult))
            vop(lambda en: en.tensor_sub(out=stg32[0:64, :], in0=At1[:, :], in1=At2[:, :]))
            vop(lambda en, dd=dd: en.tensor_scalar(out=At1[:, :], in0=Pvr[:, :], scalar1=deci[:, dd:dd + 1],
                                                   scalar2=None, op0=ALU.mult))
            vop(lambda en, dd=dd: en.tensor_scalar(out=At2[:, :], in0=Pvi[:, :], scalar1=decr[:, dd:dd + 1],
                                                   scalar2=None, op0=ALU.mult))
            dv_st = vop(lambda en: en.tensor_add(out=stg32[64:128, :], in0=At1[:, :], in1=At2[:, :]))
            pe_m = pop(lambda en: en.transpose(out=p2[:, 0:128], in_=stg32[:], identity=identf[:]),
                       waits=[dv_st])
            dv_done = vop(lambda en, cc=cc: en.tensor_copy(out=lhsTmega[cc], in_=p2[:, 0:128]),
                          waits=[pe_m])

        vop(lambda en: en.tensor_copy(out=stgb[0:64, :], in_=Tr[:, :]))
        dv_st2 = vop(lambda en: en.tensor_scalar(out=stgb[64:128, :], in0=Ti[:, :], scalar1=-1.0,
                                                 scalar2=None, op0=ALU.mult))
        k1 = pop(lambda en: en.matmul(out=pA[:, 0:512], lhsT=stgb, rhs=gstk[:, 0:512], start=True,
                                      stop=True), waits=[dv_st2, dv_g])
        k2 = pop(lambda en: en.matmul(out=pA[:, 512:1024], lhsT=stgb, rhs=gstk[:, 512:1024], start=True,
                                      stop=True))
        vop(lambda en: en.tensor_copy(out=kr0[:, 0:512], in_=pA[:, 0:512]), waits=[k1])
        dv_kr = vop(lambda en: en.tensor_copy(out=kr0[:, 512:1024], in_=pA[:, 512:1024]), waits=[k2])
        for (tblL, tblO) in ((fcos, khR), (fsinn, khI)):
            k1 = pop(lambda en, l=tblL: en.matmul(out=pA[:, 0:512], lhsT=l, rhs=kr0[:, 0:512],
                                                  start=True, stop=True), waits=[dv_kr, dv_fc, dv_fs, dv_ic])
            k2 = pop(lambda en, l=tblL: en.matmul(out=pA[:, 512:1024], lhsT=l, rhs=kr0[:, 512:1024],
                                                  start=True, stop=True))
            vop(lambda en, o=tblO: en.tensor_copy(out=o[:, 0:512], in_=pA[:, 0:512]), waits=[k1])
            dv_kr = vop(lambda en, o=tblO: en.tensor_copy(out=o[:, 512:1024], in_=pA[:, 512:1024]),
                        waits=[k2])
        k1 = pop(lambda en: en.matmul(out=pA[0:1, 0:512], lhsT=altcol[:], rhs=kr0[:, 0:512], start=True,
                                      stop=True), waits=[dv_kr, dv_alt])
        k2 = pop(lambda en: en.matmul(out=pA[0:1, 512:1024], lhsT=altcol[:], rhs=kr0[:, 512:1024],
                                      start=True, stop=True))
        vop(lambda en: en.tensor_copy(out=rowpB[0:1, 3072:3584], in_=pA[0:1, 0:512]), waits=[k1])
        dv_kh8 = vop(lambda en: en.tensor_copy(out=rowpB[0:1, 3584:4096], in_=pA[0:1, 512:1024]),
                     waits=[k2])

        for ei in range(8):
            m = dma("sync", stg32[0:32, :], b1[ei, :].rearrange("(m p) -> m p", p=128), cLD,
                    waits=[(cDV, cDV.v)])
            pe_m = pop(lambda en: en.transpose(out=p2[:, 0:32], in_=stg32[0:32, :],
                                               identity=identf[0:32, 0:32]),
                       waits=[m, (cDV, cDV.v)])
            vop(lambda en, ei=ei: en.tensor_copy(out=b1c[:, ei * 32:(ei + 1) * 32], in_=p2[:, 0:32]),
                waits=[pe_m])

        vop(lambda en: en.memset(carry[:, :], 0.0))
        B.force_ss = False
        dv_setup = (cDV, cDV.v)
        gp_done = (cGP, cGP.v)
        pe_setup = (cPE, cPE.v)
        ac_setup = (cAC, cAC.v)

        last_pe_uuse = pe_setup
        pe_uuse_slot = [pe_setup, pe_setup]
        last_dv_xfm = dv_setup
        last_dv_x1pre = dv_setup
        last_dv_cum = dv_setup
        last_ac_st2 = ac_setup
        last_mscr = None
        last_dv_zg = dv_setup
        last_dv_sums = dv_setup
        last_ac_stats = ac_setup
        last_dv_soft = dv_setup
        last_p01_free = dv_setup
        last_dv_xfm1 = dv_setup
        free_x = [dv_setup, dv_setup]
        sg_ready = None
        zg_marks = [None] * NOWN
        x1f_written = []
        moe_acc_marks = {}

        uf32B = [uf32, uf32]
        last_dv_fft = dv_setup

        def head(t):
            nonlocal last_dv_xfm, last_p01_free, last_ac_stats, last_dv_x1pre
            own = t >= NPRE
            sl = t % 2
            rows = slice(t * 128, (t + 1) * 128)
            mx = dma("sync", sx[sl][:], xfull[rows, :], cLD, waits=[free_x[sl]])
            mxb = None
            if own:
                mxb = dma("gpsimd", sxb[sl], xfull[rows, :], cGL, waits=[free_x[sl]])
            tp_last = None
            for k in range(8):
                pd = (p0 if k < 4 else p1)[:, (k % 4) * 128:(k % 4) * 128 + 128]
                w = [mx]
                if k == 0:
                    w += [last_dv_xfm, last_p01_free, gp_done]
                tp_last = pop(lambda en, k=k, pd=pd, s=sl: en.transpose(
                    out=pd, in_=sx[s][:, k * 128:(k + 1) * 128], identity=identf[:]), waits=w)
            dvm = None
            for k in range(8):
                dvm = aop(lambda en, k=k: en.activation(
                    out=xfm[:, k * 128:(k + 1) * 128],
                    in_=(p0 if k < 4 else p1)[:, (k % 4) * 128:(k % 4) * 128 + 128],
                    func=AF.Copy),
                    waits=[tp_last] if k == 0 else [])
            last_dv_xfm = dvm
            if not own:
                free_x[sl] = dvm
            zm = None
            for half in range(2):
                pd = p0 if half == 0 else p1
                hs = slice(half * 512, (half + 1) * 512)
                for k in range(8):
                    pop(lambda en, k=k, pd=pd, hs=hs: en.matmul(
                        out=pd[:], lhsT=xfm[:, k * 128:(k + 1) * 128], rhs=winT[k][:, hs],
                        start=(k == 0), stop=False),
                        waits=[last_dv_xfm] if (k == 0 and half == 0) else ())
                pop(lambda en, pd=pd, hs=hs: en.matmul(out=pd[:], lhsT=onesall[0:1, :],
                                                       rhs=rowpA[0:1, hs], start=False, stop=False))
                zm = pop(lambda en, pd=pd, hs=hs: en.matmul(
                    out=pd[:], lhsT=onesall[0:1, :],
                    rhs=rowpA[0:1, 1024 + hs.start:1024 + hs.stop], start=False, stop=True))
            return dict(t=t, mx=mx, mxb=mxb, zm=zm, sl=sl, own=own)

        def stats_part(hh):
            nonlocal last_p01_free, last_ac_stats
            t = hh["t"]
            sl = t % 2
            uf = uf32B[sl]
            ac_st = ln_stats(p0[:], p1[:], True, waits=[hh["zm"], last_dv_x1pre, (cDV, cDV.v), gp_done],
                             uf_dst=uf, sb=sl * 16)
            last_ac_stats = ac_st
            last_p01_free = ac_st
            u_hi = u_hiB[sl]
            u_lo = u_loB[sl]
            vop(lambda en, uf=uf, sl=sl: en.tensor_scalar(
                out=uf, in0=uf, scalar1=stat[:, sl * 16 + 7:sl * 16 + 8],
                scalar2=stat[:, sl * 16 + 8:sl * 16 + 9], op0=ALU.mult, op1=ALU.add),
                waits=[ac_st, pe_uuse_slot[sl]])
            vop(lambda en, uf=uf: en.tensor_mul(out=uf, in0=uf, in1=grep))
            vop(lambda en, uf=uf: en.tensor_add(out=uf, in0=uf, in1=brep))
            vop(lambda en, uf=uf, uh=u_hi: en.tensor_copy(out=uh, in_=uf))
            hh["dv_u"] = vop(lambda en, uf=uf, uh=u_hi, ul=u_lo: en.tensor_sub(out=ul, in0=uf, in1=uh))

        def mm_part(hh):
            nonlocal last_pe_uuse, last_dv_xfm1, last_dv_x1pre, last_dv_cum, last_dv_zg
            nonlocal last_dv_sums, last_dv_soft, last_ac_st2, last_mscr, sg_ready, last_dv_fft
            t = hh["t"]
            own = t >= NPRE
            co = t - NPRE
            sl = t % 2
            u_hi = u_hiB[sl]
            u_lo = u_loB[sl]
            mxb = hh["mxb"]
            dv_u = hh["dv_u"]

            if not own:
                w0 = [dv_u] + ([last_dv_x1pre] if t == 0 else [])
                pop(lambda en, t=t, uh=u_hi: en.matmul(out=pA[:, 0:512], lhsT=lhsTmega[t],
                                                       rhs=uh[:, 0:512],
                                                       start=(t == 0), stop=(t == NPRE - 1)), waits=w0)
                pe_mega = pop(lambda en, t=t, uh=u_hi: en.matmul(out=pA[:, 512:1024], lhsT=lhsTmega[t],
                                                                 rhs=uh[:, 512:1024],
                                                                 start=(t == 0), stop=(t == NPRE - 1)))
                pop(lambda en, uh=u_hi: en.matmul(out=p2[0:1, :], lhsT=onescol[:], rhs=uh[:, 0:512],
                                                  start=True, stop=False),
                    waits=[last_dv_sums, (cDV, cDV.v)])
                pop(lambda en, ul=u_lo: en.matmul(out=p2[0:1, :], lhsT=onescol[:], rhs=ul[:, 0:512],
                                                  start=False, stop=True))
                pop(lambda en, uh=u_hi: en.matmul(out=p3[0:1, :], lhsT=onescol[:], rhs=uh[:, 512:1024],
                                                  start=True, stop=False))
                pe_sums = pop(lambda en, ul=u_lo: en.matmul(out=p3[0:1, :], lhsT=onescol[:],
                                                            rhs=ul[:, 512:1024], start=False, stop=True))
                last_pe_uuse = pe_sums
                pe_uuse_slot[sl] = pe_sums
                vop(lambda en: en.tensor_copy(out=crow[:, 0:512], in_=p2[0:1, :]),
                    waits=[pe_sums], ss=True)
                vop(lambda en: en.tensor_copy(out=crow[:, 512:1024], in_=p3[0:1, :]), ss=True)
                last_dv_sums = vop(lambda en: en.tensor_add(out=carry[:, :], in0=carry[:, :],
                                                            in1=crow[:, :]), ss=True)
                if t == NPRE - 1:
                    vop(lambda en: en.scalar_tensor_tensor(out=ufR[0:64, :], in0=pA[0:64, :],
                                                           scalar=selc[0:64, 0:1], in1=gstk[0:64, :],
                                                           op0=ALU.mult, op1=ALU.mult),
                        waits=[pe_mega, ld_small])
                    vop(lambda en: en.scalar_tensor_tensor(out=ufI[0:64, :], in0=pA[64:128, :],
                                                           scalar=selc[0:64, 0:1], in1=gswp[0:64, :],
                                                           op0=ALU.mult, op1=ALU.mult))
                    vop(lambda en: en.tensor_sub(out=yR[0:64, :], in0=ufR[0:64, :], in1=ufI[0:64, :]))
                    vop(lambda en: en.scalar_tensor_tensor(out=ufR[0:64, :], in0=pA[0:64, :],
                                                           scalar=selc[0:64, 0:1], in1=gswp[0:64, :],
                                                           op0=ALU.mult, op1=ALU.mult))
                    vop(lambda en: en.scalar_tensor_tensor(out=ufI[0:64, :], in0=pA[64:128, :],
                                                           scalar=selc[0:64, 0:1], in1=gstk[0:64, :],
                                                           op0=ALU.mult, op1=ALU.mult))
                    sg_ready = vop(lambda en: en.tensor_add(out=yR[64:128, :], in0=ufR[0:64, :],
                                                            in1=ufI[0:64, :]))
                    last_dv_x1pre = sg_ready
                    last_dv_sums = vop(lambda en: en.tensor_scalar(out=carry[:, :], in0=carry[:, :],
                                                                   scalar1=selc[0:1, 0:1], scalar2=None,
                                                                   op0=ALU.mult))
                return

            dv_x0lo = vop(lambda en, s=sl: en.tensor_sub(out=x0lo, in0=sx[s][:], in1=sxb[s]),
                          waits=[hh["mx"], mxb, pe_uuse_slot[0], pe_uuse_slot[1]])
            if co == 0:
                for i in range(8):
                    sg_ready = vop(lambda en, i=i: en.tensor_copy(
                        out=lhsTmega[i], in_=yR[:, i * 128:(i + 1) * 128]), waits=[sg_ready])
            pz1 = pop(lambda en, uh=u_hi: en.matmul(out=p2[:], lhsT=lhsTz, rhs=uh[:, 0:512], start=True,
                                                    stop=True),
                      waits=[dv_u, last_dv_zg, last_dv_sums, last_dv_fft])
            pz2 = pop(lambda en, uh=u_hi: en.matmul(out=p3[:], lhsT=lhsTz, rhs=uh[:, 512:1024],
                                                    start=True, stop=True))
            vop(lambda en: en.tensor_mul(out=ufR[0:64, 0:512], in0=gstk[0:64, 0:512], in1=p2[0:64, :]),
                waits=[pz1, sg_ready])
            vop(lambda en: en.tensor_mul(out=ufR[0:64, 512:1024], in0=gstk[0:64, 512:1024],
                                         in1=p3[0:64, :]), waits=[pz2])
            vop(lambda en: en.tensor_mul(out=ufI[0:64, 0:512], in0=gswp[0:64, 0:512], in1=p2[64:128, :]))
            vop(lambda en: en.tensor_mul(out=ufI[0:64, 512:1024], in0=gswp[0:64, 512:1024],
                                         in1=p3[64:128, :]))
            vop(lambda en, co=co: en.tensor_sub(out=ZG[co][0:64, :], in0=ufR[0:64, :], in1=ufI[0:64, :]))
            vop(lambda en: en.tensor_mul(out=ufR[0:64, 0:512], in0=gswp[0:64, 0:512], in1=p2[0:64, :]))
            vop(lambda en: en.tensor_mul(out=ufR[0:64, 512:1024], in0=gswp[0:64, 512:1024],
                                         in1=p3[0:64, :]))
            vop(lambda en: en.tensor_mul(out=ufI[0:64, 0:512], in0=gstk[0:64, 0:512], in1=p2[64:128, :]))
            vop(lambda en: en.tensor_mul(out=ufI[0:64, 512:1024], in0=gstk[0:64, 512:1024],
                                         in1=p3[64:128, :]))
            dv_zg = vop(lambda en, co=co: en.tensor_add(out=ZG[co][64:128, :], in0=ufR[0:64, :],
                                                        in1=ufI[0:64, :]))
            zg_marks[co] = dv_zg
            last_dv_zg = dv_zg
            pop(lambda en, uh=u_hi: en.matmul(out=p2[0:1, :], lhsT=onescol[:], rhs=uh[:, 0:512],
                                              start=True, stop=False), waits=[dv_zg, last_dv_sums])
            pop(lambda en, ul=u_lo: en.matmul(out=p2[0:1, :], lhsT=onescol[:], rhs=ul[:, 0:512],
                                              start=False, stop=True))
            pop(lambda en, uh=u_hi: en.matmul(out=p3[0:1, :], lhsT=onescol[:], rhs=uh[:, 512:1024],
                                              start=True, stop=False))
            pe_sums = pop(lambda en, ul=u_lo: en.matmul(out=p3[0:1, :], lhsT=onescol[:],
                                                        rhs=ul[:, 512:1024], start=False, stop=True))
            vop(lambda en: en.tensor_copy(out=rowpB[0:1, 0:1024], in_=carry[:, :]),
                waits=[last_dv_sums], ss=True)
            dv_cr = vop(lambda en: en.tensor_sub(out=rowpB[0:1, 1024:2048], in0=carry[:, :],
                                                 in1=rowpB[0:1, 0:1024]), ss=True)
            vop(lambda en: en.tensor_copy(out=crow[:, 0:512], in_=p2[0:1, :]), waits=[pe_sums], ss=True)
            vop(lambda en: en.tensor_copy(out=crow[:, 512:1024], in_=p3[0:1, :]), ss=True)
            dv_se = vop(lambda en: en.tensor_add(out=carry[:, :], in0=carry[:, :], in1=crow[:, :]),
                        ss=True)
            last_dv_sums = dv_se
            pe_accB = None
            for half in range(2):
                hs = slice(half * 512, (half + 1) * 512)
                pd = pB[:, hs]
                seq = [(tri, u_hi[:, hs], [dv_cr, last_dv_cum, last_dv_soft] if half == 0 else []),
                       (tri, u_lo[:, hs], []),
                       (onesall[0:1, :], rowpB[0:1, hs], []),
                       (onesall[0:1, :], rowpB[0:1, 1024 + hs.start:1024 + hs.stop], [])]
                for i, (l, r, w) in enumerate(seq):
                    pe_accB = pop(lambda en, l=l, r=r, pd=pd, st=(i == 0), sp=(i == len(seq) - 1):
                                  en.matmul(out=pd, lhsT=l, rhs=r, start=st, stop=sp), waits=w)
            f1 = pop(lambda en, uh=u_hi: en.matmul(out=p2[:], lhsT=fcos, rhs=uh[:, 0:512], start=True,
                                                   stop=True), waits=[dv_u, dv_se])
            f2 = pop(lambda en, uh=u_hi: en.matmul(out=p3[:], lhsT=fcos, rhs=uh[:, 512:1024], start=True,
                                                   stop=True))
            aop(lambda en: en.activation(out=ufR[:, 0:512], in_=p2[:], func=AF.Copy), waits=[f1])
            dv_ufr = aop(lambda en: en.activation(out=ufR[:, 512:1024], in_=p3[:], func=AF.Copy),
                         waits=[f2])
            f1 = pop(lambda en, uh=u_hi: en.matmul(out=p2[:], lhsT=fsinn, rhs=uh[:, 0:512], start=True,
                                                   stop=True), waits=[dv_ufr])
            f2 = pop(lambda en, uh=u_hi: en.matmul(out=p3[:], lhsT=fsinn, rhs=uh[:, 512:1024],
                                                   start=True, stop=True))
            pop(lambda en, uh=u_hi: en.matmul(out=pA[0:1, 0:512], lhsT=altcol[:], rhs=uh[:, 0:512],
                                              start=True, stop=True), waits=[last_dv_xfm1])
            f8 = pop(lambda en, uh=u_hi: en.matmul(out=pA[0:1, 512:1024], lhsT=altcol[:],
                                                   rhs=uh[:, 512:1024], start=True, stop=True))
            aop(lambda en: en.activation(out=ufI[:, 0:512], in_=p2[:], func=AF.Copy), waits=[f1])
            dv_ufi = aop(lambda en: en.activation(out=ufI[:, 512:1024], in_=p3[:], func=AF.Copy),
                         waits=[f2])
            last_dv_fft = dv_ufi
            vop(lambda en: en.tensor_mul(out=yR, in0=ufR, in1=khR), waits=[dv_ufi, dv_kh8, dv_ialt])
            vop(lambda en: en.tensor_mul(out=yI, in0=ufI, in1=khI))
            vop(lambda en: en.tensor_sub(out=yR, in0=yR, in1=yI))
            vop(lambda en: en.tensor_mul(out=yI, in0=ufI, in1=khR))
            vop(lambda en: en.tensor_mul(out=ufI, in0=ufR, in1=khI))
            vop(lambda en: en.tensor_add(out=yI, in0=yI, in1=ufI))
            vop(lambda en: en.tensor_mul(out=rowpB[0:1, 2048:2560], in0=pA[0:1, 0:512],
                                         in1=rowpB[0:1, 3072:3584]), waits=[f8], ss=True)
            dv_y = vop(lambda en: en.tensor_mul(out=rowpB[0:1, 2560:3072], in0=pA[0:1, 512:1024],
                                                in1=rowpB[0:1, 3584:4096]), ss=True)
            last_dv_zg = dv_y
            pe_accA = None
            for half in range(2):
                hs = slice(half * 512, (half + 1) * 512)
                pd = pA[:, hs]
                seq = []
                seq.append((icosw, yR[:, hs], [dv_y, last_dv_x1pre, last_dv_xfm1] if half == 0 else []))
                seq.append((isinnw, yI[:, hs], []))
                seq.append((rowpA[0:1, 2064:2192], rowpB[0:1, 2048 + hs.start:2048 + hs.stop], []))
                for cp in range(co):
                    seq.append((lhsTD[co - 1 - cp], ZG[cp][:, hs],
                                [zg_marks[cp]] if half == 0 else []))
                seq.append((lhsTD[co], sg_home[:, hs], [sg_ready] if half == 0 else []))
                seq.append((identb, u_hi[:, hs], []))
                seq.append((identb, u_lo[:, hs], []))
                seq.append((identb, sxb[sl][:, hs], [mxb] if half == 0 else []))
                seq.append((identb, x0lo[:, hs], [dv_x0lo] if half == 0 else []))
                for i, (l, r, w) in enumerate(seq):
                    pe_accA = pop(lambda en, l=l, r=r, pd=pd, st=(i == 0), sp=(i == len(seq) - 1):
                                  en.matmul(out=pd, lhsT=l, rhs=r, start=st, stop=sp), waits=w)
            last_pe_uuse = pe_accB
            pe_uuse_slot[sl] = pe_accA
            vop(lambda en: en.tensor_mul(out=x1pre, in0=drep, in1=pB[:]),
                waits=[pe_accB, last_ac_st2])
            dv_pre = vop(lambda en: en.tensor_add(out=x1pre, in0=x1pre, in1=pA[:]), waits=[pe_accA])
            last_dv_x1pre = dv_pre
            last_dv_cum = dv_pre
            free_x[sl] = dv_pre
            ac_st2 = ln_stats(x1pre[:, 0:512], x1pre[:, 512:1024], False,
                              waits=[dv_pre, (cDV, cDV.v)])
            last_ac_st2 = ac_st2
            dv_x1 = vop(lambda en: en.tensor_scalar(out=x1f, in0=x1pre, scalar1=stat[:, 39:40],
                                                    scalar2=stat[:, 40:41], op0=ALU.mult, op1=ALU.add),
                        waits=[ac_st2] + ([last_mscr] if last_mscr else []))
            mscr = dma("sync", x1scr[co * 128:(co + 1) * 128, :], x1f, cLD, waits=[dv_x1])
            x1f_written.append(mscr)
            last_mscr = mscr
            x1tp = None
            for k in range(8):
                pd = pA[:, k * 128:(k + 1) * 128]
                x1tp = pop(lambda en, k=k, pd=pd: en.transpose(
                    out=pd, in_=x1f[:, k * 128:(k + 1) * 128], identity=identf[:]),
                    waits=[dv_x1] if k == 0 else ())
            dv_xfm1 = None
            for k in range(8):
                dv_xfm1 = aop(lambda en, k=k, co=co: en.activation(
                    out=x1fm[k][:, co * 128:(co + 1) * 128], in_=pA[:, k * 128:(k + 1) * 128],
                    func=AF.Copy),
                    waits=[x1tp] if k == 0 else [])
            last_dv_xfm1 = dv_xfm1
            pop(lambda en, co=co: en.matmul(out=pB[:, 0:8], lhsT=x1fm[0][:, co * 128:(co + 1) * 128],
                                            rhs=wrbf[:, 0:8], start=True, stop=False),
                waits=[dv_xfm1, gl_b2])
            for k in range(1, 8):
                pop(lambda en, k=k, co=co: en.matmul(out=pB[:, 0:8],
                                                     lhsT=x1fm[k][:, co * 128:(co + 1) * 128],
                                                     rhs=wrbf[:, k * 8:(k + 1) * 8], start=False,
                                                     stop=False))
            pop(lambda en: en.matmul(out=pB[:, 0:8], lhsT=onesall[0:1, :],
                                     rhs=rowpA[0:1, 2048:2056], start=False, stop=False))
            pe_rt = pop(lambda en: en.matmul(out=pB[:, 0:8], lhsT=onesall[0:1, :],
                                             rhs=rowpA[0:1, 2056:2064], start=False, stop=True))
            vop(lambda en: en.reduce_max(out=stat[:, 9:10], in_=pB[:, 0:8], axis=AX.X),
                waits=[pe_rt], ss=True)
            vop(lambda en: en.tensor_scalar(out=l2t[:], in0=pB[:, 0:8], scalar1=stat[:, 9:10],
                                            scalar2=None, op0=ALU.subtract), ss=True)
            vop(lambda en: en.tensor_single_scalar(out=msk8[:], in_=l2t[:], scalar=0.0, op=ALU.is_ge),
                ss=True)
            dv_l2 = vop(lambda en: en.tensor_tensor_scan(out=cs8[:], data0=msk8[:], data1=ones8[:],
                                                         initial=0.0, op0=ALU.add, op1=ALU.mult), ss=True)
            ac_e = aop(lambda en: en.activation(out=exp8[:], in_=l2t[:], func=AF.Exp), waits=[dv_l2])
            vop(lambda en: en.tensor_single_scalar(out=cs8[:], in_=cs8[:], scalar=1.5, op=ALU.is_le),
                ss=True)
            vop(lambda en: en.tensor_mul(out=msk8[:], in0=msk8[:], in1=cs8[:]), ss=True)
            vop(lambda en: en.reduce_sum(out=stat[:, 10:11], in_=exp8[:], axis=AX.X), waits=[ac_e],
                ss=True)
            vop(lambda en: en.reciprocal(out=stat[:, 11:12], in_=stat[:, 10:11]), ss=True)
            last_dv_soft = vop(lambda en, co=co: en.tensor_scalar(
                out=geAll[:, co * 8:(co + 1) * 8], in0=msk8[:], scalar1=stat[:, 11:12], scalar2=None,
                op0=ALU.mult), ss=True)

        if run_attn:
            ts_list = list(range(n_pre)) + list(range(NPRE, NPRE + n_own))
            prev = None
            for t in ts_list:
                hh = head(t)
                stats_part(hh)
                if prev is not None:
                    mm_part(prev)
                prev = hh
            if prev is not None:
                mm_part(prev)

        attn_done = (cDV, cDV.v)
        pe_attn_done = (cPE, cPE.v)
        ld_attn_done = (cLD, cLD.v)
        ac_attn_done = (cAC, cAC.v)
        pe_hdone = [None] * NE
        gl_w1 = [None] * NE
        last_dv_gate = attn_done
        prev_gate = attn_done
        ac_hev = {}
        w2c_marks = {}

        for ei in range(n_experts):
            if ei == 0:
                w = [attn_done, pe_attn_done, ac_attn_done]
                for k in range(8):
                    dma("gpsimd", w1res[k], w1[ei, k * 128:(k + 1) * 128, :], cGL,
                        waits=w if k == 0 else ())
                gl_w1[ei] = (cGL, cGL.v)
            for tt in range(NTT):
                ts = slice(tt * TT_TOK, (tt + 1) * TT_TOK)
                prev_gate = last_dv_gate
                for mt in range(32):
                    pd = (p0, p1, p2)[mt % 3]
                    w = []
                    if mt == 0:
                        w = [gl_w1[ei], last_dv_gate]
                        if ei == 0 and tt == 0:
                            w.append(ac_attn_done)
                    elif mt < 3:
                        w = [last_dv_gate]
                    else:
                        pm = ac_hev.get((ei, tt, mt - 3))
                        if pm:
                            w = [pm]
                    for k in range(8):
                        pop(lambda en, k=k, mt=mt, pd=pd, ts=ts: en.matmul(
                            out=pd[:], lhsT=w1res[k][:, mt * 128:(mt + 1) * 128], rhs=x1fm[k][:, ts],
                            start=(k == 0), stop=(k == 7)), waits=w if k == 0 else ())
                    pe_h = (cPE, cPE.v)
                    if mt == 31:
                        pe_hdone[ei] = pe_h
                    ac_hev[(ei, tt, mt)] = aop(lambda en, mt=mt, pd=pd, ei=ei: en.activation(
                        out=hbuf[:, mt * TT_TOK:(mt + 1) * TT_TOK], in_=pd[:], func=AF.Relu,
                        bias=b1c[:, ei * 32 + mt:ei * 32 + mt + 1]), waits=[pe_h])
                ps_halves = [(pA[:, 0:512], pA[:, 512:1024]), (pB[:, 0:512], pB[:, 512:1024]),
                             (p0[:], p1[:]), (p2[:], p3[:])]
                pe_w2 = None
                for k in range(32):
                    slot = k % 4
                    wd = []
                    pm = w2c_marks.get((ei, tt, k - 4))
                    if pm:
                        wd.append(pm)
                    elif k < 4:
                        pm = w2c_marks.get((ei, tt - 1, 28 + k)) or \
                            w2c_marks.get((ei - 1, NTT - 1, 28 + k))
                        wd.append(pm if pm else prev_gate)
                    m = dma("gpsimd", w2buf[slot], w2[ei, k * 128:(k + 1) * 128, :], cGL, waits=wd)
                    if tt == NTT - 1 and ei + 1 < n_experts and 8 <= k < 16:
                        kk = k - 8
                        dma("gpsimd", w1res[kk], w1[ei + 1, kk * 128:(kk + 1) * 128, :], cGL,
                            waits=[pe_hdone[ei]] if kk == 0 else ())
                        if kk == 7:
                            gl_w1[ei + 1] = (cGL, cGL.v)
                    for tM in range(4):
                        for hf in range(2):
                            w = []
                            if tM == 0 and hf == 0:
                                w = [m]
                                if k == 0:
                                    w += [prev_gate, ac_hev[(ei, tt, 31)]]
                            pe_w2 = pop(lambda en, k=k, tM=tM, hf=hf, slot=slot: en.matmul(
                                out=ps_halves[tM][hf],
                                lhsT=hbuf[:, k * TT_TOK + tM * 128:k * TT_TOK + (tM + 1) * 128],
                                rhs=w2buf[slot][:, hf * 512:(hf + 1) * 512],
                                start=(k == 0), stop=(k == 31)), waits=w)
                    w2c_marks[(ei, tt, k)] = pe_w2

                dvg = None
                for tM in range(4):
                    g = tt * 4 + tM
                    w = [pe_w2, last_dv_soft]
                    if ei == 0 and tt == 0:
                        w.append(ld_attn_done)
                    pm = moe_acc_marks.get((g, ei - 1))
                    if pm:
                        w.append(pm)
                    gcol = geAll[:, g * 8 + ei:g * 8 + ei + 1]
                    if tM == 0:
                        dvg = vop(lambda en, gc=gcol: en.tensor_scalar(out=tmpg[0], in0=pA[:], scalar1=gc,
                                                                       scalar2=None, op0=ALU.mult), waits=w)
                    elif tM == 1:
                        dvg = vop(lambda en, gc=gcol: en.tensor_scalar(out=tmpg[1], in0=pB[:], scalar1=gc,
                                                                       scalar2=None, op0=ALU.mult), waits=w)
                    else:
                        pp = (p0, p1) if tM == 2 else (p2, p3)
                        vop(lambda en, gc=gcol, pp=pp, tM=tM: en.tensor_scalar(
                            out=tmpg[tM][:, 0:512], in0=pp[0][:], scalar1=gc, scalar2=None,
                            op0=ALU.mult), waits=w)
                        dvg = vop(lambda en, gc=gcol, pp=pp, tM=tM: en.tensor_scalar(
                            out=tmpg[tM][:, 512:1024], in0=pp[1][:], scalar1=gc, scalar2=None,
                            op0=ALU.mult))
                    gr = slice(g * 128, (g + 1) * 128)
                    mden = OP("gpsimd", lambda en, gr=gr, tM=tM, ei=ei: en.dma_start(
                        out=accd[gr, :], in_=tmpg[tM],
                        accum_op=(ALU.bypass if ei == 0 else ALU.add)), cGL, k=16, waits=[dvg])
                    moe_acc_marks[(g, ei)] = mden
                last_dv_gate = dvg

        moe_all_gl = (cGL, cGL.v)
        moe_all_dv = (cDV, cDV.v)

        st_marks = []
        for g in range(16 if run_final else 0):
            sl = g % 2
            gr = slice(g * 128, (g + 1) * 128)
            w = []
            if n_experts > 0:
                w.append(moe_acc_marks[(g, n_experts - 1)])
            if run_attn and g < n_own:
                w.append(x1f_written[g])
            if g == 0:
                w += [moe_all_gl, moe_all_dv]
            if len(st_marks) >= 2:
                w.append(st_marks[-2])
            m1 = dma("sync", x1g[sl], x1scr[gr, :], cLD, waits=w)
            m2 = dma("sync", accg[sl], accd[gr, :], cLD)
            ptp = pop(lambda en, g=g: en.transpose(out=p2[0:8, 0:128], in_=geAll[:, g * 8:(g + 1) * 8],
                                                   identity=identf[:]),
                      waits=([moe_all_dv, (cDV, cDV.v)] if True else []))
            dge = vop(lambda en: en.tensor_copy(out=geT[:], in_=p2[0:8, 0:128]), waits=[ptp])
            pb2 = None
            for hf in range(2):
                pd = p0 if hf == 0 else p1
                pop(lambda en, pd=pd, hf=hf: en.matmul(out=pd[:], lhsT=geT[:],
                                                       rhs=b2hi_t[:, hf * 512:(hf + 1) * 512],
                                                       start=True, stop=False), waits=[dge])
                pb2 = pop(lambda en, pd=pd, hf=hf: en.matmul(out=pd[:], lhsT=geT[:],
                                                             rhs=b2lo_t[:, hf * 512:(hf + 1) * 512],
                                                             start=False, stop=True))
            dv1 = vop(lambda en, sl=sl: en.tensor_add(out=resb, in0=x1g[sl], in1=accg[sl]),
                      waits=[m1, m2, (cAC, cAC.v)])
            vop(lambda en: en.tensor_add(out=resb[:, 0:512], in0=resb[:, 0:512], in1=p0[:]),
                waits=[pb2])
            dv1 = vop(lambda en: en.tensor_add(out=resb[:, 512:1024], in0=resb[:, 512:1024], in1=p1[:]))
            ac3 = ln_stats(resb[:, 0:512], resb[:, 512:1024], False, waits=[dv1])
            dv2 = vop(lambda en, sl=sl: en.tensor_scalar(out=outb[sl], in0=resb, scalar1=stat[:, 39:40],
                                                         scalar2=stat[:, 40:41], op0=ALU.mult,
                                                         op1=ALU.add), waits=[ac3])
            st_marks.append(dma("sync", out[gr, :], outb[sl], cST, waits=[dv2]))
        if dbg:
            B.wait("vector", (cPE, cPE.v))
            B.wait("vector", (cAC, cAC.v))
            dcp = vop(lambda en: en.tensor_copy(out=ftmp, in_=uf32))
            dma("sync", dbg1[:, 0:1024], ftmp, cST, waits=[dcp])
            dma("sync", dbg1[:, 1024:1152], fcos, cST)
            B.plan["sync"].pop()
            cST.v -= 16
            dcp = vop(lambda en: en.tensor_copy(out=ftmp[:, 0:128], in_=fcos))
            dcp = vop(lambda en: en.tensor_copy(out=ftmp[:, 128:256], in_=icosw))
            dcp = vop(lambda en: en.tensor_copy(out=ftmp[:, 256:384], in_=lhsTD[0]))
            dcp = vop(lambda en: en.tensor_copy(out=ftmp[:, 384:512], in_=lhsTz))
            dcp = vop(lambda en: en.tensor_copy(out=ftmp[:, 512:640], in_=tri))
            dma("sync", dbg1[:, 1024:1664], ftmp[:, 0:640], cST, waits=[dcp])
            dcp = vop(lambda en: en.tensor_copy(out=ftmp, in_=khR))
            dma("sync", dbg1[:, 2048:3072], ftmp, cST, waits=[dcp])
            dcp = vop(lambda en: en.tensor_copy(out=ftmp, in_=u_hi))
            dma("sync", dbg1[:, 3072:4096], ftmp, cST, waits=[dcp])
            dcp = vop(lambda en: en.tensor_copy(out=ftmp, in_=gstk))
            dma("sync", dbg1[:, 4096:5120], ftmp, cST, waits=[dcp])
            dcp = vop(lambda en: en.tensor_copy(out=ftmp, in_=ZG[15]))
            dma("sync", dbg1[:, 5120:6144], ftmp, cST, waits=[dcp])
            dma("sync", dbg1[:, 6144:6656], dbgS[:, :], cST)
            dcp = vop(lambda en: en.tensor_copy(out=ftmp[0:1, 256:1024], in_=carry[:, 0:768]))
            dma("sync", dbg1[:, 6656:7680], ftmp, cST, waits=[dcp])
            dcp = vop(lambda en: en.tensor_copy(out=ftmp, in_=x1pre))
            dma("sync", dbg1[:, 7168:8192], ftmp, cST, waits=[dcp])
        B.plan["sync"].append(("wait", cST.h, cST.v))

        B.emit()
    return nc


_NC_CACHE = None


def kernel(**inp):
    global _NC_CACHE
    x = np.ascontiguousarray(np.asarray(inp["L0_attn_x"], dtype=np.float32))
    _ = np.asarray(inp["L0_attn_D_im"])
    ins = {
        "w_in": np.ascontiguousarray(np.asarray(inp["L0_attn_W_in"], np.float32)),
        "b_in": np.ascontiguousarray(np.asarray(inp["L0_attn_b_in"], np.float32)),
        "gamma": np.ascontiguousarray(np.asarray(inp["L0_attn_gamma"], np.float32)),
        "beta": np.ascontiguousarray(np.asarray(inp["L0_attn_beta"], np.float32)),
        "a_re": np.ascontiguousarray(np.asarray(inp["L0_attn_A_re"], np.float32)),
        "a_im": np.ascontiguousarray(np.asarray(inp["L0_attn_A_im"], np.float32)),
        "b_re": np.ascontiguousarray(np.asarray(inp["L0_attn_B_re"], np.float32)),
        "b_im": np.ascontiguousarray(np.asarray(inp["L0_attn_B_im"], np.float32)),
        "c_re": np.ascontiguousarray(np.asarray(inp["L0_attn_C_re"], np.float32)),
        "c_im": np.ascontiguousarray(np.asarray(inp["L0_attn_C_im"], np.float32)),
        "d_re": np.ascontiguousarray(np.asarray(inp["L0_attn_D_re"], np.float32)),
        "r_w": np.ascontiguousarray(np.asarray(inp["L0_ffn_router_W"], np.float32)),
        "r_b": np.ascontiguousarray(np.asarray(inp["L0_ffn_router_b"], np.float32)),
        "w1": np.ascontiguousarray(np.asarray(inp["L0_ffn_W1"], np.float32)),
        "b1": np.ascontiguousarray(np.asarray(inp["L0_ffn_b1"], np.float32)),
        "w2": np.ascontiguousarray(np.asarray(inp["L0_ffn_W2"], np.float32)),
        "b2": np.ascontiguousarray(np.asarray(inp["L0_ffn_b2"], np.float32)),
    }
    if _NC_CACHE is None:
        _NC_CACHE = build_program()
    nc = _NC_CACHE
    in_maps = []
    for cid in range(8):
        b, h = cid >> 1, cid & 1
        xf = np.concatenate([x[b, 0:2048], x[b, h * 2048:(h + 1) * 2048]], axis=0)
        m = dict(ins)
        m["xfull"] = np.ascontiguousarray(xf)
        m["sel"] = np.full((128, 1), float(h), np.float32)
        in_maps.append(m)
    res = run_bass_kernel_spmd(nc, in_maps, core_ids=list(range(8)))
    outa = np.empty((4, 4096, D), np.float32)
    for cid, r in enumerate(res.results):
        b, h = cid >> 1, cid & 1
        outa[b, h * 2048:(h + 1) * 2048] = r["out"]
    return outa
